# version 08
# baseline (speedup 1.0000x reference)
import contextlib
import math
import numpy as np
import ml_dtypes
import concourse.bass as bass
import concourse.mybir as mybir
from concourse.bass_utils import run_bass_kernel_spmd

F32 = mybir.dt.float32
BF16 = mybir.dt.bfloat16
FP16 = mybir.dt.float16
AF = mybir.ActivationFunctionType
ALU = mybir.AluOpType
AX = mybir.AxisListType.X

D = 2048
S = 8192
HEADS = 16
DH = 128
DFF = 8192
NDC = D // 128
NFC = DFF // 128
NKC = S // 128
NT = 16
NSLOT = 4
SCALE = 1.0 / math.sqrt(DH)
EPS = 1e-5
CM_CONST = -0.5 * math.log(128.0) + math.log(1.05)

ENGINES = ("t", "v", "s", "y", "g")


class Sem:
    def __init__(self, stack, nc, name):
        self.h = stack.enter_context(nc.semaphore(name))
        self.n = 0

    def bump(self, k=1):
        self.n += k
        return self.n


class Prog:

    def __init__(self, nc):
        self.nc = nc
        self.ops = []
        self.ninstr = 0

    def op(self, eng, name, *args, inc=None, k=1, **kw):
        self.ops.append((eng, name, args, kw, inc, k))
        self.ninstr += 1
        if inc is not None:
            return inc.bump(k)
        return None

    def wait(self, eng, sem, n):
        if n > 0:
            self.ops.append((eng, "wait_ge", (sem.h, n), {}, None, 0))
            self.ninstr += 1

    def dma(self, eng, out, in_, inc, **kw):
        return self.op(eng, "dma_start", inc=inc, k=16, out=out, in_=in_, **kw)

    def flush(self):
        streams = {e: [] for e in ENGINES}
        for o in self.ops:
            streams[o[0]].append(o)
        self.ops = []

        def runner(engname):
            def run(e):
                for (_, name, args, kw, inc, k) in streams[engname]:
                    ins = getattr(e, name)(*args, **kw)
                    if inc is not None:
                        ins.then_inc(inc.h, k)
            return run

        with self.nc.Block() as block:
            for attr, key in (("tensor", "t"), ("vector", "v"), ("scalar", "s"),
                              ("sync", "y"), ("gpsimd", "g")):
                if streams[key]:
                    getattr(block, attr)(runner(key))


def p_list_for(j):
    return sorted({j, 7 - j, 8 + j, 15 - j})


def build(debug=()):
    nc = bass.Bass("TRN2", target_bir_lowering=False)
    dbg = set(debug)

    xseq = nc.dram_tensor("xseq", [S, D], F32, kind="ExternalInput").ap()
    xown = nc.dram_tensor("xown", [NT * 128, D], F32, kind="ExternalInput").ap()
    wqkvT = nc.dram_tensor("wqkvT", [D, 2304], F32, kind="ExternalInput").ap()
    woutT = nc.dram_tensor("woutT", [D, D], F32, kind="ExternalInput").ap()
    w1T = nc.dram_tensor("w1T", [D, DFF], F32, kind="ExternalInput").ap()
    w2T = nc.dram_tensor("w2T", [DFF, D], F32, kind="ExternalInput").ap()
    qkvb_i = nc.dram_tensor("qkvb", [128, 18], F32, kind="ExternalInput").ap()
    kb_i = nc.dram_tensor("kb", [128, 128], F32, kind="ExternalInput").ap()
    vb_i = nc.dram_tensor("vb", [128, 128], F32, kind="ExternalInput").ap()
    outb_i = nc.dram_tensor("outb", [128, D], BF16, kind="ExternalInput").ap()
    b1c_i = nc.dram_tensor("b1c", [128, NFC], F32, kind="ExternalInput").ap()
    b2c_i = nc.dram_tensor("b2c", [128, NDC], F32, kind="ExternalInput").ap()
    id16_i = nc.dram_tensor("id16", [128, 128], BF16, kind="ExternalInput").ap()
    idh_i = nc.dram_tensor("idh", [128, 128], FP16, kind="ExternalInput").ap()
    id32_i = nc.dram_tensor("id32", [128, 128], F32, kind="ExternalInput").ap()
    onescol_i = nc.dram_tensor("onescol", [128, 4], F32, kind="ExternalInput").ap()
    onescolb_i = nc.dram_tensor("onescolb", [128, 4], BF16, kind="ExternalInput").ap()
    onescolh_i = nc.dram_tensor("onescolh", [128, 4], FP16, kind="ExternalInput").ap()
    onesrow_i = nc.dram_tensor("onesrow", [1, 128], F32, kind="ExternalInput").ap()
    onesrowh_i = nc.dram_tensor("onesrowh", [1, 128], BF16, kind="ExternalInput").ap()
    iota_i = nc.dram_tensor("iota", [128, 1024], FP16, kind="ExternalInput").ap()
    cthr_i = nc.dram_tensor("cthr", [128, 64], F32, kind="ExternalInput").ap()

    yscr = nc.dram_tensor("yscr", [NT * 128, D], F32, kind="Internal").ap()
    ln2scr = nc.dram_tensor("ln2scr", [NT, NDC, 128, 128], BF16, kind="Internal").ap()
    w1q = nc.dram_tensor("w1q", [NDC, 128, DFF], BF16, kind="Internal").ap()
    w2q = nc.dram_tensor("w2q", [NFC, 128, D], BF16, kind="Internal").ap()

    out_t = nc.dram_tensor("out", [NT * 128, D], F32, kind="ExternalOutput").ap()
    dbg_t = {}

    def dbg_tensor(name, shape, dt=F32):
        if name in dbg:
            dbg_t[name] = nc.dram_tensor("dbg_" + name, shape, dt, kind="ExternalOutput").ap()
        return dbg_t.get(name)

    d_kfm = dbg_tensor("kfm", [128, NKC * 128])
    d_v = dbg_tensor("vst", [128, NKC * 128])
    d_qfm = dbg_tensor("qfm", [128, HEADS * 512])
    d_cfm = dbg_tensor("cfm", [128, HEADS * 512])
    d_wq1 = dbg_tensor("wq1", [128, DFF])
    d_thr = dbg_tensor("thr", [128, 8])
    d_ln1 = dbg_tensor("ln1t0", [128, NDC * 128])
    d_st = dbg_tensor("stat", [128, 16])
    d_y = dbg_tensor("y", [NT * 128, D])
    d_ffn = dbg_tensor("ffn", [128, NDC * 512])
    d_fst = dbg_tensor("fst", [1, 8 * 512])
    d_frs = dbg_tensor("frs", [128, 512], BF16)
    d_fsh = dbg_tensor("fsh", [128, 512])
    d_fot = dbg_tensor("fot", [128, D])

    with contextlib.ExitStack() as gst:
        E = gst.enter_context
        sx = Sem(gst, nc, "sx")
        sg = Sem(gst, nc, "sg")
        sp = Sem(gst, nc, "sp")
        sv = Sem(gst, nc, "sv")
        ss = Sem(gst, nc, "ss")
        so = Sem(gst, nc, "so")
        ALLSEMS = (sx, sg, sp, sv, ss, so)

        id16 = E(nc.sbuf_tensor("id16s", [128, 128], BF16))
        idh = E(nc.sbuf_tensor("idhs", [128, 128], FP16))
        id32 = E(nc.sbuf_tensor("id32s", [128, 128], F32))
        onescol = E(nc.sbuf_tensor("onescols", [128, 4], F32))
        onescolb = E(nc.sbuf_tensor("onescolbs", [128, 4], BF16))
        onescolh = E(nc.sbuf_tensor("onescolhs", [128, 4], FP16))
        onesrow = E(nc.sbuf_tensor("onesrows", [1, 128], F32))
        onesrowh = E(nc.sbuf_tensor("onesrowhs", [1, 128], BF16))
        iota = E(nc.sbuf_tensor("iotas", [128, 1024], FP16))
        cthr = E(nc.sbuf_tensor("cthrs", [128, 64], F32))
        qkvb = E(nc.sbuf_tensor("qkvbs", [128, 18], F32))
        kb_s = E(nc.sbuf_tensor("kbs", [128, 128], F32))
        vb_s = E(nc.sbuf_tensor("vbs", [128, 128], F32))
        outb_s = E(nc.sbuf_tensor("outbs", [128, D], BF16))
        b1c = E(nc.sbuf_tensor("b1cs", [128, NFC], F32))
        b2c = E(nc.sbuf_tensor("b2cs", [128, NDC], F32))
        cmt = E(nc.sbuf_tensor("cmt", [128, 1], F32))
        epst = E(nc.sbuf_tensor("epst", [128, 1], F32))
        negMb = E(nc.sbuf_tensor("negMb", [128, 1], F32))
        stt = E(nc.sbuf_tensor("stt", [128, 16], F32))
        kmax2 = E(nc.sbuf_tensor("kmax2", [1, 4], F32))
        qmax2 = E(nc.sbuf_tensor("qmax2", [1, 4], F32))
        sqjunk = E(nc.sbuf_tensor("sqjunk", [128, 1024], F32))
        thr1 = E(nc.sbuf_tensor("thr1", [128, 4], F32))
        absacc = E(nc.sbuf_tensor("absacc", [128, 2], F32))
        mtmp = E(nc.sbuf_tensor("mtmp", [128, 8], F32))

        P = Prog(nc)

        def barrier():
            for e in ENGINES:
                for sem in ALLSEMS:
                    P.wait(e, sem, sem.n)

        for dst, src in ((id16, id16_i), (idh, idh_i), (id32, id32_i),
                         (onescol, onescol_i), (onescolb, onescolb_i),
                         (onescolh, onescolh_i), (onesrow, onesrow_i),
                         (onesrowh, onesrowh_i), (iota, iota_i), (cthr, cthr_i),
                         (qkvb, qkvb_i), (kb_s, kb_i), (vb_s, vb_i),
                         (outb_s, outb_i), (b1c, b1c_i), (b2c, b2c_i)):
            P.dma("y", dst[:], src, sx)
        P.op("v", "memset", cmt[:], CM_CONST, inc=sv)
        P.op("v", "memset", epst[:], EPS, inc=sv)
        consts_done = sx.n


        with contextlib.ExitStack() as ast_:
            EA = ast_.enter_context
            kfm = EA(nc.sbuf_tensor("kfm", [128, NKC, 128], BF16))
            vst = EA(nc.sbuf_tensor("vst", [128, NKC, 128], BF16))
            qfm = EA(nc.sbuf_tensor("qfm", [128, HEADS, 512], BF16))
            cfm = EA(nc.sbuf_tensor("cfm", [128, HEADS, 512], BF16))
            wbig = EA(nc.sbuf_tensor("wbig", [128, NDC, D], BF16))
            xbuf = EA(nc.sbuf_tensor("xbuf", [128, 2, D], F32))
            ln1own = EA(nc.sbuf_tensor("ln1own", [128, NDC, 512], BF16))
            pt = EA(nc.sbuf_tensor("pt", [128, 2, 1024], BF16))
            maskb = EA(nc.sbuf_tensor("maskb", [128, 1024], BF16))
            ktmp = EA(nc.sbuf_tensor("ktmp", [128, 128], BF16))
            rlrow = EA(nc.sbuf_tensor("rlrow", [1, 2, 512], F32))
            rl16 = EA(nc.sbuf_tensor("rl16", [1, 512], BF16))
            rlb = EA(nc.sbuf_tensor("rlb", [128, 512], F32))
            qsq = None
            ao = EA(nc.sbuf_tensor("ao", [128, D], F32))
            ln2tm = EA(nc.sbuf_tensor("ln2tm", [128, D], BF16))
            qsq = pt[:, 0, 0:512]
            ln1fm = ln1own[:, 8:12, :].rearrange("p c (a e) -> p (c a) e", a=4)
            l2b = maskb[:].rearrange("p (c e) -> p c e", c=8)
            psAB = [EA(nc.psum_tensor(f"psAB{i}", [128, 1024], F32)) for i in range(2)]
            psA = [psAB[0][:, 0:512], psAB[1][:, 0:512]]
            psCc = EA(nc.psum_tensor("psCc", [128, 512], F32))
            psL = EA(nc.psum_tensor("psL", [128, 512], F32))
            psS = EA(nc.psum_tensor("psS", [128, 512], F32))
            psT = EA(nc.psum_tensor("psT", [128, 1024], BF16))

            wkv_t = EA(nc.sbuf_tensor("wkv", [128, NDC, 256], BF16))
            wkv = wkv_t[:]
            P.dma("g", wkv, wqkvT[:, 2048:2304].rearrange("(c p) e -> p c e", p=128), sg)
            wkv_done = sg.n

            def C(par, i):
                return stt[:, 8 * par + i:8 * par + i + 1]

            def ln_stats1(src, par):
                P.op("v", "tensor_reduce", C(par, 0), src, AX, ALU.add, inc=sv)
                P.wait("s", sv, sv.n)
                P.op("s", "activation", sqjunk[:], src[:, 0:1024], AF.Square,
                     accum_out=C(par, 2), inc=ss)
                P.op("s", "activation", sqjunk[:], src[:, 1024:2048], AF.Square,
                     accum_out=C(par, 1), inc=ss)
                return ss.n

            def ln_stats2(par, s1done):
                P.wait("v", ss, s1done)
                P.op("v", "tensor_add", C(par, 1), C(par, 1), C(par, 2), inc=sv)
                P.op("v", "tensor_scalar", C(par, 3), C(par, 0), 1.0 / D, None, ALU.mult, inc=sv)
                P.wait("v", sv, sv.n)
                P.op("v", "tensor_mul", C(par, 4), C(par, 3), C(par, 3), inc=sv)
                P.op("v", "tensor_scalar", C(par, 2), C(par, 1), 1.0 / D, None, ALU.mult, inc=sv)
                P.wait("v", sv, sv.n)
                P.op("v", "tensor_sub", C(par, 4), C(par, 2), C(par, 4), inc=sv)
                P.wait("v", sv, sv.n)
                P.wait("s", sv, sv.n)
                P.op("s", "activation", C(par, 7), C(par, 4), AF.Ln, bias=epst[:, 0:1], inc=ss)
                P.op("s", "activation", C(par, 5), C(par, 7), AF.Exp, scale=-0.5, inc=ss)
                P.wait("v", ss, ss.n)
                P.op("v", "scalar_tensor_tensor", C(par, 6), C(par, 3), -1.0,
                     C(par, 5), ALU.mult, ALU.mult, inc=sv)
                P.wait("s", sv, sv.n)

            def ln_stats(src, par=0):
                d1 = ln_stats1(src, par)
                ln_stats2(par, d1)

            def norm_cast_transpose(src, dst_chunked, par=0, use_ln2=False):
                if use_ln2:
                    stg = tuple(ln2tm[:, 512 * i:512 * (i + 1)] for i in range(4))
                else:
                    stg = (pt[:, 0, 0:512], pt[:, 0, 512:1024], pt[:, 1, 0:512], pt[:, 1, 512:1024])
                P.wait("s", sp, sp.n)
                for q4 in range(4):
                    P.op("s", "activation", stg[q4], src[:, 512 * q4:512 * (q4 + 1)],
                         AF.Identity, bias=C(par, 6), scale=C(par, 5), inc=ss)
                norm_done = ss.n
                P.wait("t", ss, norm_done)
                for half in range(2):
                    for i in range(8):
                        blk = 8 * half + i
                        q4, r = divmod(blk, 4)
                        P.op("t", "transpose", psT[:, 128 * i:128 * (i + 1)],
                             stg[q4][:, 128 * r:128 * (r + 1)], id16[:],
                             inc=(sp if i == 7 else None))
                    P.wait("v", sp, sp.n)
                    P.op("v", "tensor_copy",
                         dst_chunked[:, 8 * half:8 * half + 8, :],
                         psT[:, 0:1024].rearrange("p (c e) -> p c e", c=8), inc=sv)
                    P.wait("t", sv, sv.n)
                return norm_done

            wfa2 = cfm[:].bitcast(F32).rearrange("p c e -> p (c e)").rearrange(
                "p (s e) -> p s e", s=2)
            wfa3 = ao[:]
            l1B = ln1own[:, 0:8, :].rearrange("p c e -> p (c e)").rearrange(
                "p (s e) -> p s e", s=2)
            wqoB = l1B
            leb = ln2tm[:]

            def wslot(u):
                s_ = u % 3
                return wfa3 if s_ == 2 else wfa2[:, s_, :]

            def wsub_in(u):
                if u < 64:
                    i, s_ = divmod(u, 4)
                    return w1T[128 * i:128 * (i + 1), 2048 * s_:2048 * (s_ + 1)]
                return w2T[128 * (u - 64):128 * (u - 63), :]

            def wsub_out(u):
                if u < 64:
                    i, s_ = divmod(u, 4)
                    return w1q[i, :, 2048 * s_:2048 * (s_ + 1)]
                return w2q[u - 64, :, :]

            qload = {}
            quse = {}
            qdma = {}

            quse_s = {}

            def quant_load(L):
                u = L if L < 128 else L - 128
                if L >= 3:
                    P.wait("y", sv, quse.get(L - 3, 0))
                    P.wait("y", ss, quse_s.get(L - 3, 0))
                P.dma("y", wslot(L), wsub_in(u), sx)
                qload[L] = sx.n

            mean_vdone = {}

            def quant_mean_sub(u):
                p2 = (u % 2) * 2
                P.wait("s", sx, qload[u])
                if u >= 2:
                    P.wait("s", sv, mean_vdone[u - 2])
                P.op("s", "activation", sqjunk[:], wslot(u)[:, 0:1024], AF.Abs,
                     accum_out=mtmp[:, p2:p2 + 1], inc=ss)
                P.op("s", "activation", sqjunk[:], wslot(u)[:, 1024:2048], AF.Abs,
                     accum_out=mtmp[:, p2 + 1:p2 + 2], inc=ss)
                quse_s[u] = ss.n
                quse[u] = 0
                tgt = absacc[:, 0:1] if u < 64 else absacc[:, 1:2]
                P.wait("v", ss, ss.n)
                P.op("v", "tensor_add", mtmp[:, 4 + p2 // 2:5 + p2 // 2],
                     mtmp[:, p2:p2 + 1], mtmp[:, p2 + 1:p2 + 2], inc=sv)
                P.wait("v", sv, sv.n)
                if u == 0 or u == 64:
                    P.op("v", "tensor_copy", tgt, mtmp[:, 4 + p2 // 2:5 + p2 // 2], inc=sv)
                else:
                    P.op("v", "tensor_add", tgt, tgt, mtmp[:, 4 + p2 // 2:5 + p2 // 2], inc=sv)
                P.wait("v", sv, sv.n)
                mean_vdone[u] = sv.n

            def quant_thr():
                P.wait("t", sv, sv.n)
                P.op("t", "matmul", psL[0:1, 0:2], onescol[:, 0:1], absacc[:, 0:2],
                     start=True, stop=True, inc=sp)
                P.wait("v", sp, sp.n)
                P.op("v", "tensor_scalar", mtmp[0:1, 0:2], psL[0:1, 0:2],
                     0.7 / float(DFF * D), None, ALU.mult, inc=sv)
                P.wait("t", sv, sv.n)
                P.op("t", "matmul", psS[:, 0:2], onesrow[:], mtmp[0:1, 0:2],
                     start=True, stop=True, inc=sp)
                P.wait("v", sp, sp.n)
                P.op("v", "tensor_copy", thr1[:, 0:2], psS[:, 0:2], inc=sv)
                P.wait("v", sv, sv.n)
                P.op("v", "tensor_scalar", thr1[:, 2:4], thr1[:, 0:2], -1.0, None,
                     ALU.mult, inc=sv)
                P.wait("v", sv, sv.n)
                if d_thr is not None:
                    P.wait("g", sv, sv.n)
                    P.dma("g", d_thr[:, 0:4], thr1[:], so)
                    P.dma("g", d_thr[:, 4:6], absacc[:], so)

            def quant_q_sub(u):
                L = 128 + u
                tcol = 0 if u < 64 else 1
                P.wait("v", sx, qload[L])
                P.op("v", "tensor_scalar", leb, wslot(L),
                     thr1[:, 2 + tcol:3 + tcol], None, ALU.is_le, inc=sv)
                if u >= 2:
                    P.wait("v", sg, qdma[u - 2])
                P.wait("v", sv, sv.n)
                P.op("v", "scalar_tensor_tensor", wqoB[:, u % 2, :], wslot(L),
                     thr1[:, tcol:tcol + 1], leb, ALU.is_ge, ALU.subtract, inc=sv)
                quse[L] = sv.n
                quse_s[L] = 0
                P.wait("g", sv, quse[L])
                P.dma("g", wsub_out(u), wqoB[:, u % 2, :], sg)
                qdma[u] = sg.n

            P.wait("t", sg, wkv_done)
            P.wait("t", sx, consts_done)
            tile_done = [None] * NKC
            s1d = [None] * NKC

            xld = [None] * (NKC + 1)
            xguard = {0: [0, 0], 1: [0, 0]}

            def x_acquire(buf):
                P.wait("y", ss, xguard[buf][0])
                P.wait("y", sv, xguard[buf][1])

            def x_load(k):
                if k >= NKC:
                    return
                x_acquire(k % 2)
                P.dma("y", xbuf[:, k % 2, :], xseq[128 * k:128 * (k + 1), :], sx)
                xld[k] = sx.n

            def kv_part1(k):
                buf = k % 2
                P.wait("v", sx, xld[k])
                P.wait("s", sx, xld[k])
                s1d[k] = ln_stats1(xbuf[:, buf, :], k % 2)

            def kv_part2(k, in_slot=False):
                buf = k % 2
                par = k % 2
                ln_stats2(par, s1d[k])
                tile_done[k] = norm_cast_transpose(xbuf[:, buf, :], ln1fm, par,
                                                   use_ln2=in_slot)
                xguard[buf][0] = tile_done[k]
                pa = psS[:, 0:256] if in_slot else psA[k % 2][:, 0:256]
                if in_slot:
                    P.wait("t", ss, ss.n)
                for dc in range(NDC):
                    P.op("t", "matmul", pa, ln1fm[:, dc, :], wkv[:, dc, :],
                         start=(dc == 0), stop=(dc == NDC - 1),
                         inc=(sp if dc == NDC - 1 else None))
                P.wait("v", sp, sp.n)
                P.op("v", "scalar_tensor_tensor", ktmp[:], pa[:, 0:128], 1.0, kb_s[:],
                     ALU.bypass, ALU.add, inc=sv)
                P.op("v", "scalar_tensor_tensor", vst[:, k, 0:128], pa[:, 128:256], 1.0,
                     vb_s[:], ALU.bypass, ALU.add, inc=sv)
                P.wait("t", sv, sv.n)
                P.op("t", "transpose", psT[:, 0:128], ktmp[:], id16[:], inc=sp)
                P.wait("v", sp, sp.n)
                P.op("v", "tensor_copy", kfm[:, k, :], psT[:, 0:128], inc=sv)
                P.wait("t", sv, sv.n)

            def knorm_group(g):
                for half in range(2):
                    c0 = 16 * g + 8 * half
                    P.wait("s", sp, sp.n)
                    P.wait("s", sv, sv.n)
                    P.op("s", "activation", sqjunk[:, 0:512],
                         kfm[:, c0:c0 + 4, :].rearrange("p c e -> p (c e)"),
                         AF.Square, inc=ss)
                    P.op("s", "activation", sqjunk[:, 512:1024],
                         kfm[:, c0 + 4:c0 + 8, :].rearrange("p c e -> p (c e)"),
                         AF.Square, inc=ss)
                    P.wait("t", ss, ss.n)
                    P.op("t", "matmul", psS[0:1, :], onescol[:, 0:1], sqjunk[:, 0:512],
                         start=True, stop=False, skip_group_check=True)
                    P.op("t", "matmul", psS[0:1, :], onescol[:, 0:1], sqjunk[:, 512:1024],
                         start=False, stop=True, skip_group_check=True, inc=sp)
                    P.wait("v", sp, sp.n)
                    P.op("v", "tensor_reduce", kmax2[0:1, 1:2], psS[0:1, :], AX,
                         ALU.max, inc=sv)
                    P.wait("v", sv, sv.n)
                    if g == 0 and half == 0:
                        P.op("v", "tensor_copy", kmax2[0:1, 0:1], kmax2[0:1, 1:2], inc=sv)
                    else:
                        P.op("v", "tensor_max", kmax2[0:1, 0:1], kmax2[0:1, 0:1],
                             kmax2[0:1, 1:2], inc=sv)
                    P.wait("v", sv, sv.n)
                    P.wait("t", sv, sv.n)

            x_load(0)
            quant_load(0)
            quant_load(1)
            next_load = [2]

            def quant_step(Lc):
                if next_load[0] < 256:
                    quant_load(next_load[0])
                    next_load[0] += 1
                if Lc < 128:
                    quant_mean_sub(Lc)
                else:
                    quant_q_sub(Lc - 128)

            for k in range(17):
                if k < 16:
                    kv_part1(k)
                if k == 9:
                    quant_thr()
                if k >= 1:
                    kv_part2(k - 1)
                if k < 16:
                    x_load(k + 1)
                if k >= 1:
                    for Lc in range(16 * (k - 1), 16 * k):
                        quant_step(Lc)

            for e in ENGINES:
                P.wait(e, sx, sx.n)
                P.wait(e, sg, sg.n)
                P.wait(e, sv, sv.n)
                P.wait(e, ss, ss.n)
            knorm_group(0)
            kmax_done = sv.n

            ydma_prev = (0, 0)
            l2dma_prev = 0
            for s_i in range(NSLOT):
                Es = 16 * (s_i + 1)
                P.wait("v", sx, sx.n)
                P.wait("g", sp, sp.n)
                P.dma("g", wbig[:], wqkvT[:, 0:2048].rearrange("(c p) e -> p c e", p=128), sg)
                wq_loaded = sg.n

                for tl in range(4):
                    tt = 4 * s_i + tl
                    buf = tl % 2
                    x_acquire(buf)
                    P.dma("y", xbuf[:, buf, :], xown[128 * tt:128 * (tt + 1), :], sx)
                    P.wait("v", sx, sx.n)
                    P.wait("s", sx, sx.n)
                    ln_stats(xbuf[:, buf, :], tl % 2)
                    norm_cast_transpose(
                        xbuf[:, buf, :],
                        ln1own[:, :, 128 * tl:128 * (tl + 1)], tl % 2)
                    xguard[buf][0] = ss.n
                P.wait("t", sg, wq_loaded)
                qev = {}
                for h in range(HEADS):
                    pa = psA[h % 2]
                    if h >= 2:
                        P.wait("t", ss, qev[h - 2])
                    for dc in range(NDC):
                        P.op("t", "matmul", pa[:, 0:512], wbig[:, dc, 128 * h:128 * (h + 1)],
                             ln1own[:, dc, :], start=(dc == 0), stop=(dc == NDC - 1),
                             inc=(sp if dc == NDC - 1 else None))
                    P.wait("s", sp, sp.n)
                    P.op("s", "activation", qfm[:, h, :],
                         pa[:, 0:512], AF.Identity, bias=qkvb[:, h:h + 1], inc=ss)
                    qev[h] = ss.n
                qproj_done = ss.n

                for h in range(HEADS):
                    P.wait("s", sp, sp.n)
                    if h == 0:
                        P.wait("s", ss, qproj_done)
                    P.op("s", "activation", qsq, qfm[:, h, :], AF.Square, inc=ss)
                    P.wait("t", ss, ss.n)
                    P.op("t", "matmul", psA[0][0:1, :], onescolb[:, 0:1], qsq,
                         start=True, stop=True, inc=sp)
                    P.wait("v", sp, sp.n)
                    P.op("v", "tensor_reduce", qmax2[0:1, 1:2], psA[0][0:1, :], AX,
                         ALU.max, inc=sv)
                    P.wait("v", sv, sv.n)
                    if h == 0:
                        P.op("v", "tensor_copy", qmax2[0:1, 0:1], qmax2[0:1, 1:2], inc=sv)
                    else:
                        P.op("v", "tensor_max", qmax2[0:1, 0:1], qmax2[0:1, 0:1],
                             qmax2[0:1, 1:2], inc=sv)
                    P.wait("v", sv, sv.n)
                    P.wait("t", sv, sv.n)
                P.wait("s", sv, sv.n)
                P.op("s", "activation", qmax2[0:1, 2:3], qmax2[0:1, 0:1], AF.Ln, inc=ss)
                P.op("s", "activation", kmax2[0:1, 2:3], kmax2[0:1, 0:1], AF.Ln, inc=ss)
                P.wait("v", ss, ss.n)
                P.op("v", "tensor_add", qmax2[0:1, 3:4], qmax2[0:1, 2:3],
                     kmax2[0:1, 2:3], inc=sv)
                P.wait("v", sv, sv.n)
                P.wait("s", sv, sv.n)
                P.op("s", "activation", qmax2[0:1, 1:2], qmax2[0:1, 3:4], AF.Exp,
                     scale=0.5, bias=cmt[0:1, 0:1], inc=ss)
                P.wait("v", ss, ss.n)
                P.op("v", "tensor_scalar", qmax2[0:1, 1:2], qmax2[0:1, 1:2], 1.0, None,
                     ALU.add, inc=sv)
                P.wait("v", sv, sv.n)
                P.op("v", "tensor_scalar", qmax2[0:1, 1:2], qmax2[0:1, 1:2], -1.0, None,
                     ALU.mult, inc=sv)
                P.wait("v", sv, sv.n)
                P.wait("t", sv, sv.n)
                P.op("t", "matmul", psA[0][:, 0:1], onesrow[:], qmax2[0:1, 1:2],
                     start=True, stop=True, inc=sp)
                P.wait("v", sp, sp.n)
                P.wait("v", ss, ss.n)
                P.op("v", "tensor_copy", negMb[:], psA[0][:, 0:1], inc=sv)
                negm_done = sv.n
                P.wait("s", sv, negm_done)
                P.wait("t", sv, negm_done)

                ctx_ev_done = {}
                sc_ev_done = {}
                if s_i < 3:
                    x_load(16 * (s_i + 1))
                for h in range(HEADS):
                    exp_done = {}
                    ready = {}
                    rsem = {}
                    ctx_done = {}

                    def emit_ctx(cp, h=h):
                        if cp == 0 and h >= 1:
                            P.wait("t", sv, ctx_ev_done[h - 1])
                        P.wait("t", rsem[cp], ready[cp])
                        pb = (cp // 2) % 2
                        psrc = pt
                        P.op("t", "matmul", psCc[:], vst[:, cp, 0:128],
                             psrc[:, pb, 0:512], start=(cp == 0), stop=False,
                             skip_group_check=True)
                        P.op("t", "matmul", psCc[:], vst[:, cp + 1, 0:128],
                             psrc[:, pb, 512:1024], start=False, stop=(cp == Es - 2),
                             skip_group_check=True)
                        P.op("t", "matmul", psL[0:1, :], onescolb[:, 0:1],
                             psrc[:, pb, 0:512], start=(cp == 0), stop=False,
                             skip_group_check=True)
                        P.op("t", "matmul", psL[0:1, :], onescolb[:, 0:1],
                             psrc[:, pb, 512:1024], start=False, stop=(cp == Es - 2),
                             skip_group_check=True, inc=sp)
                        ctx_done[cp] = sp.n

                    for cp in range(0, Es, 2):
                        pb = (cp // 2) % 2
                        pa = psAB[pb]
                        if cp >= 4:
                            P.wait("t", ss, exp_done[cp - 4])
                        P.op("t", "matmul", pa[:, 0:512], kfm[:, cp, :], qfm[:, h, :],
                             start=True, stop=True)
                        P.op("t", "matmul", pa[:, 512:1024], kfm[:, cp + 1, :], qfm[:, h, :],
                             start=True, stop=True, inc=sp)
                        sc_mm = sp.n
                        P.wait("s", sp, sc_mm)
                        if cp >= 4:
                            P.wait("s", sp, ctx_done[cp - 4])
                        P.op("s", "activation", pt[:, pb, :], pa[:, 0:1024], AF.Exp,
                             bias=negMb[:, 0:1], scale=SCALE, inc=ss)
                        exp_done[cp] = ss.n
                        if cp >= 16 * s_i:
                            P.op("v", "tensor_scalar", maskb[:], iota[:],
                                 cthr[:, cp:cp + 1], None, ALU.is_ge, inc=sv)
                            P.wait("v", sv, sv.n)
                            P.wait("v", ss, exp_done[cp])
                            P.op("v", "tensor_mul", pt[:, pb, :], pt[:, pb, :],
                                 maskb[:], inc=sv)
                            ready[cp], rsem[cp] = sv.n, sv
                        else:
                            ready[cp], rsem[cp] = exp_done[cp], ss
                        if cp >= 2:
                            emit_ctx(cp - 2)
                    emit_ctx(Es - 2)
                    P.wait("v", sp, ctx_done[Es - 2])
                    P.op("v", "reciprocal", rlrow[0:1, h % 2, :], psL[0:1, :], inc=sv)
                    P.wait("v", sv, sv.n)
                    P.op("v", "tensor_copy", rl16[0:1, :], rlrow[0:1, h % 2, :], inc=sv)
                    P.wait("t", sv, sv.n)
                    if h >= 1:
                        P.wait("t", ss, sc_ev_done[h - 1])
                    P.op("t", "matmul", psS[:], onesrowh[:], rl16[0:1, :],
                         start=True, stop=True, inc=sp)
                    P.wait("s", sp, sp.n)
                    P.op("s", "activation", rlb[:], psS[:], AF.Identity, inc=ss)
                    sc_ev_done[h] = ss.n
                    P.wait("v", ss, sc_ev_done[h])
                    P.op("v", "tensor_mul", cfm[:, h, :], psCc[:], rlb[:], inc=sv)
                    ctx_ev_done[h] = sv.n
                    if s_i < 3:
                        t_ = 16 * (s_i + 1) + h
                        if h < HEADS - 1:
                            x_load(t_ + 1)
                        kv_part1(t_)
                        kv_part2(t_, in_slot=True)
                attn_done = sv.n
                if s_i < 3:
                    knorm_group(s_i + 1)

                if s_i == 0 and d_qfm is not None:
                    P.wait("g", ss, qproj_done)
                    P.dma("g", d_qfm, qfm[:].rearrange("p h q -> p (h q)"), so)
                if s_i == 0 and d_cfm is not None:
                    P.wait("g", sv, attn_done)
                    P.dma("g", d_cfm, cfm[:].rearrange("p h q -> p (h q)"), so)

                P.wait("g", ss, qproj_done)
                P.dma("g", wbig[:], woutT.rearrange("(c p) e -> p c e", p=128), sg)
                wo_loaded = sg.n
                P.wait("t", sg, wo_loaded)
                P.wait("t", sv, attn_done)
                for tl in range(4):
                    tt = 4 * s_i + tl
                    x_acquire(tl % 2)
                    P.dma("y", xbuf[:, tl % 2, :], xown[128 * tt:128 * (tt + 1), :], sx)
                    xo_ld = sx.n
                    aoev = {}
                    for dspan in range(4):
                        pa = psA[dspan % 2]
                        if dspan >= 2:
                            P.wait("t", sv, aoev[dspan - 2])
                        for ec in range(NDC):
                            P.op("t", "matmul", pa[:], cfm[:, ec, 128 * tl:128 * (tl + 1)],
                                 wbig[:, ec, 512 * dspan:512 * (dspan + 1)],
                                 start=(ec == 0), stop=(ec == NDC - 1),
                                 inc=(sp if ec == NDC - 1 else None))
                        P.wait("v", sp, sp.n)
                        if dspan == 0:
                            P.wait("v", sx, ydma_prev[0])
                            P.wait("v", so, ydma_prev[1])
                        P.op("v", "scalar_tensor_tensor", ao[:, 512 * dspan:512 * (dspan + 1)],
                             pa[:, 0:512], 1.0, outb_s[:, 512 * dspan:512 * (dspan + 1)],
                             ALU.bypass, ALU.add, inc=sv)
                        aoev[dspan] = sv.n
                    ln_stats(ao[:])
                    for q2 in range(2):
                        P.op("s", "activation", sqjunk[:, 0:1024],
                             ao[:, 1024 * q2:1024 * (q2 + 1)], AF.Identity,
                             bias=stt[:, 6:7], scale=stt[:, 5:6], inc=ss)
                        P.wait("v", ss, ss.n)
                        if q2 == 0:
                            P.wait("v", sx, xo_ld)
                        P.op("v", "tensor_add", ao[:, 1024 * q2:1024 * (q2 + 1)],
                             sqjunk[:, 0:1024], xbuf[:, tl % 2, 1024 * q2:1024 * (q2 + 1)],
                             inc=sv)
                        if q2 == 0:
                            P.wait("s", sv, sv.n)
                    xguard[tl % 2][1] = sv.n
                    y_done = sv.n
                    P.wait("y", sv, y_done)
                    P.dma("y", yscr[128 * tt:128 * (tt + 1), :], ao[:], sx)
                    if d_y is not None:
                        P.dma("y", d_y[128 * tt:128 * (tt + 1), :], ao[:], so)
                    ydma_prev = (sx.n, so.n)
                    ln_stats(ao[:])
                    P.op("s", "activation", ln2tm[:], ao[:], AF.Identity,
                         bias=stt[:, 6:7], scale=stt[:, 5:6], inc=ss)
                    P.wait("t", ss, ss.n)
                    for half in range(2):
                        P.wait("v", sx, l2dma_prev)
                        for i in range(8):
                            blk = 8 * half + i
                            P.op("t", "transpose", psT[:, 128 * i:128 * (i + 1)],
                                 ln2tm[:, 128 * blk:128 * (blk + 1)], id16[:],
                                 inc=(sp if i == 7 else None))
                        P.wait("v", sp, sp.n)
                        P.op("v", "tensor_copy", l2b[:],
                             psT[:, 0:1024].rearrange("p (c e) -> p c e", c=8), inc=sv)
                        P.wait("t", sv, sv.n)
                        P.wait("y", sv, sv.n)
                        P.dma("y", ln2scr[tt, 8 * half:8 * half + 8].rearrange("c p e -> p c e"),
                              l2b[:], sx)
                        l2dma_prev = sx.n
            P.flush()
        barrier()

        with contextlib.ExitStack() as fst:
            EF = fst.enter_context
            l2f = EF(nc.sbuf_tensor("l2f", [128, NDC, 512], BF16))
            hsp = EF(nc.sbuf_tensor("hsp", [128, NFC, 512], BF16))
            wg1 = EF(nc.sbuf_tensor("wg1", [128, 2, NDC, 512], BF16))
            wg2 = EF(nc.sbuf_tensor("wg2", [128, 2, 2, 1024], BF16))
            ff16 = EF(nc.sbuf_tensor("ff16", [128, NDC, 512], BF16))
            sq16 = EF(nc.sbuf_tensor("sq16", [128, 2, 512], F32))
            rstd16 = EF(nc.sbuf_tensor("rstd16", [128, 512], BF16))
            shiftb = EF(nc.sbuf_tensor("shiftb", [128, 512], F32))
            rows = EF(nc.sbuf_tensor("rows", [1, 8, 512], F32))
            rowsh = EF(nc.sbuf_tensor("rowsh", [1, 2, 512], BF16))
            lnt32 = EF(nc.sbuf_tensor("lnt32", [128, 2, 128], F32))
            ysp = EF(nc.sbuf_tensor("ysp", [128, D], F32))
            otile = EF(nc.sbuf_tensor("otile", [128, 2, D], F32))
            psF = [EF(nc.psum_tensor(f"psF{i}", [128, 512], F32)) for i in range(8)]

            a_used = {}
            b_used = {}
            outdma = {}
            for sp_i in range(NSLOT):
                span_start_sv = sv.n
                P.wait("y", sp, sp.n)
                for t_ in range(4):
                    P.dma("y", l2f[:, :, 128 * t_:128 * (t_ + 1)],
                          ln2scr[4 * sp_i + t_].rearrange("c p e -> p c e"), sx)
                l2f_ld = sx.n

                gelu_done = {}
                for g in range(16):
                    gb = g % 2
                    ka = 16 * sp_i + g
                    if ka >= 2:
                        P.wait("g", sp, a_used[ka - 2])
                    P.dma("g", wg1[:, gb, :, :], w1q[:, :, 512 * g:512 * (g + 1)].rearrange(
                        "c p e -> p c e"), sg)
                    g_ld = sg.n
                    for fi in range(4):
                        fc = 4 * g + fi
                        pf = psF[fc % 2]
                        if fi == 0:
                            P.wait("t", sg, g_ld)
                        if fc == 0:
                            P.wait("t", sx, l2f_ld)
                            P.wait("t", sv, span_start_sv)
                        if fc >= 2:
                            P.wait("t", ss, gelu_done[fc - 2])
                        for dc in range(NDC):
                            P.op("t", "matmul", pf[:],
                                 wg1[:, gb, dc, 128 * fi:128 * (fi + 1)],
                                 l2f[:, dc, :], start=(dc == 0), stop=(dc == NDC - 1),
                                 inc=(sp if dc == NDC - 1 else None))
                        P.wait("s", sp, sp.n)
                        P.op("s", "activation", hsp[:, fc, :], pf[:], AF.Gelu,
                             bias=b1c[:, fc:fc + 1], inc=ss)
                        gelu_done[fc] = ss.n
                    a_used[ka] = sp.n
                a_done = ss.n

                for dh in range(2):
                    b_ld = {}
                    for fc in range(NFC):
                        g2, fi = divmod(fc, 2)
                        kb = 32 * (2 * sp_i + dh) + g2
                        if fi == 0:
                            if kb >= 2:
                                P.wait("g", sp, b_used[kb - 2])
                            P.dma("g", wg2[:, g2 % 2, :, :],
                                  w2q[2 * g2:2 * g2 + 2, :, 1024 * dh:1024 * (dh + 1)].rearrange(
                                      "c p e -> p c e"), sg)
                            b_ld[g2] = sg.n
                            P.wait("t", sg, b_ld[g2])
                        if fc == 0:
                            P.wait("t", ss, a_done)
                            P.wait("t", sv, span_start_sv)
                            if dh == 1:
                                P.wait("t", ss, evac_dh0)
                        for dcl in range(8):
                            P.op("t", "matmul", psF[dcl][:],
                                 wg2[:, g2 % 2, fi, 128 * dcl:128 * (dcl + 1)],
                                 hsp[:, fc, :], start=(fc == 0), stop=(fc == NFC - 1),
                                 inc=(sp if dcl == 7 else None))
                        if fi == 1:
                            b_used[kb] = sp.n
                    bmm_done = sp.n
                    P.wait("s", sp, bmm_done)
                    for dcl in range(8):
                        dc = 8 * dh + dcl
                        P.op("s", "activation", ff16[:, dc, :], psF[dcl][:], AF.Identity,
                             bias=b2c[:, dc:dc + 1], inc=ss)
                    if dh == 0:
                        evac_dh0 = ss.n
                b_done = ss.n

                if sp_i == 0 and d_ffn is not None:
                    P.wait("g", ss, b_done)
                    P.dma("g", d_ffn, ff16[:].rearrange("p c e -> p (c e)"), so)

                P.wait("t", ss, b_done)
                for dc in range(NDC):
                    P.op("t", "matmul", psF[0][0:1, :], onescolb[:, 0:1], ff16[:, dc, :],
                         start=(dc == 0), stop=(dc == NDC - 1),
                         inc=(sp if dc == NDC - 1 else None))
                sqmm = {}
                for dc in range(NDC):
                    if dc >= 2:
                        P.wait("s", sp, sqmm[dc - 2])
                    P.op("s", "activation", sq16[:, dc % 2, :], ff16[:, dc, :],
                         AF.Square, inc=ss)
                    P.wait("t", ss, ss.n)
                    P.op("t", "matmul", psF[1][0:1, :], onescol[:, 0:1], sq16[:, dc % 2, :],
                         start=(dc == 0), stop=(dc == NDC - 1), inc=sp)
                    sqmm[dc] = sp.n
                sumsq_done = sp.n
                P.wait("v", sp, sumsq_done)
                P.op("v", "tensor_scalar", rows[0:1, 0, :], psF[0][0:1, :], 1.0 / D, None,
                     ALU.mult, inc=sv)
                P.op("v", "tensor_scalar", rows[0:1, 1, :], psF[1][0:1, :], 1.0 / D, None,
                     ALU.mult, inc=sv)
                P.wait("v", sv, sv.n)
                P.op("v", "tensor_mul", rows[0:1, 2, :], rows[0:1, 0, :], rows[0:1, 0, :], inc=sv)
                P.wait("v", sv, sv.n)
                P.op("v", "tensor_sub", rows[0:1, 1, :], rows[0:1, 1, :], rows[0:1, 2, :], inc=sv)
                P.wait("v", sv, sv.n)
                P.wait("s", sv, sv.n)
                P.op("s", "activation", rows[0:1, 3, :], rows[0:1, 1, :], AF.Ln,
                     bias=epst[0:1, 0:1], inc=ss)
                P.op("s", "activation", rows[0:1, 4, :], rows[0:1, 3, :], AF.Exp,
                     scale=-0.5, inc=ss)
                P.wait("v", ss, ss.n)
                P.op("v", "scalar_tensor_tensor", rows[0:1, 5, :], rows[0:1, 0, :], -1.0,
                     rows[0:1, 4, :], ALU.mult, ALU.mult, inc=sv)
                P.wait("v", sv, sv.n)
                P.op("v", "tensor_copy", rowsh[0:1, 0, :], rows[0:1, 4, :], inc=sv)
                P.op("v", "tensor_copy", rowsh[0:1, 1, :], rows[0:1, 5, :], inc=sv)
                P.wait("t", sv, sv.n)
                P.op("t", "matmul", psF[2][:], onesrowh[:], rowsh[0:1, 0, :],
                     start=True, stop=True, inc=sp)
                P.op("t", "matmul", psF[3][:], onesrowh[:], rowsh[0:1, 1, :],
                     start=True, stop=True, inc=sp)
                P.wait("v", sp, sp.n)
                P.op("v", "tensor_copy", rstd16[:], psF[2][:], inc=sv)
                P.op("v", "tensor_copy", shiftb[:], psF[3][:], inc=sv)
                bcast_done = sv.n
                P.wait("t", sv, bcast_done)
                if sp_i == 0 and d_fst is not None:
                    P.wait("g", sv, bcast_done)
                    P.dma("g", d_fst, rows[0:1, :, :].rearrange("p a b -> p (a b)"), so)
                    P.dma("g", d_frs, rstd16[:], so)
                    P.dma("g", d_fsh, shiftb[:], so)

                for tl in range(4):
                    tt = 4 * sp_i + tl
                    P.wait("y", sv, sv.n)
                    P.dma("y", ysp[:], yscr[128 * tt:128 * (tt + 1), :], sx)
                    y_ld = sx.n
                    P.wait("v", sx, y_ld)
                    if tt >= 2:
                        P.wait("v", so, outdma[tt - 2])
                    tr_done = {}
                    for dc in range(NDC):
                        lb = dc % 2
                        if dc >= 2:
                            P.wait("v", sp, tr_done[dc - 2])
                        P.op("v", "tensor_mul", lnt32[:, lb, :],
                             ff16[:, dc, 128 * tl:128 * (tl + 1)],
                             rstd16[:, 128 * tl:128 * (tl + 1)], inc=sv)
                        P.wait("v", sv, sv.n)
                        P.op("v", "tensor_add", lnt32[:, lb, :], lnt32[:, lb, :],
                             shiftb[:, 128 * tl:128 * (tl + 1)], inc=sv)
                        P.wait("t", sv, sv.n)
                        bank = 4 + (dc % 4)
                        P.op("t", "transpose", psF[bank][:, 0:128], lnt32[:, lb, :],
                             id32[:], inc=sp)
                        tr_done[dc] = sp.n
                        P.wait("v", sp, tr_done[dc])
                        P.op("v", "tensor_add", otile[:, tl % 2, 128 * dc:128 * (dc + 1)],
                             psF[bank][:, 0:128], ysp[:, 128 * dc:128 * (dc + 1)],
                             inc=sv)
                    P.wait("y", sv, sv.n)
                    if tt == 0 and d_fot is not None:
                        P.dma("g", d_fot, otile[:, 0, :], so)
                    P.dma("y", out_t[128 * tt:128 * (tt + 1), :], otile[:, tl % 2, :], so)
                    outdma[tt] = so.n
            P.wait("y", so, so.n)
            P.flush()

    print(f"[mk] total instructions: {P.ninstr}")
    return nc



def make_inputs(L0_attn_x, L0_attn_Wqkv_w, L0_attn_Wqkv_b, L0_attn_out_w,
                L0_attn_out_b, L0_ffn_w1, L0_ffn_b1, L0_ffn_w2, L0_ffn_b2):
    f = np.float32
    wqkvT = np.ascontiguousarray(np.asarray(L0_attn_Wqkv_w, f).T)
    woutT = np.ascontiguousarray(np.asarray(L0_attn_out_w, f).T)
    w1T = np.ascontiguousarray(np.asarray(L0_ffn_w1, f).T)
    w2T = np.ascontiguousarray(np.asarray(L0_ffn_w2, f).T)
    qkvb = np.ascontiguousarray(np.asarray(L0_attn_Wqkv_b, f).reshape(18, 128).T)
    kb = np.ascontiguousarray(np.broadcast_to(np.asarray(L0_attn_Wqkv_b[2048:2176], f), (128, 128)))
    vb = np.ascontiguousarray(np.broadcast_to(np.asarray(L0_attn_Wqkv_b[2176:2304], f), (128, 128)))
    outb = np.ascontiguousarray(np.broadcast_to(
        np.asarray(L0_attn_out_b, f).astype(ml_dtypes.bfloat16), (128, D)))
    b1c = np.ascontiguousarray(np.asarray(L0_ffn_b1, f).reshape(NFC, 128).T)
    b2c = np.ascontiguousarray(np.asarray(L0_ffn_b2, f).reshape(NDC, 128).T)
    id16 = np.eye(128, dtype=ml_dtypes.bfloat16)
    idh = np.eye(128, dtype=np.float16)
    id32 = np.eye(128, dtype=f)
    onescol = np.ones((128, 4), f)
    onescolb = np.ones((128, 4), ml_dtypes.bfloat16)
    onescolh = np.ones((128, 4), np.float16)
    onesrow = np.ones((1, 128), f)
    onesrowh = np.ones((1, 128), ml_dtypes.bfloat16)
    iov = np.concatenate([np.arange(512, dtype=np.float16),
                          np.arange(512, dtype=np.float16) - np.float16(128.0)])
    iota = np.ascontiguousarray(np.broadcast_to(iov, (128, 1024)))

    x = np.asarray(L0_attn_x, f)
    in_maps = []
    layout = []
    for c in range(8):
        b, j = divmod(c, 4)
        plist = p_list_for(j)
        xo = np.ascontiguousarray(
            np.concatenate([x[b, 512 * p:512 * (p + 1), :] for p in plist], axis=0))
        cth = np.zeros((128, 64), np.float32)
        pvec = np.arange(128, dtype=np.float32)
        for s_i in range(4):
            Pm = plist[s_i]
            for cc in range(16 * s_i, 16 * (s_i + 1)):
                t = 128.0 * cc + pvec - 512.0 * Pm
                cth[:, cc] = np.clip(t, -640.0, 520.0)
        cth = cth.astype(np.float32)
        in_maps.append({
            "xseq": np.ascontiguousarray(x[b]),
            "xown": xo,
            "wqkvT": wqkvT, "woutT": woutT, "w1T": w1T, "w2T": w2T,
            "qkvb": qkvb, "kb": kb, "vb": vb, "outb": outb,
            "b1c": b1c, "b2c": b2c,
            "id16": id16, "idh": idh, "id32": id32,
            "onescol": onescol, "onescolb": onescolb, "onescolh": onescolh,
            "onesrow": onesrow, "onesrowh": onesrowh,
            "iota": iota, "cthr": cth,
        })
        layout.append((b, plist))
    return in_maps, layout


def kernel(L0_attn_x, L0_attn_Wqkv_w, L0_attn_Wqkv_b, L0_attn_out_w, L0_attn_out_b,
           L0_ffn_w1, L0_ffn_b1, L0_ffn_w2, L0_ffn_b2, debug=()):
    in_maps, layout = make_inputs(L0_attn_x, L0_attn_Wqkv_w, L0_attn_Wqkv_b,
                                  L0_attn_out_w, L0_attn_out_b, L0_ffn_w1,
                                  L0_ffn_b1, L0_ffn_w2, L0_ffn_b2)
    nc = build(debug=debug)
    res = run_bass_kernel_spmd(nc, in_maps, core_ids=list(range(8)))
    out = np.empty((2, S, D), np.float32)
    for c in range(8):
        b, plist = layout[c]
        o = res.results[c]["out"]
        for si, p in enumerate(plist):
            out[b, 512 * p:512 * (p + 1), :] = o[512 * si:512 * (si + 1), :]
    if debug:
        return out, res
    return out


# version 09
# speedup vs baseline: 1.0365x; 1.0365x over previous
import contextlib
import math
import numpy as np
import ml_dtypes
import concourse.bass as bass
import concourse.mybir as mybir
from concourse.bass_utils import run_bass_kernel_spmd

F32 = mybir.dt.float32
BF16 = mybir.dt.bfloat16
FP16 = mybir.dt.float16
AF = mybir.ActivationFunctionType
ALU = mybir.AluOpType
AX = mybir.AxisListType.X

D = 2048
S = 8192
HEADS = 16
DH = 128
DFF = 8192
NDC = D // 128
NFC = DFF // 128
NKC = S // 128
NT = 16
NSLOT = 4
SCALE = 1.0 / math.sqrt(DH)
EPS = 1e-5
CM_CONST = -0.5 * math.log(128.0) + math.log(1.05)

ENGINES = ("t", "v", "s", "y", "g")


class Sem:
    def __init__(self, stack, nc, name):
        self.h = stack.enter_context(nc.semaphore(name))
        self.n = 0

    def bump(self, k=1):
        self.n += k
        return self.n


class Prog:

    def __init__(self, nc):
        self.nc = nc
        self.ops = []
        self.ninstr = 0

    def op(self, eng, name, *args, inc=None, k=1, **kw):
        self.ops.append((eng, name, args, kw, inc, k))
        self.ninstr += 1
        if inc is not None:
            return inc.bump(k)
        return None

    def wait(self, eng, sem, n):
        if n > 0:
            self.ops.append((eng, "wait_ge", (sem.h, n), {}, None, 0))
            self.ninstr += 1

    def dma(self, eng, out, in_, inc, **kw):
        return self.op(eng, "dma_start", inc=inc, k=16, out=out, in_=in_, **kw)

    def flush(self):
        streams = {e: [] for e in ENGINES}
        for o in self.ops:
            streams[o[0]].append(o)
        self.ops = []

        def runner(engname):
            def run(e):
                for (_, name, args, kw, inc, k) in streams[engname]:
                    ins = getattr(e, name)(*args, **kw)
                    if inc is not None:
                        ins.then_inc(inc.h, k)
            return run

        with self.nc.Block() as block:
            for attr, key in (("tensor", "t"), ("vector", "v"), ("scalar", "s"),
                              ("sync", "y"), ("gpsimd", "g")):
                if streams[key]:
                    getattr(block, attr)(runner(key))


def p_list_for(j):
    return sorted({j, 7 - j, 8 + j, 15 - j})


def build(debug=()):
    nc = bass.Bass("TRN2", target_bir_lowering=False)
    dbg = set(debug)

    xseq = nc.dram_tensor("xseq", [S, D], F32, kind="ExternalInput").ap()
    xown = nc.dram_tensor("xown", [NT * 128, D], F32, kind="ExternalInput").ap()
    wqkvT = nc.dram_tensor("wqkvT", [D, 2304], F32, kind="ExternalInput").ap()
    woutT = nc.dram_tensor("woutT", [D, D], F32, kind="ExternalInput").ap()
    w1T = nc.dram_tensor("w1T", [D, DFF], F32, kind="ExternalInput").ap()
    w2T = nc.dram_tensor("w2T", [DFF, D], F32, kind="ExternalInput").ap()
    qkvb_i = nc.dram_tensor("qkvb", [128, 18], F32, kind="ExternalInput").ap()
    kb_i = nc.dram_tensor("kb", [128, 128], F32, kind="ExternalInput").ap()
    vb_i = nc.dram_tensor("vb", [128, 128], F32, kind="ExternalInput").ap()
    outb_i = nc.dram_tensor("outb", [128, D], BF16, kind="ExternalInput").ap()
    b1c_i = nc.dram_tensor("b1c", [128, NFC], F32, kind="ExternalInput").ap()
    b2c_i = nc.dram_tensor("b2c", [128, NDC], F32, kind="ExternalInput").ap()
    id16_i = nc.dram_tensor("id16", [128, 128], BF16, kind="ExternalInput").ap()
    idh_i = nc.dram_tensor("idh", [128, 128], FP16, kind="ExternalInput").ap()
    id32_i = nc.dram_tensor("id32", [128, 128], F32, kind="ExternalInput").ap()
    onescol_i = nc.dram_tensor("onescol", [128, 4], F32, kind="ExternalInput").ap()
    onescolb_i = nc.dram_tensor("onescolb", [128, 4], BF16, kind="ExternalInput").ap()
    onescolh_i = nc.dram_tensor("onescolh", [128, 4], FP16, kind="ExternalInput").ap()
    onesrow_i = nc.dram_tensor("onesrow", [1, 128], F32, kind="ExternalInput").ap()
    onesrowh_i = nc.dram_tensor("onesrowh", [1, 128], BF16, kind="ExternalInput").ap()
    iota_i = nc.dram_tensor("iota", [128, 1024], FP16, kind="ExternalInput").ap()
    cthr_i = nc.dram_tensor("cthr", [128, 64], F32, kind="ExternalInput").ap()

    yscr = nc.dram_tensor("yscr", [NT * 128, D], F32, kind="Internal").ap()
    ln2scr = nc.dram_tensor("ln2scr", [NT, NDC, 128, 128], BF16, kind="Internal").ap()
    w1q = nc.dram_tensor("w1q", [NDC, 128, DFF], BF16, kind="Internal").ap()
    w2q = nc.dram_tensor("w2q", [NFC, 128, D], BF16, kind="Internal").ap()

    out_t = nc.dram_tensor("out", [NT * 128, D], F32, kind="ExternalOutput").ap()
    dbg_t = {}

    def dbg_tensor(name, shape, dt=F32):
        if name in dbg:
            dbg_t[name] = nc.dram_tensor("dbg_" + name, shape, dt, kind="ExternalOutput").ap()
        return dbg_t.get(name)

    d_kfm = dbg_tensor("kfm", [128, NKC * 128])
    d_v = dbg_tensor("vst", [128, NKC * 128])
    d_qfm = dbg_tensor("qfm", [128, HEADS * 512])
    d_cfm = dbg_tensor("cfm", [128, HEADS * 512])
    d_wq1 = dbg_tensor("wq1", [128, DFF])
    d_thr = dbg_tensor("thr", [128, 8])
    d_ln1 = dbg_tensor("ln1t0", [128, NDC * 128])
    d_st = dbg_tensor("stat", [128, 16])
    d_y = dbg_tensor("y", [NT * 128, D])
    d_ffn = dbg_tensor("ffn", [128, NDC * 512])
    d_fst = dbg_tensor("fst", [1, 8 * 512])
    d_frs = dbg_tensor("frs", [128, 512], BF16)
    d_fsh = dbg_tensor("fsh", [128, 512])
    d_fot = dbg_tensor("fot", [128, D])

    with contextlib.ExitStack() as gst:
        E = gst.enter_context
        sx = Sem(gst, nc, "sx")
        sg = Sem(gst, nc, "sg")
        sp = Sem(gst, nc, "sp")
        sv = Sem(gst, nc, "sv")
        ss = Sem(gst, nc, "ss")
        so = Sem(gst, nc, "so")
        ALLSEMS = (sx, sg, sp, sv, ss, so)

        id16 = E(nc.sbuf_tensor("id16s", [128, 128], BF16))
        idh = E(nc.sbuf_tensor("idhs", [128, 128], FP16))
        id32 = E(nc.sbuf_tensor("id32s", [128, 128], F32))
        onescol = E(nc.sbuf_tensor("onescols", [128, 4], F32))
        onescolb = E(nc.sbuf_tensor("onescolbs", [128, 4], BF16))
        onescolh = E(nc.sbuf_tensor("onescolhs", [128, 4], FP16))
        onesrow = E(nc.sbuf_tensor("onesrows", [1, 128], F32))
        onesrowh = E(nc.sbuf_tensor("onesrowhs", [1, 128], BF16))
        iota = E(nc.sbuf_tensor("iotas", [128, 1024], FP16))
        cthr = E(nc.sbuf_tensor("cthrs", [128, 64], F32))
        qkvb = E(nc.sbuf_tensor("qkvbs", [128, 18], F32))
        kb_s = E(nc.sbuf_tensor("kbs", [128, 128], F32))
        vb_s = E(nc.sbuf_tensor("vbs", [128, 128], F32))
        outb_s = E(nc.sbuf_tensor("outbs", [128, D], BF16))
        b1c = E(nc.sbuf_tensor("b1cs", [128, NFC], F32))
        b2c = E(nc.sbuf_tensor("b2cs", [128, NDC], F32))
        cmt = E(nc.sbuf_tensor("cmt", [128, 1], F32))
        epst = E(nc.sbuf_tensor("epst", [128, 1], F32))
        negMb = E(nc.sbuf_tensor("negMb", [128, 1], F32))
        stt = E(nc.sbuf_tensor("stt", [128, 16], F32))
        kmax2 = E(nc.sbuf_tensor("kmax2", [1, 4], F32))
        qmax2 = E(nc.sbuf_tensor("qmax2", [1, 4], F32))
        sqjunk = E(nc.sbuf_tensor("sqjunk", [128, 1024], F32))
        thr1 = E(nc.sbuf_tensor("thr1", [128, 4], F32))
        absacc = E(nc.sbuf_tensor("absacc", [128, 2], F32))
        mtmp = E(nc.sbuf_tensor("mtmp", [128, 8], F32))

        P = Prog(nc)

        def barrier():
            for e in ENGINES:
                for sem in ALLSEMS:
                    P.wait(e, sem, sem.n)

        for dst, src in ((id16, id16_i), (idh, idh_i), (id32, id32_i),
                         (onescol, onescol_i), (onescolb, onescolb_i),
                         (onescolh, onescolh_i), (onesrow, onesrow_i),
                         (onesrowh, onesrowh_i), (iota, iota_i), (cthr, cthr_i),
                         (qkvb, qkvb_i), (kb_s, kb_i), (vb_s, vb_i),
                         (outb_s, outb_i), (b1c, b1c_i), (b2c, b2c_i)):
            P.dma("y", dst[:], src, sx)
        P.op("v", "memset", cmt[:], CM_CONST, inc=sv)
        P.op("v", "memset", epst[:], EPS, inc=sv)
        consts_done = sx.n


        with contextlib.ExitStack() as ast_:
            EA = ast_.enter_context
            kfm = EA(nc.sbuf_tensor("kfm", [128, NKC, 128], BF16))
            vst = EA(nc.sbuf_tensor("vst", [128, NKC, 128], BF16))
            qfm = EA(nc.sbuf_tensor("qfm", [128, HEADS, 512], BF16))
            cfm = EA(nc.sbuf_tensor("cfm", [128, HEADS, 512], BF16))
            wbig = EA(nc.sbuf_tensor("wbig", [128, NDC, D], BF16))
            xbuf = EA(nc.sbuf_tensor("xbuf", [128, 2, D], F32))
            ln1own = EA(nc.sbuf_tensor("ln1own", [128, NDC, 512], BF16))
            pt = EA(nc.sbuf_tensor("pt", [128, 2, 1024], BF16))
            maskb = EA(nc.sbuf_tensor("maskb", [128, 1024], BF16))
            ktmp = EA(nc.sbuf_tensor("ktmp", [128, 128], BF16))
            rlrow = EA(nc.sbuf_tensor("rlrow", [1, 2, 512], F32))
            rl16 = EA(nc.sbuf_tensor("rl16", [1, 512], BF16))
            rlb = EA(nc.sbuf_tensor("rlb", [128, 512], F32))
            qsq = None
            ao = EA(nc.sbuf_tensor("ao", [128, D], F32))
            ln2tm = EA(nc.sbuf_tensor("ln2tm", [128, D], BF16))
            qsq = pt[:, 0, 0:512]
            ln1fm = ln1own[:, 8:12, :].rearrange("p c (a e) -> p (c a) e", a=4)
            l2b = maskb[:].rearrange("p (c e) -> p c e", c=8)
            psAB = [EA(nc.psum_tensor(f"psAB{i}", [128, 1024], F32)) for i in range(2)]
            psA = [psAB[0][:, 0:512], psAB[1][:, 0:512]]
            psCc = EA(nc.psum_tensor("psCc", [128, 512], F32))
            psL = EA(nc.psum_tensor("psL", [128, 512], F32))
            psS = EA(nc.psum_tensor("psS", [128, 512], F32))
            psT = EA(nc.psum_tensor("psT", [128, 1024], BF16))

            wkv_t = EA(nc.sbuf_tensor("wkv", [128, NDC, 256], BF16))
            wkv = wkv_t[:]
            P.dma("g", wkv, wqkvT[:, 2048:2304].rearrange("(c p) e -> p c e", p=128), sg)
            wkv_done = sg.n

            def C(par, i):
                return stt[:, 8 * par + i:8 * par + i + 1]

            def ln_stats1(src, par):
                P.op("v", "tensor_reduce", C(par, 0), src, AX, ALU.add, inc=sv)
                P.wait("s", sv, sv.n)
                P.op("s", "activation", sqjunk[:], src[:, 0:1024], AF.Square,
                     accum_out=C(par, 2), inc=ss)
                P.op("s", "activation", sqjunk[:], src[:, 1024:2048], AF.Square,
                     accum_out=C(par, 1), inc=ss)
                return ss.n

            def ln_stats2(par, s1done):
                P.wait("v", ss, s1done)
                P.op("v", "tensor_add", C(par, 1), C(par, 1), C(par, 2), inc=sv)
                P.op("v", "tensor_scalar", C(par, 3), C(par, 0), 1.0 / D, None, ALU.mult, inc=sv)
                P.wait("v", sv, sv.n)
                P.op("v", "tensor_mul", C(par, 4), C(par, 3), C(par, 3), inc=sv)
                P.op("v", "tensor_scalar", C(par, 2), C(par, 1), 1.0 / D, None, ALU.mult, inc=sv)
                P.wait("v", sv, sv.n)
                P.op("v", "tensor_sub", C(par, 4), C(par, 2), C(par, 4), inc=sv)
                P.wait("v", sv, sv.n)
                P.wait("s", sv, sv.n)
                P.op("s", "activation", C(par, 7), C(par, 4), AF.Ln, bias=epst[:, 0:1], inc=ss)
                P.op("s", "activation", C(par, 5), C(par, 7), AF.Exp, scale=-0.5, inc=ss)
                P.wait("v", ss, ss.n)
                P.op("v", "scalar_tensor_tensor", C(par, 6), C(par, 3), -1.0,
                     C(par, 5), ALU.mult, ALU.mult, inc=sv)
                P.wait("s", sv, sv.n)

            def ln_stats(src, par=0):
                d1 = ln_stats1(src, par)
                ln_stats2(par, d1)

            def norm_cast_transpose(src, dst_chunked, par=0):
                stg = (pt[:, 0, 0:512], pt[:, 0, 512:1024], pt[:, 1, 0:512], pt[:, 1, 512:1024])
                P.wait("s", sp, sp.n)
                for q4 in range(4):
                    P.op("s", "activation", stg[q4], src[:, 512 * q4:512 * (q4 + 1)],
                         AF.Identity, bias=C(par, 6), scale=C(par, 5), inc=ss)
                norm_done = ss.n
                P.wait("t", ss, norm_done)
                for half in range(2):
                    for i in range(8):
                        blk = 8 * half + i
                        q4, r = divmod(blk, 4)
                        P.op("t", "transpose", psT[:, 128 * i:128 * (i + 1)],
                             stg[q4][:, 128 * r:128 * (r + 1)], id16[:],
                             inc=(sp if i == 7 else None))
                    P.wait("v", sp, sp.n)
                    P.op("v", "tensor_copy",
                         dst_chunked[:, 8 * half:8 * half + 8, :],
                         psT[:, 0:1024].rearrange("p (c e) -> p c e", c=8), inc=sv)
                    P.wait("t", sv, sv.n)
                return norm_done

            wfa2 = cfm[:].bitcast(F32).rearrange("p c e -> p (c e)").rearrange(
                "p (s e) -> p s e", s=2)
            wfa3 = ao[:]
            l1B = ln1own[:, 0:8, :].rearrange("p c e -> p (c e)").rearrange(
                "p (s e) -> p s e", s=2)
            wqoB = l1B
            leb = ln2tm[:]

            def wslot(u):
                s_ = u % 3
                return wfa3 if s_ == 2 else wfa2[:, s_, :]

            def wsub_in(u):
                if u < 64:
                    i, s_ = divmod(u, 4)
                    return w1T[128 * i:128 * (i + 1), 2048 * s_:2048 * (s_ + 1)]
                return w2T[128 * (u - 64):128 * (u - 63), :]

            def wsub_out(u):
                if u < 64:
                    i, s_ = divmod(u, 4)
                    return w1q[i, :, 2048 * s_:2048 * (s_ + 1)]
                return w2q[u - 64, :, :]

            qload = {}
            quse = {}
            qdma = {}

            quse_s = {}

            def quant_load(L):
                u = L if L < 128 else L - 128
                if L >= 3:
                    P.wait("y", sv, quse.get(L - 3, 0))
                    P.wait("y", ss, quse_s.get(L - 3, 0))
                P.dma("y", wslot(L), wsub_in(u), sx)
                qload[L] = sx.n

            mean_vdone = {}

            def quant_mean_sub(u):
                p2 = (u % 2) * 2
                P.wait("s", sx, qload[u])
                if u >= 2:
                    P.wait("s", sv, mean_vdone[u - 2])
                P.op("s", "activation", sqjunk[:], wslot(u)[:, 0:1024], AF.Abs,
                     accum_out=mtmp[:, p2:p2 + 1], inc=ss)
                P.op("s", "activation", sqjunk[:], wslot(u)[:, 1024:2048], AF.Abs,
                     accum_out=mtmp[:, p2 + 1:p2 + 2], inc=ss)
                quse_s[u] = ss.n
                quse[u] = 0
                tgt = absacc[:, 0:1] if u < 64 else absacc[:, 1:2]
                P.wait("v", ss, ss.n)
                P.op("v", "tensor_add", mtmp[:, 4 + p2 // 2:5 + p2 // 2],
                     mtmp[:, p2:p2 + 1], mtmp[:, p2 + 1:p2 + 2], inc=sv)
                P.wait("v", sv, sv.n)
                if u == 0 or u == 64:
                    P.op("v", "tensor_copy", tgt, mtmp[:, 4 + p2 // 2:5 + p2 // 2], inc=sv)
                else:
                    P.op("v", "tensor_add", tgt, tgt, mtmp[:, 4 + p2 // 2:5 + p2 // 2], inc=sv)
                P.wait("v", sv, sv.n)
                mean_vdone[u] = sv.n

            def quant_thr():
                P.wait("t", sv, sv.n)
                P.op("t", "matmul", psL[0:1, 0:2], onescol[:, 0:1], absacc[:, 0:2],
                     start=True, stop=True, inc=sp)
                P.wait("v", sp, sp.n)
                P.op("v", "tensor_scalar", mtmp[0:1, 0:2], psL[0:1, 0:2],
                     0.7 / float(DFF * D), None, ALU.mult, inc=sv)
                P.wait("t", sv, sv.n)
                P.op("t", "matmul", psS[:, 0:2], onesrow[:], mtmp[0:1, 0:2],
                     start=True, stop=True, inc=sp)
                P.wait("v", sp, sp.n)
                P.op("v", "tensor_copy", thr1[:, 0:2], psS[:, 0:2], inc=sv)
                P.wait("v", sv, sv.n)
                P.op("v", "tensor_scalar", thr1[:, 2:4], thr1[:, 0:2], -1.0, None,
                     ALU.mult, inc=sv)
                P.wait("v", sv, sv.n)
                if d_thr is not None:
                    P.wait("g", sv, sv.n)
                    P.dma("g", d_thr[:, 0:4], thr1[:], so)
                    P.dma("g", d_thr[:, 4:6], absacc[:], so)

            def quant_q_sub(u):
                L = 128 + u
                tcol = 0 if u < 64 else 1
                P.wait("v", sx, qload[L])
                P.op("v", "tensor_scalar", leb, wslot(L),
                     thr1[:, 2 + tcol:3 + tcol], None, ALU.is_le, inc=sv)
                if u >= 2:
                    P.wait("v", sg, qdma[u - 2])
                P.wait("v", sv, sv.n)
                P.op("v", "scalar_tensor_tensor", wqoB[:, u % 2, :], wslot(L),
                     thr1[:, tcol:tcol + 1], leb, ALU.is_ge, ALU.subtract, inc=sv)
                quse[L] = sv.n
                quse_s[L] = 0
                P.wait("g", sv, quse[L])
                P.dma("g", wsub_out(u), wqoB[:, u % 2, :], sg)
                qdma[u] = sg.n

            P.wait("t", sg, wkv_done)
            P.wait("t", sx, consts_done)
            tile_done = [None] * NKC
            s1d = [None] * NKC

            xld = [None] * (NKC + 1)
            xguard = {0: [0, 0], 1: [0, 0]}

            def x_acquire(buf):
                P.wait("y", ss, xguard[buf][0])
                P.wait("y", sv, xguard[buf][1])

            def x_load(k):
                if k >= NKC:
                    return
                x_acquire(k % 2)
                P.dma("y", xbuf[:, k % 2, :], xseq[128 * k:128 * (k + 1), :], sx)
                xld[k] = sx.n

            def kv_part1(k):
                buf = k % 2
                P.wait("v", sx, xld[k])
                P.wait("s", sx, xld[k])
                s1d[k] = ln_stats1(xbuf[:, buf, :], k % 2)

            def kv_evac(k, pa):
                P.wait("v", sp, sp.n)
                P.op("v", "scalar_tensor_tensor", ktmp[:], pa[:, 0:128], 1.0, kb_s[:],
                     ALU.bypass, ALU.add, inc=sv)
                P.op("v", "scalar_tensor_tensor", vst[:, k, 0:128], pa[:, 128:256], 1.0,
                     vb_s[:], ALU.bypass, ALU.add, inc=sv)
                P.wait("t", sv, sv.n)
                P.op("t", "transpose", psT[:, 0:128], ktmp[:], id16[:], inc=sp)
                P.wait("v", sp, sp.n)
                P.op("v", "tensor_copy", kfm[:, k, :], psT[:, 0:128], inc=sv)
                P.wait("t", sv, sv.n)

            def kv_part2(k):
                buf = k % 2
                par = k % 2
                ln_stats2(par, s1d[k])
                tile_done[k] = norm_cast_transpose(xbuf[:, buf, :], ln1fm, par)
                xguard[buf][0] = tile_done[k]
                pa = psA[k % 2][:, 0:256]
                for dc in range(NDC):
                    P.op("t", "matmul", pa, ln1fm[:, dc, :], wkv[:, dc, :],
                         start=(dc == 0), stop=(dc == NDC - 1),
                         inc=(sp if dc == NDC - 1 else None))
                kv_evac(k, pa)

            psT32 = psT[:].bitcast(F32)

            ln2rd = [0]

            def kv_stage(k):
                par = k % 2
                ln_stats2(par, s1d[k])
                P.wait("s", sp, ln2rd[0])
                for q4 in range(4):
                    P.op("s", "activation", ln2tm[:, 512 * q4:512 * (q4 + 1)],
                         xbuf[:, k % 2, 512 * q4:512 * (q4 + 1)], AF.Identity,
                         bias=C(par, 6), scale=C(par, 5), inc=ss)
                tile_done[k] = ss.n
                xguard[k % 2][0] = ss.n

            def kv_finish(k):
                P.wait("t", ss, tile_done[k])
                for half in range(2):
                    for i in range(8):
                        blk = 8 * half + i
                        P.op("t", "transpose", psT[:, 128 * i:128 * (i + 1)],
                             ln2tm[:, 128 * blk:128 * (blk + 1)], id16[:],
                             inc=(sp if i == 7 else None))
                    P.wait("v", sp, sp.n)
                    P.op("v", "tensor_copy", ln1fm[:, 8 * half:8 * half + 8, :],
                         psT[:, 0:1024].rearrange("p (c e) -> p c e", c=8), inc=sv)
                    P.wait("t", sv, sv.n)
                ln2rd[0] = sp.n
                pa = psT32[:, 0:256]
                for dc in range(NDC):
                    P.op("t", "matmul", pa, ln1fm[:, dc, :], wkv[:, dc, :],
                         start=(dc == 0), stop=(dc == NDC - 1),
                         inc=(sp if dc == NDC - 1 else None))
                kv_evac(k, pa)

            def knorm_group(g):
                for half in range(2):
                    c0 = 16 * g + 8 * half
                    P.wait("s", sp, sp.n)
                    P.wait("s", sv, sv.n)
                    P.op("s", "activation", sqjunk[:, 0:512],
                         kfm[:, c0:c0 + 4, :].rearrange("p c e -> p (c e)"),
                         AF.Square, inc=ss)
                    P.op("s", "activation", sqjunk[:, 512:1024],
                         kfm[:, c0 + 4:c0 + 8, :].rearrange("p c e -> p (c e)"),
                         AF.Square, inc=ss)
                    P.wait("t", ss, ss.n)
                    P.op("t", "matmul", psS[0:1, :], onescol[:, 0:1], sqjunk[:, 0:512],
                         start=True, stop=False, skip_group_check=True)
                    P.op("t", "matmul", psS[0:1, :], onescol[:, 0:1], sqjunk[:, 512:1024],
                         start=False, stop=True, skip_group_check=True, inc=sp)
                    P.wait("v", sp, sp.n)
                    P.op("v", "tensor_reduce", kmax2[0:1, 1:2], psS[0:1, :], AX,
                         ALU.max, inc=sv)
                    P.wait("v", sv, sv.n)
                    if g == 0 and half == 0:
                        P.op("v", "tensor_copy", kmax2[0:1, 0:1], kmax2[0:1, 1:2], inc=sv)
                    else:
                        P.op("v", "tensor_max", kmax2[0:1, 0:1], kmax2[0:1, 0:1],
                             kmax2[0:1, 1:2], inc=sv)
                    P.wait("v", sv, sv.n)
                    P.wait("t", sv, sv.n)

            x_load(0)
            quant_load(0)
            quant_load(1)
            next_load = [2]

            def quant_step(Lc):
                if next_load[0] < 256:
                    quant_load(next_load[0])
                    next_load[0] += 1
                if Lc < 128:
                    quant_mean_sub(Lc)
                else:
                    quant_q_sub(Lc - 128)

            for k in range(17):
                if k < 16:
                    kv_part1(k)
                if k == 9:
                    quant_thr()
                if k >= 1:
                    kv_part2(k - 1)
                if k < 16:
                    x_load(k + 1)
                if k >= 1:
                    for Lc in range(16 * (k - 1), 16 * k):
                        quant_step(Lc)

            for e in ENGINES:
                P.wait(e, sx, sx.n)
                P.wait(e, sg, sg.n)
                P.wait(e, sv, sv.n)
                P.wait(e, ss, ss.n)
            knorm_group(0)
            kmax_done = sv.n

            ydma_prev = (0, 0)
            l2dma_prev = 0
            for s_i in range(NSLOT):
                Es = 16 * (s_i + 1)
                P.wait("v", sx, sx.n)
                P.wait("g", sp, sp.n)
                P.dma("g", wbig[:], wqkvT[:, 0:2048].rearrange("(c p) e -> p c e", p=128), sg)
                wq_loaded = sg.n

                for tl in range(4):
                    tt = 4 * s_i + tl
                    buf = tl % 2
                    x_acquire(buf)
                    P.dma("y", xbuf[:, buf, :], xown[128 * tt:128 * (tt + 1), :], sx)
                    P.wait("v", sx, sx.n)
                    P.wait("s", sx, sx.n)
                    ln_stats(xbuf[:, buf, :], tl % 2)
                    norm_cast_transpose(
                        xbuf[:, buf, :],
                        ln1own[:, :, 128 * tl:128 * (tl + 1)], tl % 2)
                    xguard[buf][0] = ss.n
                P.wait("t", sg, wq_loaded)
                qev = {}
                for h in range(HEADS):
                    pa = psA[h % 2]
                    if h >= 2:
                        P.wait("t", ss, qev[h - 2])
                    for dc in range(NDC):
                        P.op("t", "matmul", pa[:, 0:512], wbig[:, dc, 128 * h:128 * (h + 1)],
                             ln1own[:, dc, :], start=(dc == 0), stop=(dc == NDC - 1),
                             inc=(sp if dc == NDC - 1 else None))
                    P.wait("s", sp, sp.n)
                    P.op("s", "activation", qfm[:, h, :],
                         pa[:, 0:512], AF.Identity, bias=qkvb[:, h:h + 1], inc=ss)
                    qev[h] = ss.n
                qproj_done = ss.n

                for h in range(HEADS):
                    P.wait("s", sp, sp.n)
                    if h == 0:
                        P.wait("s", ss, qproj_done)
                    P.op("s", "activation", qsq, qfm[:, h, :], AF.Square, inc=ss)
                    P.wait("t", ss, ss.n)
                    P.op("t", "matmul", psA[0][0:1, :], onescolb[:, 0:1], qsq,
                         start=True, stop=True, inc=sp)
                    P.wait("v", sp, sp.n)
                    P.op("v", "tensor_reduce", qmax2[0:1, 1:2], psA[0][0:1, :], AX,
                         ALU.max, inc=sv)
                    P.wait("v", sv, sv.n)
                    if h == 0:
                        P.op("v", "tensor_copy", qmax2[0:1, 0:1], qmax2[0:1, 1:2], inc=sv)
                    else:
                        P.op("v", "tensor_max", qmax2[0:1, 0:1], qmax2[0:1, 0:1],
                             qmax2[0:1, 1:2], inc=sv)
                    P.wait("v", sv, sv.n)
                    P.wait("t", sv, sv.n)
                P.wait("s", sv, sv.n)
                P.op("s", "activation", qmax2[0:1, 2:3], qmax2[0:1, 0:1], AF.Ln, inc=ss)
                P.op("s", "activation", kmax2[0:1, 2:3], kmax2[0:1, 0:1], AF.Ln, inc=ss)
                P.wait("v", ss, ss.n)
                P.op("v", "tensor_add", qmax2[0:1, 3:4], qmax2[0:1, 2:3],
                     kmax2[0:1, 2:3], inc=sv)
                P.wait("v", sv, sv.n)
                P.wait("s", sv, sv.n)
                P.op("s", "activation", qmax2[0:1, 1:2], qmax2[0:1, 3:4], AF.Exp,
                     scale=0.5, bias=cmt[0:1, 0:1], inc=ss)
                P.wait("v", ss, ss.n)
                P.op("v", "tensor_scalar", qmax2[0:1, 1:2], qmax2[0:1, 1:2], 1.0, None,
                     ALU.add, inc=sv)
                P.wait("v", sv, sv.n)
                P.op("v", "tensor_scalar", qmax2[0:1, 1:2], qmax2[0:1, 1:2], -1.0, None,
                     ALU.mult, inc=sv)
                P.wait("v", sv, sv.n)
                P.wait("t", sv, sv.n)
                P.op("t", "matmul", psA[0][:, 0:1], onesrow[:], qmax2[0:1, 1:2],
                     start=True, stop=True, inc=sp)
                P.wait("v", sp, sp.n)
                P.wait("v", ss, ss.n)
                P.op("v", "tensor_copy", negMb[:], psA[0][:, 0:1], inc=sv)
                negm_done = sv.n
                P.wait("s", sv, negm_done)
                P.wait("t", sv, negm_done)

                ctx_ev_done = {}
                sc_ev_done = {}
                if s_i < 3:
                    x_load(16 * (s_i + 1))
                for h in range(HEADS):
                    exp_done = {}
                    ready = {}
                    rsem = {}
                    ctx_done = {}

                    def emit_ctx(cp, h=h):
                        if cp == 0 and h >= 1:
                            P.wait("t", sv, ctx_ev_done[h - 1])
                        P.wait("t", rsem[cp], ready[cp])
                        pb = (cp // 2) % 2
                        psrc = pt
                        P.op("t", "matmul", psCc[:], vst[:, cp, 0:128],
                             psrc[:, pb, 0:512], start=(cp == 0), stop=False,
                             skip_group_check=True)
                        P.op("t", "matmul", psCc[:], vst[:, cp + 1, 0:128],
                             psrc[:, pb, 512:1024], start=False, stop=(cp == Es - 2),
                             skip_group_check=True)
                        P.op("t", "matmul", psL[0:1, :], onescolb[:, 0:1],
                             psrc[:, pb, 0:512], start=(cp == 0), stop=False,
                             skip_group_check=True)
                        P.op("t", "matmul", psL[0:1, :], onescolb[:, 0:1],
                             psrc[:, pb, 512:1024], start=False, stop=(cp == Es - 2),
                             skip_group_check=True, inc=sp)
                        ctx_done[cp] = sp.n

                    for cp in range(0, Es, 2):
                        pb = (cp // 2) % 2
                        pa = psAB[pb]
                        if cp >= 4:
                            P.wait("t", ss, exp_done[cp - 4])
                        P.op("t", "matmul", pa[:, 0:512], kfm[:, cp, :], qfm[:, h, :],
                             start=True, stop=True)
                        P.op("t", "matmul", pa[:, 512:1024], kfm[:, cp + 1, :], qfm[:, h, :],
                             start=True, stop=True, inc=sp)
                        sc_mm = sp.n
                        P.wait("s", sp, sc_mm)
                        if cp >= 4:
                            P.wait("s", sp, ctx_done[cp - 4])
                        P.op("s", "activation", pt[:, pb, :], pa[:, 0:1024], AF.Exp,
                             bias=negMb[:, 0:1], scale=SCALE, inc=ss)
                        exp_done[cp] = ss.n
                        if cp >= 16 * s_i:
                            P.op("v", "tensor_scalar", maskb[:], iota[:],
                                 cthr[:, cp:cp + 1], None, ALU.is_ge, inc=sv)
                            P.wait("v", sv, sv.n)
                            P.wait("v", ss, exp_done[cp])
                            P.op("v", "tensor_mul", pt[:, pb, :], pt[:, pb, :],
                                 maskb[:], inc=sv)
                            ready[cp], rsem[cp] = sv.n, sv
                        else:
                            ready[cp], rsem[cp] = exp_done[cp], ss
                        if cp >= 2:
                            emit_ctx(cp - 2)
                    emit_ctx(Es - 2)
                    if s_i < 3:
                        t_ = 16 * (s_i + 1) + h
                        if h < HEADS - 1:
                            x_load(t_ + 1)
                        kv_part1(t_)
                        kv_stage(t_)
                    P.wait("v", sp, ctx_done[Es - 2])
                    P.op("v", "reciprocal", rlrow[0:1, h % 2, :], psL[0:1, :], inc=sv)
                    P.wait("v", sv, sv.n)
                    P.op("v", "tensor_copy", rl16[0:1, :], rlrow[0:1, h % 2, :], inc=sv)
                    P.wait("t", sv, sv.n)
                    if h >= 1:
                        P.wait("t", ss, sc_ev_done[h - 1])
                    P.op("t", "matmul", psS[:], onesrowh[:], rl16[0:1, :],
                         start=True, stop=True, inc=sp)
                    P.wait("s", sp, sp.n)
                    P.op("s", "activation", rlb[:], psS[:], AF.Identity, inc=ss)
                    sc_ev_done[h] = ss.n
                    P.wait("v", ss, sc_ev_done[h])
                    P.op("v", "tensor_mul", cfm[:, h, :], psCc[:], rlb[:], inc=sv)
                    ctx_ev_done[h] = sv.n
                    if s_i < 3:
                        kv_finish(16 * (s_i + 1) + h)
                attn_done = sv.n
                if s_i < 3:
                    knorm_group(s_i + 1)

                if s_i == 0 and d_qfm is not None:
                    P.wait("g", ss, qproj_done)
                    P.dma("g", d_qfm, qfm[:].rearrange("p h q -> p (h q)"), so)
                if s_i == 0 and d_cfm is not None:
                    P.wait("g", sv, attn_done)
                    P.dma("g", d_cfm, cfm[:].rearrange("p h q -> p (h q)"), so)

                P.wait("g", ss, qproj_done)
                P.dma("g", wbig[:], woutT.rearrange("(c p) e -> p c e", p=128), sg)
                wo_loaded = sg.n
                P.wait("t", sg, wo_loaded)
                P.wait("t", sv, attn_done)
                for tl in range(4):
                    tt = 4 * s_i + tl
                    x_acquire(tl % 2)
                    P.dma("y", xbuf[:, tl % 2, :], xown[128 * tt:128 * (tt + 1), :], sx)
                    xo_ld = sx.n
                    aoev = {}
                    for dspan in range(4):
                        pa = psA[dspan % 2]
                        if dspan >= 2:
                            P.wait("t", sv, aoev[dspan - 2])
                        for ec in range(NDC):
                            P.op("t", "matmul", pa[:], cfm[:, ec, 128 * tl:128 * (tl + 1)],
                                 wbig[:, ec, 512 * dspan:512 * (dspan + 1)],
                                 start=(ec == 0), stop=(ec == NDC - 1),
                                 inc=(sp if ec == NDC - 1 else None))
                        P.wait("v", sp, sp.n)
                        if dspan == 0:
                            P.wait("v", sx, ydma_prev[0])
                            P.wait("v", so, ydma_prev[1])
                        P.op("v", "scalar_tensor_tensor", ao[:, 512 * dspan:512 * (dspan + 1)],
                             pa[:, 0:512], 1.0, outb_s[:, 512 * dspan:512 * (dspan + 1)],
                             ALU.bypass, ALU.add, inc=sv)
                        aoev[dspan] = sv.n
                    ln_stats(ao[:])
                    for q2 in range(2):
                        P.op("s", "activation", sqjunk[:, 0:1024],
                             ao[:, 1024 * q2:1024 * (q2 + 1)], AF.Identity,
                             bias=stt[:, 6:7], scale=stt[:, 5:6], inc=ss)
                        P.wait("v", ss, ss.n)
                        if q2 == 0:
                            P.wait("v", sx, xo_ld)
                        P.op("v", "tensor_add", ao[:, 1024 * q2:1024 * (q2 + 1)],
                             sqjunk[:, 0:1024], xbuf[:, tl % 2, 1024 * q2:1024 * (q2 + 1)],
                             inc=sv)
                        if q2 == 0:
                            P.wait("s", sv, sv.n)
                    xguard[tl % 2][1] = sv.n
                    y_done = sv.n
                    P.wait("y", sv, y_done)
                    P.dma("y", yscr[128 * tt:128 * (tt + 1), :], ao[:], sx)
                    if d_y is not None:
                        P.dma("y", d_y[128 * tt:128 * (tt + 1), :], ao[:], so)
                    ydma_prev = (sx.n, so.n)
                    ln_stats(ao[:])
                    P.op("s", "activation", ln2tm[:], ao[:], AF.Identity,
                         bias=stt[:, 6:7], scale=stt[:, 5:6], inc=ss)
                    P.wait("t", ss, ss.n)
                    for half in range(2):
                        P.wait("v", sx, l2dma_prev)
                        for i in range(8):
                            blk = 8 * half + i
                            P.op("t", "transpose", psT[:, 128 * i:128 * (i + 1)],
                                 ln2tm[:, 128 * blk:128 * (blk + 1)], id16[:],
                                 inc=(sp if i == 7 else None))
                        P.wait("v", sp, sp.n)
                        P.op("v", "tensor_copy", l2b[:],
                             psT[:, 0:1024].rearrange("p (c e) -> p c e", c=8), inc=sv)
                        P.wait("t", sv, sv.n)
                        P.wait("y", sv, sv.n)
                        P.dma("y", ln2scr[tt, 8 * half:8 * half + 8].rearrange("c p e -> p c e"),
                              l2b[:], sx)
                        l2dma_prev = sx.n
                        ln2rd[0] = sp.n
            P.flush()
        barrier()

        with contextlib.ExitStack() as fst:
            EF = fst.enter_context
            l2f = EF(nc.sbuf_tensor("l2f", [128, NDC, 512], BF16))
            hsp = EF(nc.sbuf_tensor("hsp", [128, NFC, 512], BF16))
            wg1 = EF(nc.sbuf_tensor("wg1", [128, 2, NDC, 512], BF16))
            wg2 = EF(nc.sbuf_tensor("wg2", [128, 2, 2, 1024], BF16))
            ff16 = EF(nc.sbuf_tensor("ff16", [128, NDC, 512], BF16))
            sq16 = EF(nc.sbuf_tensor("sq16", [128, 2, 512], F32))
            rstd16 = EF(nc.sbuf_tensor("rstd16", [128, 512], BF16))
            shiftb = EF(nc.sbuf_tensor("shiftb", [128, 512], F32))
            rows = EF(nc.sbuf_tensor("rows", [1, 8, 512], F32))
            rowsh = EF(nc.sbuf_tensor("rowsh", [1, 2, 512], BF16))
            lnt32 = EF(nc.sbuf_tensor("lnt32", [128, 2, 128], F32))
            ysp = EF(nc.sbuf_tensor("ysp", [128, D], F32))
            otile = EF(nc.sbuf_tensor("otile", [128, 2, D], F32))
            psF = [EF(nc.psum_tensor(f"psF{i}", [128, 512], F32)) for i in range(8)]

            a_used = {}
            b_used = {}
            outdma = {}
            for sp_i in range(NSLOT):
                span_start_sv = sv.n
                P.wait("y", sp, sp.n)
                for t_ in range(4):
                    P.dma("y", l2f[:, :, 128 * t_:128 * (t_ + 1)],
                          ln2scr[4 * sp_i + t_].rearrange("c p e -> p c e"), sx)
                l2f_ld = sx.n

                gelu_done = {}
                for g in range(16):
                    gb = g % 2
                    ka = 16 * sp_i + g
                    if ka >= 2:
                        P.wait("g", sp, a_used[ka - 2])
                    P.dma("g", wg1[:, gb, :, :], w1q[:, :, 512 * g:512 * (g + 1)].rearrange(
                        "c p e -> p c e"), sg)
                    g_ld = sg.n
                    for fi in range(4):
                        fc = 4 * g + fi
                        pf = psF[fc % 2]
                        if fi == 0:
                            P.wait("t", sg, g_ld)
                        if fc == 0:
                            P.wait("t", sx, l2f_ld)
                            P.wait("t", sv, span_start_sv)
                        if fc >= 2:
                            P.wait("t", ss, gelu_done[fc - 2])
                        for dc in range(NDC):
                            P.op("t", "matmul", pf[:],
                                 wg1[:, gb, dc, 128 * fi:128 * (fi + 1)],
                                 l2f[:, dc, :], start=(dc == 0), stop=(dc == NDC - 1),
                                 inc=(sp if dc == NDC - 1 else None))
                        P.wait("s", sp, sp.n)
                        P.op("s", "activation", hsp[:, fc, :], pf[:], AF.Gelu,
                             bias=b1c[:, fc:fc + 1], inc=ss)
                        gelu_done[fc] = ss.n
                    a_used[ka] = sp.n
                a_done = ss.n

                for dh in range(2):
                    b_ld = {}
                    for fc in range(NFC):
                        g2, fi = divmod(fc, 2)
                        kb = 32 * (2 * sp_i + dh) + g2
                        if fi == 0:
                            if kb >= 2:
                                P.wait("g", sp, b_used[kb - 2])
                            P.dma("g", wg2[:, g2 % 2, :, :],
                                  w2q[2 * g2:2 * g2 + 2, :, 1024 * dh:1024 * (dh + 1)].rearrange(
                                      "c p e -> p c e"), sg)
                            b_ld[g2] = sg.n
                            P.wait("t", sg, b_ld[g2])
                        if fc == 0:
                            P.wait("t", ss, a_done)
                            P.wait("t", sv, span_start_sv)
                            if dh == 1:
                                P.wait("t", ss, evac_dh0)
                        for dcl in range(8):
                            P.op("t", "matmul", psF[dcl][:],
                                 wg2[:, g2 % 2, fi, 128 * dcl:128 * (dcl + 1)],
                                 hsp[:, fc, :], start=(fc == 0), stop=(fc == NFC - 1),
                                 inc=(sp if dcl == 7 else None))
                        if fi == 1:
                            b_used[kb] = sp.n
                    bmm_done = sp.n
                    P.wait("s", sp, bmm_done)
                    for dcl in range(8):
                        dc = 8 * dh + dcl
                        P.op("s", "activation", ff16[:, dc, :], psF[dcl][:], AF.Identity,
                             bias=b2c[:, dc:dc + 1], inc=ss)
                    if dh == 0:
                        evac_dh0 = ss.n
                b_done = ss.n

                if sp_i == 0 and d_ffn is not None:
                    P.wait("g", ss, b_done)
                    P.dma("g", d_ffn, ff16[:].rearrange("p c e -> p (c e)"), so)

                P.wait("t", ss, b_done)
                for dc in range(NDC):
                    P.op("t", "matmul", psF[0][0:1, :], onescolb[:, 0:1], ff16[:, dc, :],
                         start=(dc == 0), stop=(dc == NDC - 1),
                         inc=(sp if dc == NDC - 1 else None))
                sqmm = {}
                for dc in range(NDC):
                    if dc >= 2:
                        P.wait("s", sp, sqmm[dc - 2])
                    P.op("s", "activation", sq16[:, dc % 2, :], ff16[:, dc, :],
                         AF.Square, inc=ss)
                    P.wait("t", ss, ss.n)
                    P.op("t", "matmul", psF[1][0:1, :], onescol[:, 0:1], sq16[:, dc % 2, :],
                         start=(dc == 0), stop=(dc == NDC - 1), inc=sp)
                    sqmm[dc] = sp.n
                sumsq_done = sp.n
                P.wait("v", sp, sumsq_done)
                P.op("v", "tensor_scalar", rows[0:1, 0, :], psF[0][0:1, :], 1.0 / D, None,
                     ALU.mult, inc=sv)
                P.op("v", "tensor_scalar", rows[0:1, 1, :], psF[1][0:1, :], 1.0 / D, None,
                     ALU.mult, inc=sv)
                P.wait("v", sv, sv.n)
                P.op("v", "tensor_mul", rows[0:1, 2, :], rows[0:1, 0, :], rows[0:1, 0, :], inc=sv)
                P.wait("v", sv, sv.n)
                P.op("v", "tensor_sub", rows[0:1, 1, :], rows[0:1, 1, :], rows[0:1, 2, :], inc=sv)
                P.wait("v", sv, sv.n)
                P.wait("s", sv, sv.n)
                P.op("s", "activation", rows[0:1, 3, :], rows[0:1, 1, :], AF.Ln,
                     bias=epst[0:1, 0:1], inc=ss)
                P.op("s", "activation", rows[0:1, 4, :], rows[0:1, 3, :], AF.Exp,
                     scale=-0.5, inc=ss)
                P.wait("v", ss, ss.n)
                P.op("v", "scalar_tensor_tensor", rows[0:1, 5, :], rows[0:1, 0, :], -1.0,
                     rows[0:1, 4, :], ALU.mult, ALU.mult, inc=sv)
                P.wait("v", sv, sv.n)
                P.op("v", "tensor_copy", rowsh[0:1, 0, :], rows[0:1, 4, :], inc=sv)
                P.op("v", "tensor_copy", rowsh[0:1, 1, :], rows[0:1, 5, :], inc=sv)
                P.wait("t", sv, sv.n)
                P.op("t", "matmul", psF[2][:], onesrowh[:], rowsh[0:1, 0, :],
                     start=True, stop=True, inc=sp)
                P.op("t", "matmul", psF[3][:], onesrowh[:], rowsh[0:1, 1, :],
                     start=True, stop=True, inc=sp)
                P.wait("v", sp, sp.n)
                P.op("v", "tensor_copy", rstd16[:], psF[2][:], inc=sv)
                P.op("v", "tensor_copy", shiftb[:], psF[3][:], inc=sv)
                bcast_done = sv.n
                P.wait("t", sv, bcast_done)
                if sp_i == 0 and d_fst is not None:
                    P.wait("g", sv, bcast_done)
                    P.dma("g", d_fst, rows[0:1, :, :].rearrange("p a b -> p (a b)"), so)
                    P.dma("g", d_frs, rstd16[:], so)
                    P.dma("g", d_fsh, shiftb[:], so)

                for tl in range(4):
                    tt = 4 * sp_i + tl
                    P.wait("y", sv, sv.n)
                    P.dma("y", ysp[:], yscr[128 * tt:128 * (tt + 1), :], sx)
                    y_ld = sx.n
                    P.wait("v", sx, y_ld)
                    if tt >= 2:
                        P.wait("v", so, outdma[tt - 2])
                    tr_done = {}
                    for dc in range(NDC):
                        lb = dc % 2
                        if dc >= 2:
                            P.wait("v", sp, tr_done[dc - 2])
                        P.op("v", "tensor_mul", lnt32[:, lb, :],
                             ff16[:, dc, 128 * tl:128 * (tl + 1)],
                             rstd16[:, 128 * tl:128 * (tl + 1)], inc=sv)
                        P.wait("v", sv, sv.n)
                        P.op("v", "tensor_add", lnt32[:, lb, :], lnt32[:, lb, :],
                             shiftb[:, 128 * tl:128 * (tl + 1)], inc=sv)
                        P.wait("t", sv, sv.n)
                        bank = 4 + (dc % 4)
                        P.op("t", "transpose", psF[bank][:, 0:128], lnt32[:, lb, :],
                             id32[:], inc=sp)
                        tr_done[dc] = sp.n
                        P.wait("v", sp, tr_done[dc])
                        P.op("v", "tensor_add", otile[:, tl % 2, 128 * dc:128 * (dc + 1)],
                             psF[bank][:, 0:128], ysp[:, 128 * dc:128 * (dc + 1)],
                             inc=sv)
                    P.wait("y", sv, sv.n)
                    if tt == 0 and d_fot is not None:
                        P.dma("g", d_fot, otile[:, 0, :], so)
                    P.dma("y", out_t[128 * tt:128 * (tt + 1), :], otile[:, tl % 2, :], so)
                    outdma[tt] = so.n
            P.wait("y", so, so.n)
            P.flush()

    print(f"[mk] total instructions: {P.ninstr}")
    return nc



def make_inputs(L0_attn_x, L0_attn_Wqkv_w, L0_attn_Wqkv_b, L0_attn_out_w,
                L0_attn_out_b, L0_ffn_w1, L0_ffn_b1, L0_ffn_w2, L0_ffn_b2):
    f = np.float32
    wqkvT = np.ascontiguousarray(np.asarray(L0_attn_Wqkv_w, f).T)
    woutT = np.ascontiguousarray(np.asarray(L0_attn_out_w, f).T)
    w1T = np.ascontiguousarray(np.asarray(L0_ffn_w1, f).T)
    w2T = np.ascontiguousarray(np.asarray(L0_ffn_w2, f).T)
    qkvb = np.ascontiguousarray(np.asarray(L0_attn_Wqkv_b, f).reshape(18, 128).T)
    kb = np.ascontiguousarray(np.broadcast_to(np.asarray(L0_attn_Wqkv_b[2048:2176], f), (128, 128)))
    vb = np.ascontiguousarray(np.broadcast_to(np.asarray(L0_attn_Wqkv_b[2176:2304], f), (128, 128)))
    outb = np.ascontiguousarray(np.broadcast_to(
        np.asarray(L0_attn_out_b, f).astype(ml_dtypes.bfloat16), (128, D)))
    b1c = np.ascontiguousarray(np.asarray(L0_ffn_b1, f).reshape(NFC, 128).T)
    b2c = np.ascontiguousarray(np.asarray(L0_ffn_b2, f).reshape(NDC, 128).T)
    id16 = np.eye(128, dtype=ml_dtypes.bfloat16)
    idh = np.eye(128, dtype=np.float16)
    id32 = np.eye(128, dtype=f)
    onescol = np.ones((128, 4), f)
    onescolb = np.ones((128, 4), ml_dtypes.bfloat16)
    onescolh = np.ones((128, 4), np.float16)
    onesrow = np.ones((1, 128), f)
    onesrowh = np.ones((1, 128), ml_dtypes.bfloat16)
    iov = np.concatenate([np.arange(512, dtype=np.float16),
                          np.arange(512, dtype=np.float16) - np.float16(128.0)])
    iota = np.ascontiguousarray(np.broadcast_to(iov, (128, 1024)))

    x = np.asarray(L0_attn_x, f)
    in_maps = []
    layout = []
    for c in range(8):
        b, j = divmod(c, 4)
        plist = p_list_for(j)
        xo = np.ascontiguousarray(
            np.concatenate([x[b, 512 * p:512 * (p + 1), :] for p in plist], axis=0))
        cth = np.zeros((128, 64), np.float32)
        pvec = np.arange(128, dtype=np.float32)
        for s_i in range(4):
            Pm = plist[s_i]
            for cc in range(16 * s_i, 16 * (s_i + 1)):
                t = 128.0 * cc + pvec - 512.0 * Pm
                cth[:, cc] = np.clip(t, -640.0, 520.0)
        cth = cth.astype(np.float32)
        in_maps.append({
            "xseq": np.ascontiguousarray(x[b]),
            "xown": xo,
            "wqkvT": wqkvT, "woutT": woutT, "w1T": w1T, "w2T": w2T,
            "qkvb": qkvb, "kb": kb, "vb": vb, "outb": outb,
            "b1c": b1c, "b2c": b2c,
            "id16": id16, "idh": idh, "id32": id32,
            "onescol": onescol, "onescolb": onescolb, "onescolh": onescolh,
            "onesrow": onesrow, "onesrowh": onesrowh,
            "iota": iota, "cthr": cth,
        })
        layout.append((b, plist))
    return in_maps, layout


def kernel(L0_attn_x, L0_attn_Wqkv_w, L0_attn_Wqkv_b, L0_attn_out_w, L0_attn_out_b,
           L0_ffn_w1, L0_ffn_b1, L0_ffn_w2, L0_ffn_b2, debug=()):
    in_maps, layout = make_inputs(L0_attn_x, L0_attn_Wqkv_w, L0_attn_Wqkv_b,
                                  L0_attn_out_w, L0_attn_out_b, L0_ffn_w1,
                                  L0_ffn_b1, L0_ffn_w2, L0_ffn_b2)
    nc = build(debug=debug)
    res = run_bass_kernel_spmd(nc, in_maps, core_ids=list(range(8)))
    out = np.empty((2, S, D), np.float32)
    for c in range(8):
        b, plist = layout[c]
        o = res.results[c]["out"]
        for si, p in enumerate(plist):
            out[b, 512 * p:512 * (p + 1), :] = o[512 * si:512 * (si + 1), :]
    if debug:
        return out, res
    return out


# version 10
# speedup vs baseline: 1.1499x; 1.1094x over previous
import contextlib
import math
import numpy as np
import ml_dtypes
import concourse.bass as bass
import concourse.mybir as mybir
from concourse.bass_utils import run_bass_kernel_spmd

F32 = mybir.dt.float32
BF16 = mybir.dt.bfloat16
FP16 = mybir.dt.float16
AF = mybir.ActivationFunctionType
ALU = mybir.AluOpType
AX = mybir.AxisListType.X

D = 2048
S = 8192
HEADS = 16
DH = 128
DFF = 8192
NDC = D // 128
NFC = DFF // 128
NKC = S // 128
NT = 16
NSLOT = 4
SCALE = 1.0 / math.sqrt(DH)
EPS = 1e-5
CM_CONST = -0.5 * math.log(128.0) + math.log(1.05)

ENGINES = ("t", "v", "s", "y", "g")


class Sem:
    def __init__(self, stack, nc, name):
        self.h = stack.enter_context(nc.semaphore(name))
        self.n = 0

    def bump(self, k=1):
        self.n += k
        return self.n


class Prog:

    def __init__(self, nc):
        self.nc = nc
        self.ops = []
        self.ninstr = 0

    def op(self, eng, name, *args, inc=None, k=1, **kw):
        self.ops.append((eng, name, args, kw, inc, k))
        self.ninstr += 1
        if inc is not None:
            return inc.bump(k)
        return None

    def wait(self, eng, sem, n):
        if n > 0:
            self.ops.append((eng, "wait_ge", (sem.h, n), {}, None, 0))
            self.ninstr += 1

    def dma(self, eng, out, in_, inc, **kw):
        return self.op(eng, "dma_start", inc=inc, k=16, out=out, in_=in_, **kw)

    def flush(self):
        streams = {e: [] for e in ENGINES}
        for o in self.ops:
            streams[o[0]].append(o)
        self.ops = []

        def runner(engname):
            def run(e):
                for (_, name, args, kw, inc, k) in streams[engname]:
                    ins = getattr(e, name)(*args, **kw)
                    if inc is not None:
                        ins.then_inc(inc.h, k)
            return run

        with self.nc.Block() as block:
            for attr, key in (("tensor", "t"), ("vector", "v"), ("scalar", "s"),
                              ("sync", "y"), ("gpsimd", "g")):
                if streams[key]:
                    getattr(block, attr)(runner(key))


def p_list_for(j):
    return sorted({j, 7 - j, 8 + j, 15 - j})


def build(debug=()):
    nc = bass.Bass("TRN2", target_bir_lowering=False)
    dbg = set(debug)

    xseq = nc.dram_tensor("xseq", [S, D], F32, kind="ExternalInput").ap()
    xown = nc.dram_tensor("xown", [NT * 128, D], F32, kind="ExternalInput").ap()
    wqkvT = nc.dram_tensor("wqkvT", [D, 2304], F32, kind="ExternalInput").ap()
    woutT = nc.dram_tensor("woutT", [D, D], F32, kind="ExternalInput").ap()
    w1T = nc.dram_tensor("w1T", [D, DFF], F32, kind="ExternalInput").ap()
    w2T = nc.dram_tensor("w2T", [DFF, D], F32, kind="ExternalInput").ap()
    qkvb_i = nc.dram_tensor("qkvb", [128, 18], F32, kind="ExternalInput").ap()
    kb_i = nc.dram_tensor("kb", [128, 128], F32, kind="ExternalInput").ap()
    vb_i = nc.dram_tensor("vb", [128, 128], F32, kind="ExternalInput").ap()
    outb_i = nc.dram_tensor("outb", [128, D], BF16, kind="ExternalInput").ap()
    b1c_i = nc.dram_tensor("b1c", [128, NFC], F32, kind="ExternalInput").ap()
    b2row_i = nc.dram_tensor("b2row", [128, D], F32, kind="ExternalInput").ap()
    id16_i = nc.dram_tensor("id16", [128, 128], BF16, kind="ExternalInput").ap()
    idh_i = nc.dram_tensor("idh", [128, 128], FP16, kind="ExternalInput").ap()
    id32_i = nc.dram_tensor("id32", [128, 128], F32, kind="ExternalInput").ap()
    onescol_i = nc.dram_tensor("onescol", [128, 4], F32, kind="ExternalInput").ap()
    onescolb_i = nc.dram_tensor("onescolb", [128, 4], BF16, kind="ExternalInput").ap()
    onescolh_i = nc.dram_tensor("onescolh", [128, 4], FP16, kind="ExternalInput").ap()
    onesrow_i = nc.dram_tensor("onesrow", [1, 128], F32, kind="ExternalInput").ap()
    onesrowh_i = nc.dram_tensor("onesrowh", [1, 128], BF16, kind="ExternalInput").ap()
    iota_i = nc.dram_tensor("iota", [128, 1024], FP16, kind="ExternalInput").ap()
    cthr_i = nc.dram_tensor("cthr", [128, 64], F32, kind="ExternalInput").ap()

    yscr = nc.dram_tensor("yscr", [NT * 128, D], F32, kind="Internal").ap()
    ln2scr = nc.dram_tensor("ln2scr", [NT, NDC, 128, 128], BF16, kind="Internal").ap()
    w1q = nc.dram_tensor("w1q", [NDC, 128, DFF], BF16, kind="Internal").ap()
    w2q = nc.dram_tensor("w2q", [NFC, 128, D], BF16, kind="Internal").ap()

    out_t = nc.dram_tensor("out", [NT * 128, D], F32, kind="ExternalOutput").ap()
    dbg_t = {}

    def dbg_tensor(name, shape, dt=F32):
        if name in dbg:
            dbg_t[name] = nc.dram_tensor("dbg_" + name, shape, dt, kind="ExternalOutput").ap()
        return dbg_t.get(name)

    d_kfm = dbg_tensor("kfm", [128, NKC * 128])
    d_v = dbg_tensor("vst", [128, NKC * 128])
    d_qfm = dbg_tensor("qfm", [128, HEADS * 512])
    d_cfm = dbg_tensor("cfm", [128, HEADS * 512])
    d_wq1 = dbg_tensor("wq1", [128, DFF])
    d_thr = dbg_tensor("thr", [128, 8])
    d_ln1 = dbg_tensor("ln1t0", [128, NDC * 128])
    d_st = dbg_tensor("stat", [128, 16])
    d_y = dbg_tensor("y", [NT * 128, D])

    with contextlib.ExitStack() as gst:
        E = gst.enter_context
        sx = Sem(gst, nc, "sx")
        sg = Sem(gst, nc, "sg")
        sp = Sem(gst, nc, "sp")
        sv = Sem(gst, nc, "sv")
        ss = Sem(gst, nc, "ss")
        so = Sem(gst, nc, "so")
        sq = Sem(gst, nc, "sq")
        ALLSEMS = (sx, sg, sp, sv, ss, so, sq)

        id16 = E(nc.sbuf_tensor("id16s", [128, 128], BF16))
        idh = E(nc.sbuf_tensor("idhs", [128, 128], FP16))
        id32 = E(nc.sbuf_tensor("id32s", [128, 128], F32))
        onescol = E(nc.sbuf_tensor("onescols", [128, 4], F32))
        onescolb = E(nc.sbuf_tensor("onescolbs", [128, 4], BF16))
        onescolh = E(nc.sbuf_tensor("onescolhs", [128, 4], FP16))
        onesrow = E(nc.sbuf_tensor("onesrows", [1, 128], F32))
        onesrowh = E(nc.sbuf_tensor("onesrowhs", [1, 128], BF16))
        iota = E(nc.sbuf_tensor("iotas", [128, 1024], FP16))
        cthr = E(nc.sbuf_tensor("cthrs", [128, 64], F32))
        qkvb = E(nc.sbuf_tensor("qkvbs", [128, 18], F32))
        kb_s = E(nc.sbuf_tensor("kbs", [128, 128], F32))
        vb_s = E(nc.sbuf_tensor("vbs", [128, 128], F32))
        outb_s = E(nc.sbuf_tensor("outbs", [128, D], BF16))
        b1c = E(nc.sbuf_tensor("b1cs", [128, NFC], F32))
        cmt = E(nc.sbuf_tensor("cmt", [128, 1], F32))
        epst = E(nc.sbuf_tensor("epst", [128, 1], F32))
        negMb = E(nc.sbuf_tensor("negMb", [128, 1], F32))
        stt = E(nc.sbuf_tensor("stt", [128, 16], F32))
        kmax2 = E(nc.sbuf_tensor("kmax2", [1, 8], F32))
        qmax2 = E(nc.sbuf_tensor("qmax2", [1, 4], F32))
        sqjunk = E(nc.sbuf_tensor("sqjunk", [128, 1024], F32))
        thr1 = E(nc.sbuf_tensor("thr1", [128, 4], F32))
        absacc = E(nc.sbuf_tensor("absacc", [128, 2], F32))
        mtmp = E(nc.sbuf_tensor("mtmp", [128, 8], F32))

        P = Prog(nc)

        def barrier():
            for e in ENGINES:
                for sem in ALLSEMS:
                    P.wait(e, sem, sem.n)

        for dst, src in ((id16, id16_i), (idh, idh_i), (id32, id32_i),
                         (onescol, onescol_i), (onescolb, onescolb_i),
                         (onescolh, onescolh_i), (onesrow, onesrow_i),
                         (onesrowh, onesrowh_i), (iota, iota_i), (cthr, cthr_i),
                         (qkvb, qkvb_i), (kb_s, kb_i), (vb_s, vb_i),
                         (outb_s, outb_i), (b1c, b1c_i)):
            P.dma("y", dst[:], src, sx)
        P.op("v", "memset", cmt[:], CM_CONST, inc=sv)
        P.op("v", "memset", epst[:], EPS, inc=sv)
        consts_done = sx.n


        with contextlib.ExitStack() as ast_:
            EA = ast_.enter_context
            kfm = EA(nc.sbuf_tensor("kfm", [128, NKC, 128], BF16))
            vst = EA(nc.sbuf_tensor("vst", [128, NKC, 128], BF16))
            qfm = EA(nc.sbuf_tensor("qfm", [128, HEADS, 512], BF16))
            cfm = EA(nc.sbuf_tensor("cfm", [128, HEADS, 512], BF16))
            wbig = EA(nc.sbuf_tensor("wbig", [128, NDC, D], BF16))
            xbuf = EA(nc.sbuf_tensor("xbuf", [128, 2, D], F32))
            ln1own = EA(nc.sbuf_tensor("ln1own", [128, NDC, 512], BF16))
            pt = EA(nc.sbuf_tensor("pt", [128, 2, 1024], BF16))
            maskb = EA(nc.sbuf_tensor("maskb", [128, 1024], BF16))
            ktmp = EA(nc.sbuf_tensor("ktmp", [128, 128], BF16))
            rlrow = EA(nc.sbuf_tensor("rlrow", [1, 2, 512], F32))
            rl16 = EA(nc.sbuf_tensor("rl16", [1, 512], BF16))
            rlb = EA(nc.sbuf_tensor("rlb", [128, 512], F32))
            qsq = None
            ao = EA(nc.sbuf_tensor("ao", [128, D], F32))
            ln2tm = EA(nc.sbuf_tensor("ln2tm", [128, D], BF16))
            qsq = pt[:, 0, 0:512]
            ln1fm = ln1own[:, 8:12, :].rearrange("p c (a e) -> p (c a) e", a=4)
            l2b = maskb[:].rearrange("p (c e) -> p c e", c=8)
            psAB = [EA(nc.psum_tensor(f"psAB{i}", [128, 1024], F32)) for i in range(2)]
            psA = [psAB[0][:, 0:512], psAB[1][:, 0:512]]
            psCc = EA(nc.psum_tensor("psCc", [128, 512], F32))
            psL = EA(nc.psum_tensor("psL", [128, 512], F32))
            psS = EA(nc.psum_tensor("psS", [128, 512], F32))
            psT = EA(nc.psum_tensor("psT", [128, 1024], BF16))

            wkv_t = EA(nc.sbuf_tensor("wkv", [128, NDC, 256], BF16))
            wkv = wkv_t[:]
            P.dma("g", wkv, wqkvT[:, 2048:2304].rearrange("(c p) e -> p c e", p=128), sg)
            wkv_done = sg.n

            def C(par, i):
                return stt[:, 8 * par + i:8 * par + i + 1]

            def ln_stats1(src, par):
                P.op("v", "tensor_reduce", C(par, 0), src, AX, ALU.add, inc=sv)
                P.wait("s", sv, sv.n)
                P.op("s", "activation", sqjunk[:], src[:, 0:1024], AF.Square,
                     accum_out=C(par, 2), inc=ss)
                P.op("s", "activation", sqjunk[:], src[:, 1024:2048], AF.Square,
                     accum_out=C(par, 1), inc=ss)
                return ss.n

            def ln_stats2(par, s1done):
                P.wait("v", ss, s1done)
                P.op("v", "tensor_add", C(par, 1), C(par, 1), C(par, 2), inc=sv)
                P.op("v", "tensor_scalar", C(par, 3), C(par, 0), 1.0 / D, None, ALU.mult, inc=sv)
                P.wait("v", sv, sv.n)
                P.op("v", "tensor_mul", C(par, 4), C(par, 3), C(par, 3), inc=sv)
                P.op("v", "tensor_scalar", C(par, 2), C(par, 1), 1.0 / D, None, ALU.mult, inc=sv)
                P.wait("v", sv, sv.n)
                P.op("v", "tensor_sub", C(par, 4), C(par, 2), C(par, 4), inc=sv)
                P.wait("v", sv, sv.n)
                P.wait("s", sv, sv.n)
                P.op("s", "activation", C(par, 7), C(par, 4), AF.Ln, bias=epst[:, 0:1], inc=ss)
                P.op("s", "activation", C(par, 5), C(par, 7), AF.Exp, scale=-0.5, inc=ss)
                P.wait("v", ss, ss.n)
                P.op("v", "scalar_tensor_tensor", C(par, 6), C(par, 3), -1.0,
                     C(par, 5), ALU.mult, ALU.mult, inc=sv)
                P.wait("s", sv, sv.n)

            def ln_stats(src, par=0):
                d1 = ln_stats1(src, par)
                ln_stats2(par, d1)

            def norm_cast_transpose(src, dst_chunked, par=0):
                stg = (pt[:, 0, 0:512], pt[:, 0, 512:1024], pt[:, 1, 0:512], pt[:, 1, 512:1024])
                P.wait("s", sp, sp.n)
                for q4 in range(4):
                    P.op("s", "activation", stg[q4], src[:, 512 * q4:512 * (q4 + 1)],
                         AF.Identity, bias=C(par, 6), scale=C(par, 5), inc=ss)
                norm_done = ss.n
                P.wait("t", ss, norm_done)
                for half in range(2):
                    for i in range(8):
                        blk = 8 * half + i
                        q4, r = divmod(blk, 4)
                        P.op("t", "transpose", psT[:, 128 * i:128 * (i + 1)],
                             stg[q4][:, 128 * r:128 * (r + 1)], id16[:],
                             inc=(sp if i == 7 else None))
                    P.wait("v", sp, sp.n)
                    P.op("v", "tensor_copy",
                         dst_chunked[:, 8 * half:8 * half + 8, :],
                         psT[:, 0:1024].rearrange("p (c e) -> p c e", c=8), inc=sv)
                    P.wait("t", sv, sv.n)
                return norm_done

            wfa2 = cfm[:].bitcast(F32).rearrange("p c e -> p (c e)").rearrange(
                "p (s e) -> p s e", s=2)
            wfa3 = ao[:]
            l1B = ln1own[:, 0:8, :].rearrange("p c e -> p (c e)").rearrange(
                "p (s e) -> p s e", s=2)
            wqoB = l1B
            leb = ln2tm[:]

            def wslot(u):
                s_ = u % 3
                return wfa3 if s_ == 2 else wfa2[:, s_, :]

            def wsub_in(u):
                if u < 64:
                    i, s_ = divmod(u, 4)
                    return w1T[128 * i:128 * (i + 1), 2048 * s_:2048 * (s_ + 1)]
                return w2T[128 * (u - 64):128 * (u - 63), :]

            def wsub_out(u):
                if u < 64:
                    i, s_ = divmod(u, 4)
                    return w1q[i, :, 2048 * s_:2048 * (s_ + 1)]
                return w2q[u - 64, :, :]

            qload = {}
            quse = {}
            qdma = {}

            quse_s = {}

            def quant_load(L):
                u = L if L < 128 else L - 128
                if L >= 3:
                    P.wait("y", sv, quse.get(L - 3, 0))
                    P.wait("y", ss, quse_s.get(L - 3, 0))
                P.dma("y", wslot(L), wsub_in(u), sx)
                qload[L] = sx.n

            mean_vdone = {}

            def quant_mean_sub(u):
                p2 = (u % 2) * 2
                P.wait("s", sx, qload[u])
                if u >= 2:
                    P.wait("s", sv, mean_vdone[u - 2])
                P.op("s", "activation", sqjunk[:], wslot(u)[:, 0:1024], AF.Abs,
                     accum_out=mtmp[:, p2:p2 + 1], inc=ss)
                P.op("s", "activation", sqjunk[:], wslot(u)[:, 1024:2048], AF.Abs,
                     accum_out=mtmp[:, p2 + 1:p2 + 2], inc=ss)
                quse_s[u] = ss.n
                quse[u] = 0
                tgt = absacc[:, 0:1] if u < 64 else absacc[:, 1:2]
                P.wait("v", ss, ss.n)
                P.op("v", "tensor_add", mtmp[:, 4 + p2 // 2:5 + p2 // 2],
                     mtmp[:, p2:p2 + 1], mtmp[:, p2 + 1:p2 + 2], inc=sv)
                P.wait("v", sv, sv.n)
                if u == 0 or u == 64:
                    P.op("v", "tensor_copy", tgt, mtmp[:, 4 + p2 // 2:5 + p2 // 2], inc=sv)
                else:
                    P.op("v", "tensor_add", tgt, tgt, mtmp[:, 4 + p2 // 2:5 + p2 // 2], inc=sv)
                P.wait("v", sv, sv.n)
                mean_vdone[u] = sv.n

            def quant_thr():
                P.wait("t", sv, sv.n)
                P.op("t", "matmul", psL[0:1, 0:2], onescol[:, 0:1], absacc[:, 0:2],
                     start=True, stop=True, inc=sp)
                P.wait("v", sp, sp.n)
                P.op("v", "tensor_scalar", mtmp[0:1, 0:2], psL[0:1, 0:2],
                     0.7 / float(DFF * D), None, ALU.mult, inc=sv)
                P.wait("t", sv, sv.n)
                P.op("t", "matmul", psS[:, 0:2], onesrow[:], mtmp[0:1, 0:2],
                     start=True, stop=True, inc=sp)
                P.wait("v", sp, sp.n)
                P.op("v", "tensor_copy", thr1[:, 0:2], psS[:, 0:2], inc=sv)
                P.wait("v", sv, sv.n)
                P.op("v", "tensor_scalar", thr1[:, 2:4], thr1[:, 0:2], -1.0, None,
                     ALU.mult, inc=sv)
                P.wait("v", sv, sv.n)
                if d_thr is not None:
                    P.wait("g", sv, sv.n)
                    P.dma("g", d_thr[:, 0:4], thr1[:], so)
                    P.dma("g", d_thr[:, 4:6], absacc[:], so)

            def quant_q_sub(u):
                L = 128 + u
                tcol = 0 if u < 64 else 1
                P.wait("v", sx, qload[L])
                P.op("v", "tensor_scalar", leb, wslot(L),
                     thr1[:, 2 + tcol:3 + tcol], None, ALU.is_le, inc=sv)
                if u >= 2:
                    P.wait("v", sg, qdma[u - 2])
                P.wait("v", sv, sv.n)
                P.op("v", "scalar_tensor_tensor", wqoB[:, u % 2, :], wslot(L),
                     thr1[:, tcol:tcol + 1], leb, ALU.is_ge, ALU.subtract, inc=sv)
                quse[L] = sv.n
                quse_s[L] = 0
                P.wait("g", sv, quse[L])
                P.dma("g", wsub_out(u), wqoB[:, u % 2, :], sg)
                qdma[u] = sg.n

            P.wait("t", sg, wkv_done)
            P.wait("t", sx, consts_done)
            tile_done = [None] * NKC
            s1d = [None] * NKC

            xld = [None] * (NKC + 1)
            xguard = {0: [0, 0], 1: [0, 0]}

            def x_acquire(buf):
                P.wait("y", ss, xguard[buf][0])
                P.wait("y", sv, xguard[buf][1])

            def x_load(k):
                if k >= NKC:
                    return
                x_acquire(k % 2)
                P.dma("y", xbuf[:, k % 2, :], xseq[128 * k:128 * (k + 1), :], sx)
                xld[k] = sx.n

            def kv_part1(k):
                buf = k % 2
                P.wait("v", sx, xld[k])
                P.wait("s", sx, xld[k])
                s1d[k] = ln_stats1(xbuf[:, buf, :], k % 2)

            def kv_evac(k, pa):
                P.wait("v", sp, sp.n)
                P.op("v", "scalar_tensor_tensor", ktmp[:], pa[:, 0:128], 1.0, kb_s[:],
                     ALU.bypass, ALU.add, inc=sv)
                P.op("v", "scalar_tensor_tensor", vst[:, k, 0:128], pa[:, 128:256], 1.0,
                     vb_s[:], ALU.bypass, ALU.add, inc=sv)
                P.wait("t", sv, sv.n)
                P.op("t", "transpose", psT[:, 0:128], ktmp[:], id16[:], inc=sp)
                P.wait("v", sp, sp.n)
                P.op("v", "tensor_copy", kfm[:, k, :], psT[:, 0:128], inc=sv)
                P.wait("t", sv, sv.n)

            def kv_part2(k):
                buf = k % 2
                par = k % 2
                ln_stats2(par, s1d[k])
                tile_done[k] = norm_cast_transpose(xbuf[:, buf, :], ln1fm, par)
                xguard[buf][0] = tile_done[k]
                pa = psA[k % 2][:, 0:256]
                for dc in range(NDC):
                    P.op("t", "matmul", pa, ln1fm[:, dc, :], wkv[:, dc, :],
                         start=(dc == 0), stop=(dc == NDC - 1),
                         inc=(sp if dc == NDC - 1 else None))
                kv_evac(k, pa)

            psT32 = psT[:].bitcast(F32)

            ln2rd = [0]

            def kv_stage(k):
                par = k % 2
                ln_stats2(par, s1d[k])
                P.wait("s", sp, ln2rd[0])
                for q4 in range(4):
                    P.op("s", "activation", ln2tm[:, 512 * q4:512 * (q4 + 1)],
                         xbuf[:, k % 2, 512 * q4:512 * (q4 + 1)], AF.Identity,
                         bias=C(par, 6), scale=C(par, 5), inc=ss)
                tile_done[k] = ss.n
                xguard[k % 2][0] = ss.n

            def kv_finish(k):
                P.wait("t", ss, tile_done[k])
                for half in range(2):
                    for i in range(8):
                        blk = 8 * half + i
                        P.op("t", "transpose", psT[:, 128 * i:128 * (i + 1)],
                             ln2tm[:, 128 * blk:128 * (blk + 1)], id16[:],
                             inc=(sp if i == 7 else None))
                    P.wait("v", sp, sp.n)
                    P.op("v", "tensor_copy", ln1fm[:, 8 * half:8 * half + 8, :],
                         psT[:, 0:1024].rearrange("p (c e) -> p c e", c=8), inc=sv)
                    P.wait("t", sv, sv.n)
                ln2rd[0] = sp.n
                pa = psT32[:, 0:256]
                for dc in range(NDC):
                    P.op("t", "matmul", pa, ln1fm[:, dc, :], wkv[:, dc, :],
                         start=(dc == 0), stop=(dc == NDC - 1),
                         inc=(sp if dc == NDC - 1 else None))
                kv_evac(k, pa)

            def knorm_group(g):
                for half in range(2):
                    c0 = 16 * g + 8 * half
                    P.wait("s", sp, sp.n)
                    P.wait("s", sv, sv.n)
                    P.op("s", "activation", sqjunk[:, 0:512],
                         kfm[:, c0:c0 + 4, :].rearrange("p c e -> p (c e)"),
                         AF.Square, inc=ss)
                    P.op("s", "activation", sqjunk[:, 512:1024],
                         kfm[:, c0 + 4:c0 + 8, :].rearrange("p c e -> p (c e)"),
                         AF.Square, inc=ss)
                    P.wait("t", ss, ss.n)
                    P.op("t", "matmul", psS[0:1, :], onescol[:, 0:1], sqjunk[:, 0:512],
                         start=True, stop=False, skip_group_check=True)
                    P.op("t", "matmul", psS[0:1, :], onescol[:, 0:1], sqjunk[:, 512:1024],
                         start=False, stop=True, skip_group_check=True, inc=sp)
                    P.wait("v", sp, sp.n)
                    P.op("v", "tensor_reduce", kmax2[0:1, 1:2], psS[0:1, :], AX,
                         ALU.max, inc=sv)
                    P.wait("v", sv, sv.n)
                    if half == 0:
                        P.op("v", "tensor_copy", kmax2[0:1, 4 + g:5 + g],
                             kmax2[0:1, 1:2], inc=sv)
                    else:
                        P.op("v", "tensor_max", kmax2[0:1, 4 + g:5 + g],
                             kmax2[0:1, 4 + g:5 + g], kmax2[0:1, 1:2], inc=sv)
                    P.wait("v", sv, sv.n)
                    P.wait("t", sv, sv.n)

            x_load(0)
            quant_load(0)
            quant_load(1)
            next_load = [2]

            def quant_step(Lc):
                if next_load[0] < 256:
                    quant_load(next_load[0])
                    next_load[0] += 1
                if Lc < 128:
                    quant_mean_sub(Lc)
                else:
                    quant_q_sub(Lc - 128)

            for k in range(NKC + 1):
                if k < NKC:
                    kv_part1(k)
                if k == 33:
                    quant_thr()
                if k >= 1:
                    kv_part2(k - 1)
                if k < NKC:
                    x_load(k + 1)
                if k >= 1:
                    for Lc in range(4 * (k - 1), 4 * k):
                        quant_step(Lc)

            for e in ENGINES:
                P.wait(e, sx, sx.n)
                P.wait(e, sg, sg.n)
                P.wait(e, sv, sv.n)
                P.wait(e, ss, ss.n)
            for g_ in range(4):
                knorm_group(g_)
            kmax_done = sv.n

            ydma_prev = (0, 0)
            l2dma_prev = 0
            for s_i in range(NSLOT):
                Es = 16 * (s_i + 1)
                P.wait("v", sx, sx.n)
                P.wait("g", sp, sp.n)
                P.dma("g", wbig[:], wqkvT[:, 0:2048].rearrange("(c p) e -> p c e", p=128), sg)
                wq_loaded = sg.n

                for tl in range(4):
                    tt = 4 * s_i + tl
                    buf = tl % 2
                    x_acquire(buf)
                    P.dma("y", xbuf[:, buf, :], xown[128 * tt:128 * (tt + 1), :], sx)
                    P.wait("v", sx, sx.n)
                    P.wait("s", sx, sx.n)
                    ln_stats(xbuf[:, buf, :], tl % 2)
                    norm_cast_transpose(
                        xbuf[:, buf, :],
                        ln1own[:, :, 128 * tl:128 * (tl + 1)], tl % 2)
                    xguard[buf][0] = ss.n
                P.wait("t", sg, wq_loaded)
                qev = {}
                for h in range(HEADS):
                    pa = psA[h % 2]
                    if h >= 2:
                        P.wait("t", ss, qev[h - 2])
                    for dc in range(NDC):
                        P.op("t", "matmul", pa[:, 0:512], wbig[:, dc, 128 * h:128 * (h + 1)],
                             ln1own[:, dc, :], start=(dc == 0), stop=(dc == NDC - 1),
                             inc=(sp if dc == NDC - 1 else None))
                    P.wait("s", sp, sp.n)
                    P.op("s", "activation", qfm[:, h, :],
                         pa[:, 0:512], AF.Identity, bias=qkvb[:, h:h + 1], inc=ss)
                    qev[h] = ss.n
                qproj_done = ss.n

                for h in range(HEADS):
                    P.wait("s", sp, sp.n)
                    if h == 0:
                        P.wait("s", ss, qproj_done)
                    P.op("s", "activation", qsq, qfm[:, h, :], AF.Square, inc=ss)
                    P.wait("t", ss, ss.n)
                    P.op("t", "matmul", psA[0][0:1, :], onescolb[:, 0:1], qsq,
                         start=True, stop=True, inc=sp)
                    P.wait("v", sp, sp.n)
                    P.op("v", "tensor_reduce", qmax2[0:1, 1:2], psA[0][0:1, :], AX,
                         ALU.max, inc=sv)
                    P.wait("v", sv, sv.n)
                    if h == 0:
                        P.op("v", "tensor_copy", qmax2[0:1, 0:1], qmax2[0:1, 1:2], inc=sv)
                    else:
                        P.op("v", "tensor_max", qmax2[0:1, 0:1], qmax2[0:1, 0:1],
                             qmax2[0:1, 1:2], inc=sv)
                    P.wait("v", sv, sv.n)
                    P.wait("t", sv, sv.n)
                P.op("v", "tensor_reduce", kmax2[0:1, 0:1], kmax2[0:1, 4:5 + s_i], AX,
                     ALU.max, inc=sv)
                P.wait("v", sv, sv.n)
                P.wait("s", sv, sv.n)
                P.op("s", "activation", qmax2[0:1, 2:3], qmax2[0:1, 0:1], AF.Ln, inc=ss)
                P.op("s", "activation", kmax2[0:1, 2:3], kmax2[0:1, 0:1], AF.Ln, inc=ss)
                P.wait("v", ss, ss.n)
                P.op("v", "tensor_add", qmax2[0:1, 3:4], qmax2[0:1, 2:3],
                     kmax2[0:1, 2:3], inc=sv)
                P.wait("v", sv, sv.n)
                P.wait("s", sv, sv.n)
                P.op("s", "activation", qmax2[0:1, 1:2], qmax2[0:1, 3:4], AF.Exp,
                     scale=0.5, bias=cmt[0:1, 0:1], inc=ss)
                P.wait("v", ss, ss.n)
                P.op("v", "tensor_scalar", qmax2[0:1, 1:2], qmax2[0:1, 1:2], 1.0, None,
                     ALU.add, inc=sv)
                P.wait("v", sv, sv.n)
                P.op("v", "tensor_scalar", qmax2[0:1, 1:2], qmax2[0:1, 1:2], -1.0, None,
                     ALU.mult, inc=sv)
                P.wait("v", sv, sv.n)
                P.wait("t", sv, sv.n)
                P.op("t", "matmul", psA[0][:, 0:1], onesrow[:], qmax2[0:1, 1:2],
                     start=True, stop=True, inc=sp)
                P.wait("v", sp, sp.n)
                P.wait("v", ss, ss.n)
                P.op("v", "tensor_copy", negMb[:], psA[0][:, 0:1], inc=sv)
                negm_done = sv.n
                P.wait("s", sv, negm_done)
                P.wait("t", sv, negm_done)

                ctx_ev_done = {}
                sc_ev_done = {}
                for h in range(HEADS):
                    exp_done = {}
                    ready = {}
                    rsem = {}
                    ctx_done = {}

                    def emit_ctx(cp, h=h):
                        if cp == 0 and h >= 1:
                            P.wait("t", sv, ctx_ev_done[h - 1])
                        P.wait("t", rsem[cp], ready[cp])
                        pb = (cp // 2) % 2
                        psrc = pt
                        P.op("t", "matmul", psCc[:], vst[:, cp, 0:128],
                             psrc[:, pb, 0:512], start=(cp == 0), stop=False,
                             skip_group_check=True)
                        P.op("t", "matmul", psCc[:], vst[:, cp + 1, 0:128],
                             psrc[:, pb, 512:1024], start=False, stop=(cp == Es - 2),
                             skip_group_check=True)
                        P.op("t", "matmul", psL[0:1, :], onescolb[:, 0:1],
                             psrc[:, pb, 0:512], start=(cp == 0), stop=False,
                             skip_group_check=True)
                        P.op("t", "matmul", psL[0:1, :], onescolb[:, 0:1],
                             psrc[:, pb, 512:1024], start=False, stop=(cp == Es - 2),
                             skip_group_check=True, inc=sp)
                        ctx_done[cp] = sp.n

                    for cp in range(0, Es, 2):
                        pb = (cp // 2) % 2
                        pa = psAB[pb]
                        if cp >= 4:
                            P.wait("t", ss, exp_done[cp - 4])
                        P.op("t", "matmul", pa[:, 0:512], kfm[:, cp, :], qfm[:, h, :],
                             start=True, stop=True)
                        P.op("t", "matmul", pa[:, 512:1024], kfm[:, cp + 1, :], qfm[:, h, :],
                             start=True, stop=True, inc=sp)
                        sc_mm = sp.n
                        P.wait("s", sp, sc_mm)
                        if cp >= 4:
                            P.wait("s", sp, ctx_done[cp - 4])
                        P.op("s", "activation", pt[:, pb, :], pa[:, 0:1024], AF.Exp,
                             bias=negMb[:, 0:1], scale=SCALE, inc=ss)
                        exp_done[cp] = ss.n
                        if cp >= 16 * s_i:
                            P.op("v", "tensor_scalar", maskb[:], iota[:],
                                 cthr[:, cp:cp + 1], None, ALU.is_ge, inc=sv)
                            P.wait("v", sv, sv.n)
                            P.wait("v", ss, exp_done[cp])
                            P.op("v", "tensor_mul", pt[:, pb, :], pt[:, pb, :],
                                 maskb[:], inc=sv)
                            ready[cp], rsem[cp] = sv.n, sv
                        else:
                            ready[cp], rsem[cp] = exp_done[cp], ss
                        if cp >= 2:
                            emit_ctx(cp - 2)
                    emit_ctx(Es - 2)
                    P.wait("v", sp, ctx_done[Es - 2])
                    P.op("v", "reciprocal", rlrow[0:1, h % 2, :], psL[0:1, :], inc=sv)
                    P.wait("v", sv, sv.n)
                    P.op("v", "tensor_copy", rl16[0:1, :], rlrow[0:1, h % 2, :], inc=sv)
                    P.wait("t", sv, sv.n)
                    if h >= 1:
                        P.wait("t", ss, sc_ev_done[h - 1])
                    P.op("t", "matmul", psS[:], onesrowh[:], rl16[0:1, :],
                         start=True, stop=True, inc=sp)
                    P.wait("s", sp, sp.n)
                    P.op("s", "activation", rlb[:], psS[:], AF.Identity, inc=ss)
                    sc_ev_done[h] = ss.n
                    P.wait("v", ss, sc_ev_done[h])
                    P.op("v", "tensor_mul", cfm[:, h, :], psCc[:], rlb[:], inc=sv)
                    ctx_ev_done[h] = sv.n
                attn_done = sv.n

                if s_i == 0 and d_qfm is not None:
                    P.wait("g", ss, qproj_done)
                    P.dma("g", d_qfm, qfm[:].rearrange("p h q -> p (h q)"), so)
                if s_i == 0 and d_cfm is not None:
                    P.wait("g", sv, attn_done)
                    P.dma("g", d_cfm, cfm[:].rearrange("p h q -> p (h q)"), so)

                P.wait("g", ss, qproj_done)
                P.dma("g", wbig[:], woutT.rearrange("(c p) e -> p c e", p=128), sg)
                wo_loaded = sg.n
                P.wait("t", sg, wo_loaded)
                P.wait("t", sv, attn_done)
                for tl in range(4):
                    tt = 4 * s_i + tl
                    x_acquire(tl % 2)
                    P.dma("y", xbuf[:, tl % 2, :], xown[128 * tt:128 * (tt + 1), :], sx)
                    xo_ld = sx.n
                    aoev = {}
                    for dspan in range(4):
                        pa = psA[dspan % 2]
                        if dspan >= 2:
                            P.wait("t", sv, aoev[dspan - 2])
                        for ec in range(NDC):
                            P.op("t", "matmul", pa[:], cfm[:, ec, 128 * tl:128 * (tl + 1)],
                                 wbig[:, ec, 512 * dspan:512 * (dspan + 1)],
                                 start=(ec == 0), stop=(ec == NDC - 1),
                                 inc=(sp if ec == NDC - 1 else None))
                        P.wait("v", sp, sp.n)
                        if dspan == 0:
                            P.wait("v", sx, ydma_prev[0])
                            P.wait("v", so, ydma_prev[1])
                        P.op("v", "scalar_tensor_tensor", ao[:, 512 * dspan:512 * (dspan + 1)],
                             pa[:, 0:512], 1.0, outb_s[:, 512 * dspan:512 * (dspan + 1)],
                             ALU.bypass, ALU.add, inc=sv)
                        aoev[dspan] = sv.n
                    ln_stats(ao[:])
                    for q2 in range(2):
                        P.op("s", "activation", sqjunk[:, 0:1024],
                             ao[:, 1024 * q2:1024 * (q2 + 1)], AF.Identity,
                             bias=stt[:, 6:7], scale=stt[:, 5:6], inc=ss)
                        P.wait("v", ss, ss.n)
                        if q2 == 0:
                            P.wait("v", sx, xo_ld)
                        P.op("v", "tensor_add", ao[:, 1024 * q2:1024 * (q2 + 1)],
                             sqjunk[:, 0:1024], xbuf[:, tl % 2, 1024 * q2:1024 * (q2 + 1)],
                             inc=sv)
                        if q2 == 0:
                            P.wait("s", sv, sv.n)
                    xguard[tl % 2][1] = sv.n
                    y_done = sv.n
                    P.wait("y", sv, y_done)
                    P.dma("y", yscr[128 * tt:128 * (tt + 1), :], ao[:], sx)
                    if d_y is not None:
                        P.dma("y", d_y[128 * tt:128 * (tt + 1), :], ao[:], so)
                    ydma_prev = (sx.n, so.n)
                    ln_stats(ao[:])
                    P.op("s", "activation", ln2tm[:], ao[:], AF.Identity,
                         bias=stt[:, 6:7], scale=stt[:, 5:6], inc=ss)
                    P.wait("t", ss, ss.n)
                    for half in range(2):
                        P.wait("v", sx, l2dma_prev)
                        for i in range(8):
                            blk = 8 * half + i
                            P.op("t", "transpose", psT[:, 128 * i:128 * (i + 1)],
                                 ln2tm[:, 128 * blk:128 * (blk + 1)], id16[:],
                                 inc=(sp if i == 7 else None))
                        P.wait("v", sp, sp.n)
                        P.op("v", "tensor_copy", l2b[:],
                             psT[:, 0:1024].rearrange("p (c e) -> p c e", c=8), inc=sv)
                        P.wait("t", sv, sv.n)
                        P.wait("y", sv, sv.n)
                        P.dma("y", ln2scr[tt, 8 * half:8 * half + 8].rearrange("c p e -> p c e"),
                              l2b[:], sx)
                        l2dma_prev = sx.n
                        ln2rd[0] = sp.n
            P.flush()
        barrier()

        with contextlib.ExitStack() as fst:
            EF = fst.enter_context
            l2f = EF(nc.sbuf_tensor("l2f", [128, NDC, 512], BF16))
            hsp = EF(nc.sbuf_tensor("hsp", [128, NFC, 512], BF16))
            wg1 = EF(nc.sbuf_tensor("wg1", [128, 2, NDC, 512], BF16))
            wg2n = EF(nc.sbuf_tensor("wg2n", [128, 4, 1024], BF16))
            ysp = EF(nc.sbuf_tensor("ysp", [128, 2, D], F32))
            otile = EF(nc.sbuf_tensor("otile", [128, 2, D], F32))
            b2row = EF(nc.sbuf_tensor("b2rows", [128, D], F32))
            psB = [EF(nc.psum_tensor(f"psB{i}", [128, 1024], F32)) for i in range(4)]
            P.dma("s", b2row[:], b2row_i, sq)
            b2row_ld = sq.n
            ffs = wg1[:].rearrange("p a c e -> p (a c e)").bitcast(F32).rearrange(
                "p (t e) -> p t e", t=4)

            WQ = ("g", "y")
            WS = {"g": sg, "y": sx}

            a_used = {}
            b_used = {}
            outdma = {}
            ysp_guard = {}
            ffs_free = (0, 0)
            prev_a_last = 0
            span_psB_free = 0
            for sp_i in range(NSLOT):
                P.wait("s", sp, prev_a_last)
                for t_ in range(4):
                    P.dma("s", l2f[:, :, 128 * t_:128 * (t_ + 1)],
                          ln2scr[4 * sp_i + t_].rearrange("c p e -> p c e"), sq)
                l2f_ld = sq.n

                gelu_done = {}
                for g in range(16):
                    gb = g % 2
                    ka = 16 * sp_i + g
                    q_ = WQ[g % 2]
                    if ka >= 2:
                        P.wait(q_, sp, a_used[ka - 2])
                    if g < 2:
                        P.wait(q_, ss, ffs_free[0])
                        P.wait(q_, sv, ffs_free[1])
                    P.dma(q_, wg1[:, gb, :, :], w1q[:, :, 512 * g:512 * (g + 1)].rearrange(
                        "c p e -> p c e"), WS[q_])
                    g_ld = WS[q_].n
                    for fi in range(4):
                        fc = 4 * g + fi
                        pf = psB[fc % 4][:, 0:512]
                        if fi == 0:
                            P.wait("t", WS[q_], g_ld)
                        if fc == 0:
                            P.wait("t", sq, l2f_ld)
                            P.wait("t", sv, span_psB_free)
                        if fc >= 4:
                            P.wait("t", ss, gelu_done[fc - 4])
                        for dc in range(NDC):
                            P.op("t", "matmul", pf,
                                 wg1[:, gb, dc, 128 * fi:128 * (fi + 1)],
                                 l2f[:, dc, :], start=(dc == 0), stop=(dc == NDC - 1),
                                 inc=(sp if dc == NDC - 1 else None))
                        P.wait("s", sp, sp.n)
                        P.op("s", "activation", hsp[:, fc, :], pf, AF.Gelu,
                             bias=b1c[:, fc:fc + 1], inc=ss)
                        gelu_done[fc] = ss.n
                    a_used[ka] = sp.n
                a_done = ss.n
                prev_a_last = sp.n

                wg2ld = {}
                evac = {}

                def b_dma(i, sp_i=sp_i):
                    dh, fc = divmod(i, NFC)
                    q_ = WQ[i % 2]
                    if i >= 4:
                        P.wait(q_, sp, b_used[i - 4])
                    P.dma(q_, wg2n[:, i % 4, :],
                          w2q[fc, :, 1024 * dh:1024 * (dh + 1)], WS[q_])
                    wg2ld[i] = (WS[q_].n, WS[q_])

                def b_mms(i):
                    dh, fc = divmod(i, NFC)
                    n_, sem_ = wg2ld[i]
                    P.wait("t", sem_, n_)
                    if fc == 0:
                        P.wait("t", ss, a_done)
                    for tl in range(4):
                        if dh == 1 and fc == 0:
                            P.wait("t", sv, evac[(0, tl)])
                        lw = hsp[:, fc, 128 * tl:128 * (tl + 1)]
                        for dq in range(2):
                            P.op("t", "matmul", psB[tl][:, 512 * dq:512 * (dq + 1)],
                                 lw, wg2n[:, i % 4, 512 * dq:512 * (dq + 1)],
                                 start=(fc == 0), stop=(fc == NFC - 1),
                                 inc=(sp if (tl == 3 and dq == 1) else None))
                    b_used[i] = sp.n

                b_dma(0)
                b_dma(1)
                for i in range(2 * NFC):
                    if i + 2 < 2 * NFC:
                        b_dma(i + 2)
                    b_mms(i)
                    if i % NFC == NFC - 1:
                        dh = i // NFC
                        P.wait("v", sp, sp.n)
                        P.wait("v", sq, b2row_ld)
                        for tl in range(4):
                            P.op("v", "scalar_tensor_tensor",
                                 ffs[:, tl, 1024 * dh:1024 * (dh + 1)],
                                 psB[tl][:, 0:1024], 1.0,
                                 b2row[:, 1024 * dh:1024 * (dh + 1)], ALU.bypass,
                                 ALU.add, inc=sv)
                            evac[(dh, tl)] = sv.n
                span_psB_free = sv.n

                for tl in range(4):
                    tt = 4 * sp_i + tl
                    tb = tl % 2
                    par = tl % 2
                    if tt >= 2:
                        P.wait("s", sv, ysp_guard[tt - 2])
                    P.dma("s", ysp[:, tb, :], yscr[128 * tt:128 * (tt + 1), :], sq)
                    y_ld = sq.n
                    P.wait("v", sv, evac[(1, tl)])
                    P.wait("s", sv, evac[(1, tl)])
                    ln_stats(ffs[:, tl, :], par)
                    for q2 in range(2):
                        P.op("s", "activation", sqjunk[:, 0:1024],
                             ffs[:, tl, 1024 * q2:1024 * (q2 + 1)], AF.Identity,
                             bias=C(par, 6), scale=C(par, 5), inc=ss)
                        P.wait("v", ss, ss.n)
                        if q2 == 0:
                            P.wait("v", sq, y_ld)
                            if tt >= 2:
                                P.wait("v", so, outdma[tt - 2])
                        P.op("v", "tensor_add", otile[:, tb, 1024 * q2:1024 * (q2 + 1)],
                             sqjunk[:, 0:1024], ysp[:, tb, 1024 * q2:1024 * (q2 + 1)],
                             inc=sv)
                        if q2 == 0:
                            P.wait("s", sv, sv.n)
                    P.wait("y", sv, sv.n)
                    P.dma("y", out_t[128 * tt:128 * (tt + 1), :], otile[:, tb, :], so)
                    outdma[tt] = so.n
                    ysp_guard[tt] = sv.n
                ffs_free = (ss.n, sv.n)
            P.wait("y", so, so.n)
            P.flush()

    print(f"[mk] total instructions: {P.ninstr}")
    return nc



def make_inputs(L0_attn_x, L0_attn_Wqkv_w, L0_attn_Wqkv_b, L0_attn_out_w,
                L0_attn_out_b, L0_ffn_w1, L0_ffn_b1, L0_ffn_w2, L0_ffn_b2):
    f = np.float32
    wqkvT = np.ascontiguousarray(np.asarray(L0_attn_Wqkv_w, f).T)
    woutT = np.ascontiguousarray(np.asarray(L0_attn_out_w, f).T)
    w1T = np.ascontiguousarray(np.asarray(L0_ffn_w1, f).T)
    w2T = np.ascontiguousarray(np.asarray(L0_ffn_w2, f).T)
    qkvb = np.ascontiguousarray(np.asarray(L0_attn_Wqkv_b, f).reshape(18, 128).T)
    kb = np.ascontiguousarray(np.broadcast_to(np.asarray(L0_attn_Wqkv_b[2048:2176], f), (128, 128)))
    vb = np.ascontiguousarray(np.broadcast_to(np.asarray(L0_attn_Wqkv_b[2176:2304], f), (128, 128)))
    outb = np.ascontiguousarray(np.broadcast_to(
        np.asarray(L0_attn_out_b, f).astype(ml_dtypes.bfloat16), (128, D)))
    b1c = np.ascontiguousarray(np.asarray(L0_ffn_b1, f).reshape(NFC, 128).T)
    b2row = np.ascontiguousarray(np.broadcast_to(np.asarray(L0_ffn_b2, f), (128, D)))
    id16 = np.eye(128, dtype=ml_dtypes.bfloat16)
    idh = np.eye(128, dtype=np.float16)
    id32 = np.eye(128, dtype=f)
    onescol = np.ones((128, 4), f)
    onescolb = np.ones((128, 4), ml_dtypes.bfloat16)
    onescolh = np.ones((128, 4), np.float16)
    onesrow = np.ones((1, 128), f)
    onesrowh = np.ones((1, 128), ml_dtypes.bfloat16)
    iov = np.concatenate([np.arange(512, dtype=np.float16),
                          np.arange(512, dtype=np.float16) - np.float16(128.0)])
    iota = np.ascontiguousarray(np.broadcast_to(iov, (128, 1024)))

    x = np.asarray(L0_attn_x, f)
    in_maps = []
    layout = []
    for c in range(8):
        b, j = divmod(c, 4)
        plist = p_list_for(j)
        xo = np.ascontiguousarray(
            np.concatenate([x[b, 512 * p:512 * (p + 1), :] for p in plist], axis=0))
        cth = np.zeros((128, 64), np.float32)
        pvec = np.arange(128, dtype=np.float32)
        for s_i in range(4):
            Pm = plist[s_i]
            for cc in range(16 * s_i, 16 * (s_i + 1)):
                t = 128.0 * cc + pvec - 512.0 * Pm
                cth[:, cc] = np.clip(t, -640.0, 520.0)
        cth = cth.astype(np.float32)
        in_maps.append({
            "xseq": np.ascontiguousarray(x[b]),
            "xown": xo,
            "wqkvT": wqkvT, "woutT": woutT, "w1T": w1T, "w2T": w2T,
            "qkvb": qkvb, "kb": kb, "vb": vb, "outb": outb,
            "b1c": b1c, "b2row": b2row,
            "id16": id16, "idh": idh, "id32": id32,
            "onescol": onescol, "onescolb": onescolb, "onescolh": onescolh,
            "onesrow": onesrow, "onesrowh": onesrowh,
            "iota": iota, "cthr": cth,
        })
        layout.append((b, plist))
    return in_maps, layout


def kernel(L0_attn_x, L0_attn_Wqkv_w, L0_attn_Wqkv_b, L0_attn_out_w, L0_attn_out_b,
           L0_ffn_w1, L0_ffn_b1, L0_ffn_w2, L0_ffn_b2, debug=()):
    in_maps, layout = make_inputs(L0_attn_x, L0_attn_Wqkv_w, L0_attn_Wqkv_b,
                                  L0_attn_out_w, L0_attn_out_b, L0_ffn_w1,
                                  L0_ffn_b1, L0_ffn_w2, L0_ffn_b2)
    nc = build(debug=debug)
    res = run_bass_kernel_spmd(nc, in_maps, core_ids=list(range(8)))
    out = np.empty((2, S, D), np.float32)
    for c in range(8):
        b, plist = layout[c]
        o = res.results[c]["out"]
        for si, p in enumerate(plist):
            out[b, 512 * p:512 * (p + 1), :] = o[512 * si:512 * (si + 1), :]
    if debug:
        return out, res
    return out


# version 13
# speedup vs baseline: 1.2436x; 1.0815x over previous
import contextlib
import math
import numpy as np
import ml_dtypes
import concourse.bass as bass
import concourse.mybir as mybir
from concourse.bass_utils import run_bass_kernel_spmd

F32 = mybir.dt.float32
BF16 = mybir.dt.bfloat16
FP16 = mybir.dt.float16
AF = mybir.ActivationFunctionType
ALU = mybir.AluOpType
AX = mybir.AxisListType.X

D = 2048
S = 8192
HEADS = 16
DH = 128
DFF = 8192
NDC = D // 128
NFC = DFF // 128
NKC = S // 128
NT = 16
NSLOT = 4
SCALE = 1.0 / math.sqrt(DH)
EPS = 1e-5
CM_CONST = -0.5 * math.log(128.0) + math.log(1.05)

ENGINES = ("t", "v", "s", "y", "g")


class Sem:
    def __init__(self, stack, nc, name):
        self.h = stack.enter_context(nc.semaphore(name))
        self.n = 0

    def bump(self, k=1):
        self.n += k
        return self.n


class Prog:

    def __init__(self, nc):
        self.nc = nc
        self.ops = []
        self.ninstr = 0

    def op(self, eng, name, *args, inc=None, k=1, **kw):
        self.ops.append((eng, name, args, kw, inc, k))
        self.ninstr += 1
        if inc is not None:
            return inc.bump(k)
        return None

    def wait(self, eng, sem, n):
        if n > 0:
            self.ops.append((eng, "wait_ge", (sem.h, n), {}, None, 0))
            self.ninstr += 1

    def dma(self, eng, out, in_, inc, **kw):
        return self.op(eng, "dma_start", inc=inc, k=16, out=out, in_=in_, **kw)

    def flush(self):
        streams = {e: [] for e in ENGINES}
        for o in self.ops:
            streams[o[0]].append(o)
        self.ops = []

        def runner(engname):
            def run(e):
                for (_, name, args, kw, inc, k) in streams[engname]:
                    ins = getattr(e, name)(*args, **kw)
                    if inc is not None:
                        ins.then_inc(inc.h, k)
            return run

        with self.nc.Block() as block:
            for attr, key in (("tensor", "t"), ("vector", "v"), ("scalar", "s"),
                              ("sync", "y"), ("gpsimd", "g")):
                if streams[key]:
                    getattr(block, attr)(runner(key))


def p_list_for(j):
    return sorted({j, 7 - j, 8 + j, 15 - j})


def build(debug=()):
    nc = bass.Bass("TRN2", target_bir_lowering=False)
    dbg = set(debug)

    xseq = nc.dram_tensor("xseq", [S, D], F32, kind="ExternalInput").ap()
    xown = nc.dram_tensor("xown", [NT * 128, D], F32, kind="ExternalInput").ap()
    wqkvT = nc.dram_tensor("wqkvT", [D, 2304], F32, kind="ExternalInput").ap()
    woutT = nc.dram_tensor("woutT", [D, D], F32, kind="ExternalInput").ap()
    w1T = nc.dram_tensor("w1T", [D, DFF], F32, kind="ExternalInput").ap()
    w2T = nc.dram_tensor("w2T", [DFF, D], F32, kind="ExternalInput").ap()
    qkvb_i = nc.dram_tensor("qkvb", [128, 18], F32, kind="ExternalInput").ap()
    kb_i = nc.dram_tensor("kb", [128, 128], F32, kind="ExternalInput").ap()
    vb_i = nc.dram_tensor("vb", [128, 128], F32, kind="ExternalInput").ap()
    outb_i = nc.dram_tensor("outb", [128, D], BF16, kind="ExternalInput").ap()
    b1c_i = nc.dram_tensor("b1c", [128, NFC], F32, kind="ExternalInput").ap()
    b2row_i = nc.dram_tensor("b2row", [128, D], F32, kind="ExternalInput").ap()
    id16_i = nc.dram_tensor("id16", [128, 128], BF16, kind="ExternalInput").ap()
    idh_i = nc.dram_tensor("idh", [128, 128], FP16, kind="ExternalInput").ap()
    id32_i = nc.dram_tensor("id32", [128, 128], F32, kind="ExternalInput").ap()
    onescol_i = nc.dram_tensor("onescol", [128, 4], F32, kind="ExternalInput").ap()
    onescolb_i = nc.dram_tensor("onescolb", [128, 4], BF16, kind="ExternalInput").ap()
    onescolh_i = nc.dram_tensor("onescolh", [128, 4], FP16, kind="ExternalInput").ap()
    onesrow_i = nc.dram_tensor("onesrow", [1, 128], F32, kind="ExternalInput").ap()
    onesrowh_i = nc.dram_tensor("onesrowh", [1, 128], BF16, kind="ExternalInput").ap()
    iota_i = nc.dram_tensor("iota", [128, 1024], FP16, kind="ExternalInput").ap()
    cthr_i = nc.dram_tensor("cthr", [128, 64], F32, kind="ExternalInput").ap()

    yscr = nc.dram_tensor("yscr", [NT * 128, D], F32, kind="Internal").ap()
    ln2scr = nc.dram_tensor("ln2scr", [NT, NDC, 128, 128], BF16, kind="Internal").ap()
    w1q = nc.dram_tensor("w1q", [NDC, 128, DFF], BF16, kind="Internal").ap()
    w2q = nc.dram_tensor("w2q", [NFC, 128, D], BF16, kind="Internal").ap()

    out_t = nc.dram_tensor("out", [NT * 128, D], F32, kind="ExternalOutput").ap()
    dbg_t = {}

    def dbg_tensor(name, shape, dt=F32):
        if name in dbg:
            dbg_t[name] = nc.dram_tensor("dbg_" + name, shape, dt, kind="ExternalOutput").ap()
        return dbg_t.get(name)

    d_kfm = dbg_tensor("kfm", [128, NKC * 128])
    d_v = dbg_tensor("vst", [128, NKC * 128])
    d_qfm = dbg_tensor("qfm", [128, HEADS * 512])
    d_cfm = dbg_tensor("cfm", [128, HEADS * 512])
    d_wq1 = dbg_tensor("wq1", [128, DFF])
    d_thr = dbg_tensor("thr", [128, 8])
    d_ln1 = dbg_tensor("ln1t0", [128, NDC * 128])
    d_st = dbg_tensor("stat", [128, 16])
    d_y = dbg_tensor("y", [NT * 128, D])

    with contextlib.ExitStack() as gst:
        E = gst.enter_context
        sx = Sem(gst, nc, "sx")
        sg = Sem(gst, nc, "sg")
        sp = Sem(gst, nc, "sp")
        sv = Sem(gst, nc, "sv")
        ss = Sem(gst, nc, "ss")
        so = Sem(gst, nc, "so")
        sq = Sem(gst, nc, "sq")
        ALLSEMS = (sx, sg, sp, sv, ss, so, sq)

        id16 = E(nc.sbuf_tensor("id16s", [128, 128], BF16))
        idh = E(nc.sbuf_tensor("idhs", [128, 128], FP16))
        id32 = E(nc.sbuf_tensor("id32s", [128, 128], F32))
        onescol = E(nc.sbuf_tensor("onescols", [128, 4], F32))
        onescolb = E(nc.sbuf_tensor("onescolbs", [128, 4], BF16))
        onescolh = E(nc.sbuf_tensor("onescolhs", [128, 4], FP16))
        onesrow = E(nc.sbuf_tensor("onesrows", [1, 128], F32))
        onesrowh = E(nc.sbuf_tensor("onesrowhs", [1, 128], BF16))
        iota = E(nc.sbuf_tensor("iotas", [128, 1024], FP16))
        cthr = E(nc.sbuf_tensor("cthrs", [128, 64], F32))
        qkvb = E(nc.sbuf_tensor("qkvbs", [128, 18], F32))
        kb_s = E(nc.sbuf_tensor("kbs", [128, 128], F32))
        vb_s = E(nc.sbuf_tensor("vbs", [128, 128], F32))
        outb_s = E(nc.sbuf_tensor("outbs", [128, D], BF16))
        b1c = E(nc.sbuf_tensor("b1cs", [128, NFC], F32))
        cmt = E(nc.sbuf_tensor("cmt", [128, 1], F32))
        epst = E(nc.sbuf_tensor("epst", [128, 1], F32))
        negMb = E(nc.sbuf_tensor("negMb", [128, 1], F32))
        stt = E(nc.sbuf_tensor("stt", [128, 16], F32))
        kmax2 = E(nc.sbuf_tensor("kmax2", [1, 8], F32))
        qmax2 = E(nc.sbuf_tensor("qmax2", [1, 4], F32))
        sqjunk = E(nc.sbuf_tensor("sqjunk", [128, 1024], F32))
        thr1 = E(nc.sbuf_tensor("thr1", [128, 4], F32))
        absacc = E(nc.sbuf_tensor("absacc", [128, 2], F32))
        mtmp = E(nc.sbuf_tensor("mtmp", [128, 8], F32))

        P = Prog(nc)

        def barrier():
            for e in ENGINES:
                for sem in ALLSEMS:
                    P.wait(e, sem, sem.n)

        for dst, src in ((id16, id16_i), (idh, idh_i), (id32, id32_i),
                         (onescol, onescol_i), (onescolb, onescolb_i),
                         (onescolh, onescolh_i), (onesrow, onesrow_i),
                         (onesrowh, onesrowh_i), (iota, iota_i), (cthr, cthr_i),
                         (qkvb, qkvb_i), (kb_s, kb_i), (vb_s, vb_i),
                         (outb_s, outb_i), (b1c, b1c_i)):
            P.dma("y", dst[:], src, sx)
        P.op("v", "memset", cmt[:], CM_CONST, inc=sv)
        P.op("v", "memset", epst[:], EPS, inc=sv)
        consts_done = sx.n


        with contextlib.ExitStack() as ast_:
            EA = ast_.enter_context
            kfm = EA(nc.sbuf_tensor("kfm", [128, NKC, 128], BF16))
            vst = EA(nc.sbuf_tensor("vst", [128, NKC, 128], BF16))
            qfm = EA(nc.sbuf_tensor("qfm", [128, HEADS, 512], BF16))
            cfm = EA(nc.sbuf_tensor("cfm", [128, HEADS, 512], BF16))
            wbig = EA(nc.sbuf_tensor("wbig", [128, NDC, D], BF16))
            xbuf = EA(nc.sbuf_tensor("xbuf", [128, 2, D], F32))
            ln1own = EA(nc.sbuf_tensor("ln1own", [128, NDC, 512], BF16))
            pt = EA(nc.sbuf_tensor("pt", [128, 2, 1024], BF16))
            maskb = EA(nc.sbuf_tensor("maskb", [128, 1024], BF16))
            ktmp = EA(nc.sbuf_tensor("ktmp", [128, 128], BF16))
            rlrow = EA(nc.sbuf_tensor("rlrow", [1, 2, 512], F32))
            rl16 = EA(nc.sbuf_tensor("rl16", [1, 512], BF16))
            rlb = EA(nc.sbuf_tensor("rlb", [128, 512], F32))
            qsq = None
            ao = EA(nc.sbuf_tensor("ao", [128, D], F32))
            ln2tm = EA(nc.sbuf_tensor("ln2tm", [128, D], BF16))
            qsq = pt[:, 0, 0:512]
            ln1fm = ln1own[:, 8:12, :].rearrange("p c (a e) -> p (c a) e", a=4)
            l2b = maskb[:].rearrange("p (c e) -> p c e", c=8)
            psAB = [EA(nc.psum_tensor(f"psAB{i}", [128, 1024], F32)) for i in range(2)]
            psA = [psAB[0][:, 0:512], psAB[1][:, 0:512]]
            psCc = EA(nc.psum_tensor("psCc", [128, 512], F32))
            psL = EA(nc.psum_tensor("psL", [128, 512], F32))
            psS = EA(nc.psum_tensor("psS", [128, 512], F32))
            psT = EA(nc.psum_tensor("psT", [128, 1024], BF16))

            wkv_t = EA(nc.sbuf_tensor("wkv", [128, NDC, 256], BF16))
            wkv = wkv_t[:]
            P.dma("g", wkv, wqkvT[:, 2048:2304].rearrange("(c p) e -> p c e", p=128), sg)
            wkv_done = sg.n

            def C(par, i):
                return stt[:, 8 * par + i:8 * par + i + 1]

            def ln_stats1(src, par):
                P.op("v", "tensor_reduce", C(par, 0), src, AX, ALU.add, inc=sv)
                P.wait("s", sv, sv.n)
                P.op("s", "activation", sqjunk[:], src[:, 0:1024], AF.Square,
                     accum_out=C(par, 2), inc=ss)
                P.op("s", "activation", sqjunk[:], src[:, 1024:2048], AF.Square,
                     accum_out=C(par, 1), inc=ss)
                return ss.n

            def ln_stats2(par, s1done):
                P.wait("v", ss, s1done)
                P.op("v", "tensor_add", C(par, 1), C(par, 1), C(par, 2), inc=sv)
                P.op("v", "tensor_scalar", C(par, 3), C(par, 0), 1.0 / D, None, ALU.mult, inc=sv)
                P.wait("v", sv, sv.n)
                P.op("v", "tensor_mul", C(par, 4), C(par, 3), C(par, 3), inc=sv)
                P.op("v", "tensor_scalar", C(par, 2), C(par, 1), 1.0 / D, None, ALU.mult, inc=sv)
                P.wait("v", sv, sv.n)
                P.op("v", "tensor_sub", C(par, 4), C(par, 2), C(par, 4), inc=sv)
                P.wait("v", sv, sv.n)
                P.wait("s", sv, sv.n)
                P.op("s", "activation", C(par, 7), C(par, 4), AF.Ln, bias=epst[:, 0:1], inc=ss)
                P.op("s", "activation", C(par, 5), C(par, 7), AF.Exp, scale=-0.5, inc=ss)
                P.wait("v", ss, ss.n)
                P.op("v", "scalar_tensor_tensor", C(par, 6), C(par, 3), -1.0,
                     C(par, 5), ALU.mult, ALU.mult, inc=sv)
                P.wait("s", sv, sv.n)

            def ln_stats(src, par=0):
                d1 = ln_stats1(src, par)
                ln_stats2(par, d1)

            def norm_cast_transpose(src, dst_chunked, par=0):
                stg = (pt[:, 0, 0:512], pt[:, 0, 512:1024], pt[:, 1, 0:512], pt[:, 1, 512:1024])
                P.wait("s", sp, sp.n)
                for q4 in range(4):
                    P.op("s", "activation", stg[q4], src[:, 512 * q4:512 * (q4 + 1)],
                         AF.Identity, bias=C(par, 6), scale=C(par, 5), inc=ss)
                norm_done = ss.n
                P.wait("t", ss, norm_done)
                for half in range(2):
                    for i in range(8):
                        blk = 8 * half + i
                        q4, r = divmod(blk, 4)
                        P.op("t", "transpose", psT[:, 128 * i:128 * (i + 1)],
                             stg[q4][:, 128 * r:128 * (r + 1)], id16[:],
                             inc=(sp if i == 7 else None))
                    P.wait("v", sp, sp.n)
                    P.op("v", "tensor_copy",
                         dst_chunked[:, 8 * half:8 * half + 8, :],
                         psT[:, 0:1024].rearrange("p (c e) -> p c e", c=8), inc=sv)
                    P.wait("t", sv, sv.n)
                return norm_done

            wfa2 = cfm[:].bitcast(F32).rearrange("p c e -> p (c e)").rearrange(
                "p (s e) -> p s e", s=2)
            wfa3 = ao[:]
            l1B = ln1own[:, 0:8, :].rearrange("p c e -> p (c e)").rearrange(
                "p (s e) -> p s e", s=2)
            wqoB = l1B
            leb = ln2tm[:]

            def wslot(u):
                s_ = u % 3
                return wfa3 if s_ == 2 else wfa2[:, s_, :]

            def wsub_in(u):
                if u < 64:
                    i, s_ = divmod(u, 4)
                    return w1T[128 * i:128 * (i + 1), 2048 * s_:2048 * (s_ + 1)]
                return w2T[128 * (u - 64):128 * (u - 63), :]

            def wsub_out(u):
                if u < 64:
                    i, s_ = divmod(u, 4)
                    return w1q[i, :, 2048 * s_:2048 * (s_ + 1)]
                return w2q[u - 64, :, :]

            qload = {}
            quse = {}
            qdma = {}

            quse_s = {}

            def quant_load(L):
                u = L if L < 128 else L - 128
                if L >= 3:
                    P.wait("y", sv, quse.get(L - 3, 0))
                    P.wait("y", ss, quse_s.get(L - 3, 0))
                P.dma("y", wslot(L), wsub_in(u), sx)
                qload[L] = sx.n

            mean_vdone = {}

            def quant_mean_sub(u):
                p2 = (u % 2) * 2
                P.wait("s", sx, qload[u])
                if u >= 2:
                    P.wait("s", sv, mean_vdone[u - 2])
                P.op("s", "activation", sqjunk[:], wslot(u)[:, 0:1024], AF.Abs,
                     accum_out=mtmp[:, p2:p2 + 1], inc=ss)
                P.op("s", "activation", sqjunk[:], wslot(u)[:, 1024:2048], AF.Abs,
                     accum_out=mtmp[:, p2 + 1:p2 + 2], inc=ss)
                quse_s[u] = ss.n
                quse[u] = 0
                tgt = absacc[:, 0:1] if u < 64 else absacc[:, 1:2]
                P.wait("v", ss, ss.n)
                P.op("v", "tensor_add", mtmp[:, 4 + p2 // 2:5 + p2 // 2],
                     mtmp[:, p2:p2 + 1], mtmp[:, p2 + 1:p2 + 2], inc=sv)
                P.wait("v", sv, sv.n)
                if u == 0 or u == 64:
                    P.op("v", "tensor_copy", tgt, mtmp[:, 4 + p2 // 2:5 + p2 // 2], inc=sv)
                else:
                    P.op("v", "tensor_add", tgt, tgt, mtmp[:, 4 + p2 // 2:5 + p2 // 2], inc=sv)
                P.wait("v", sv, sv.n)
                mean_vdone[u] = sv.n

            def quant_thr():
                P.wait("t", sv, sv.n)
                P.op("t", "matmul", psL[0:1, 0:2], onescol[:, 0:1], absacc[:, 0:2],
                     start=True, stop=True, inc=sp)
                P.wait("v", sp, sp.n)
                P.op("v", "tensor_scalar", mtmp[0:1, 0:2], psL[0:1, 0:2],
                     0.7 / float(DFF * D), None, ALU.mult, inc=sv)
                P.wait("t", sv, sv.n)
                P.op("t", "matmul", psS[:, 0:2], onesrow[:], mtmp[0:1, 0:2],
                     start=True, stop=True, inc=sp)
                P.wait("v", sp, sp.n)
                P.op("v", "tensor_copy", thr1[:, 0:2], psS[:, 0:2], inc=sv)
                P.wait("v", sv, sv.n)
                P.op("v", "tensor_scalar", thr1[:, 2:4], thr1[:, 0:2], -1.0, None,
                     ALU.mult, inc=sv)
                P.wait("v", sv, sv.n)
                if d_thr is not None:
                    P.wait("g", sv, sv.n)
                    P.dma("g", d_thr[:, 0:4], thr1[:], so)
                    P.dma("g", d_thr[:, 4:6], absacc[:], so)

            def quant_q_sub(u):
                L = 128 + u
                tcol = 0 if u < 64 else 1
                P.wait("v", sx, qload[L])
                P.op("v", "tensor_scalar", leb, wslot(L),
                     thr1[:, 2 + tcol:3 + tcol], None, ALU.is_le, inc=sv)
                if u >= 2:
                    P.wait("v", sg, qdma[u - 2])
                P.wait("v", sv, sv.n)
                P.op("v", "scalar_tensor_tensor", wqoB[:, u % 2, :], wslot(L),
                     thr1[:, tcol:tcol + 1], leb, ALU.is_ge, ALU.subtract, inc=sv)
                quse[L] = sv.n
                quse_s[L] = 0
                P.wait("g", sv, quse[L])
                P.dma("g", wsub_out(u), wqoB[:, u % 2, :], sg)
                qdma[u] = sg.n

            P.wait("t", sg, wkv_done)
            P.wait("t", sx, consts_done)
            tile_done = [None] * NKC
            s1d = [None] * NKC

            xld = [None] * (NKC + 1)
            xguard = {0: [0, 0], 1: [0, 0]}

            def x_acquire(buf):
                P.wait("y", ss, xguard[buf][0])
                P.wait("y", sv, xguard[buf][1])

            def x_load(k):
                if k >= NKC:
                    return
                x_acquire(k % 2)
                P.dma("y", xbuf[:, k % 2, :], xseq[128 * k:128 * (k + 1), :], sx)
                xld[k] = sx.n

            def kv_part1(k):
                buf = k % 2
                P.wait("v", sx, xld[k])
                P.wait("s", sx, xld[k])
                s1d[k] = ln_stats1(xbuf[:, buf, :], k % 2)

            def kv_evac(k, pa):
                P.wait("v", sp, sp.n)
                P.op("v", "scalar_tensor_tensor", ktmp[:], pa[:, 0:128], 1.0, kb_s[:],
                     ALU.bypass, ALU.add, inc=sv)
                P.op("v", "scalar_tensor_tensor", vst[:, k, 0:128], pa[:, 128:256], 1.0,
                     vb_s[:], ALU.bypass, ALU.add, inc=sv)
                P.wait("t", sv, sv.n)
                P.op("t", "transpose", psT[:, 0:128], ktmp[:], id16[:], inc=sp)
                P.wait("v", sp, sp.n)
                P.op("v", "tensor_copy", kfm[:, k, :], psT[:, 0:128], inc=sv)
                P.wait("t", sv, sv.n)

            def kv_part2(k):
                buf = k % 2
                par = k % 2
                ln_stats2(par, s1d[k])
                tile_done[k] = norm_cast_transpose(xbuf[:, buf, :], ln1fm, par)
                xguard[buf][0] = tile_done[k]
                pa = psA[k % 2][:, 0:256]
                for dc in range(NDC):
                    P.op("t", "matmul", pa, ln1fm[:, dc, :], wkv[:, dc, :],
                         start=(dc == 0), stop=(dc == NDC - 1),
                         inc=(sp if dc == NDC - 1 else None))
                kv_evac(k, pa)

            psT32 = psT[:].bitcast(F32)

            ln2rd = [0]

            def kv_stage(k):
                par = k % 2
                ln_stats2(par, s1d[k])
                P.wait("s", sp, ln2rd[0])
                for q4 in range(4):
                    P.op("s", "activation", ln2tm[:, 512 * q4:512 * (q4 + 1)],
                         xbuf[:, k % 2, 512 * q4:512 * (q4 + 1)], AF.Identity,
                         bias=C(par, 6), scale=C(par, 5), inc=ss)
                tile_done[k] = ss.n
                xguard[k % 2][0] = ss.n

            def kv_finish(k):
                P.wait("t", ss, tile_done[k])
                for half in range(2):
                    for i in range(8):
                        blk = 8 * half + i
                        P.op("t", "transpose", psT[:, 128 * i:128 * (i + 1)],
                             ln2tm[:, 128 * blk:128 * (blk + 1)], id16[:],
                             inc=(sp if i == 7 else None))
                    P.wait("v", sp, sp.n)
                    P.op("v", "tensor_copy", ln1fm[:, 8 * half:8 * half + 8, :],
                         psT[:, 0:1024].rearrange("p (c e) -> p c e", c=8), inc=sv)
                    P.wait("t", sv, sv.n)
                ln2rd[0] = sp.n
                pa = psT32[:, 0:256]
                for dc in range(NDC):
                    P.op("t", "matmul", pa, ln1fm[:, dc, :], wkv[:, dc, :],
                         start=(dc == 0), stop=(dc == NDC - 1),
                         inc=(sp if dc == NDC - 1 else None))
                kv_evac(k, pa)

            def knorm_group(g):
                for half in range(2):
                    c0 = 16 * g + 8 * half
                    P.wait("s", sp, sp.n)
                    P.wait("s", sv, sv.n)
                    P.op("s", "activation", sqjunk[:, 0:512],
                         kfm[:, c0:c0 + 4, :].rearrange("p c e -> p (c e)"),
                         AF.Square, inc=ss)
                    P.op("s", "activation", sqjunk[:, 512:1024],
                         kfm[:, c0 + 4:c0 + 8, :].rearrange("p c e -> p (c e)"),
                         AF.Square, inc=ss)
                    P.wait("t", ss, ss.n)
                    P.op("t", "matmul", psS[0:1, :], onescol[:, 0:1], sqjunk[:, 0:512],
                         start=True, stop=False, skip_group_check=True)
                    P.op("t", "matmul", psS[0:1, :], onescol[:, 0:1], sqjunk[:, 512:1024],
                         start=False, stop=True, skip_group_check=True, inc=sp)
                    P.wait("v", sp, sp.n)
                    P.op("v", "tensor_reduce", kmax2[0:1, 1:2], psS[0:1, :], AX,
                         ALU.max, inc=sv)
                    P.wait("v", sv, sv.n)
                    if half == 0:
                        P.op("v", "tensor_copy", kmax2[0:1, 4 + g:5 + g],
                             kmax2[0:1, 1:2], inc=sv)
                    else:
                        P.op("v", "tensor_max", kmax2[0:1, 4 + g:5 + g],
                             kmax2[0:1, 4 + g:5 + g], kmax2[0:1, 1:2], inc=sv)
                    P.wait("v", sv, sv.n)
                    P.wait("t", sv, sv.n)

            x_load(0)
            quant_load(0)
            quant_load(1)
            next_load = [2]

            def quant_step(Lc):
                if next_load[0] < 128:
                    quant_load(next_load[0])
                    next_load[0] += 1
                quant_mean_sub(Lc)

            for k in range(NKC + 1):
                if k < NKC:
                    kv_part1(k)
                if k >= 1:
                    kv_part2(k - 1)
                if k < NKC:
                    x_load(k + 1)
                if k >= 1:
                    for Lc in range(2 * (k - 1), 2 * k):
                        quant_step(Lc)

            for e in ENGINES:
                P.wait(e, sx, sx.n)
                P.wait(e, sg, sg.n)
                P.wait(e, sv, sv.n)
                P.wait(e, ss, ss.n)
            quant_thr()
            for g_ in range(4):
                knorm_group(g_)
            kmax_done = sv.n

            regF = wbig[:].bitcast(F32).rearrange("p c e -> p (c e)")
            regB = wbig[:].rearrange("p c e -> p (c e)")
            wfa2n = regF[:, 8192:12288].rearrange("p (s e) -> p s e", s=2)
            wqoBn = regB[:, 24576:28672].rearrange("p (s e) -> p s e", s=2)
            lebn = regB[:, 28672:30720]
            p2_load = {}
            p2_use = {}
            p2_dma = {}
            p2_next = [0]
            p2_proc = [0]
            p2_sx = [0]
            p2_sv = [0]
            p2_gate = [None]

            def p2_load_one():
                u = p2_next[0]
                if u >= 128:
                    return
                if p2_gate[0] is not None:
                    P.wait("y", ss, p2_gate[0])
                    p2_gate[0] = None
                if u >= 2:
                    P.wait("y", sv, p2_use.get(u - 2, 0))
                P.dma("y", wfa2n[:, u % 2, :], wsub_in(u), sx)
                p2_load[u] = sx.n
                p2_sx[0] = sx.n
                p2_next[0] += 1

            def p2_step(n, limit):
                for _ in range(n):
                    while p2_next[0] < min(p2_proc[0] + 2, limit):
                        p2_load_one()
                    u = p2_proc[0]
                    if u >= limit:
                        return
                    tcol = 0 if u < 64 else 1
                    P.wait("v", sx, p2_load[u])
                    P.op("v", "tensor_scalar", lebn, wfa2n[:, u % 2, :],
                         thr1[:, 2 + tcol:3 + tcol], None, ALU.is_le, inc=sv)
                    if u >= 2:
                        P.wait("v", sg, p2_dma[u - 2])
                    P.wait("v", sv, sv.n)
                    P.op("v", "scalar_tensor_tensor", wqoBn[:, u % 2, :],
                         wfa2n[:, u % 2, :], thr1[:, tcol:tcol + 1], lebn,
                         ALU.is_ge, ALU.subtract, inc=sv)
                    p2_use[u] = sv.n
                    p2_sv[0] = sv.n
                    P.wait("g", sv, p2_use[u])
                    P.dma("g", wsub_out(u), wqoBn[:, u % 2, :], sg)
                    p2_dma[u] = sg.n
                    p2_proc[0] += 1

            P2Q = (
                [0] * HEADS,
                [3 if h_ < 4 else 2 for h_ in range(HEADS)],
                [3 if h_ < 14 else 2 for h_ in range(HEADS)],
                [3 if h_ < 14 else 2 for h_ in range(HEADS)],
            )

            ydma_prev = (0, 0)
            l2dma_prev = 0
            for s_i in range(NSLOT):
                Es = 16 * (s_i + 1)
                P.wait("v", sx, sx.n)
                P.wait("g", sp, sp.n)
                P.dma("g", wbig[:], wqkvT[:, 0:2048].rearrange("(c p) e -> p c e", p=128), sg)
                wq_loaded = sg.n

                for tl in range(4):
                    tt = 4 * s_i + tl
                    buf = tl % 2
                    x_acquire(buf)
                    P.dma("y", xbuf[:, buf, :], xown[128 * tt:128 * (tt + 1), :], sx)
                    P.wait("v", sx, sx.n)
                    P.wait("s", sx, sx.n)
                    ln_stats(xbuf[:, buf, :], tl % 2)
                    norm_cast_transpose(
                        xbuf[:, buf, :],
                        ln1own[:, :, 128 * tl:128 * (tl + 1)], tl % 2)
                    xguard[buf][0] = ss.n
                P.wait("t", sg, wq_loaded)
                qev = {}
                for h in range(HEADS):
                    pa = psA[h % 2]
                    if h >= 2:
                        P.wait("t", ss, qev[h - 2])
                    for dc in range(NDC):
                        P.op("t", "matmul", pa[:, 0:512], wbig[:, dc, 128 * h:128 * (h + 1)],
                             ln1own[:, dc, :], start=(dc == 0), stop=(dc == NDC - 1),
                             inc=(sp if dc == NDC - 1 else None))
                    P.wait("s", sp, sp.n)
                    P.op("s", "activation", qfm[:, h, :],
                         pa[:, 0:512], AF.Identity, bias=qkvb[:, h:h + 1], inc=ss)
                    qev[h] = ss.n
                qproj_done = ss.n
                p2_gate[0] = qproj_done

                for h in range(HEADS):
                    P.wait("s", sp, sp.n)
                    if h == 0:
                        P.wait("s", ss, qproj_done)
                    P.op("s", "activation", qsq, qfm[:, h, :], AF.Square, inc=ss)
                    P.wait("t", ss, ss.n)
                    P.op("t", "matmul", psA[0][0:1, :], onescolb[:, 0:1], qsq,
                         start=True, stop=True, inc=sp)
                    P.wait("v", sp, sp.n)
                    P.op("v", "tensor_reduce", qmax2[0:1, 1:2], psA[0][0:1, :], AX,
                         ALU.max, inc=sv)
                    P.wait("v", sv, sv.n)
                    if h == 0:
                        P.op("v", "tensor_copy", qmax2[0:1, 0:1], qmax2[0:1, 1:2], inc=sv)
                    else:
                        P.op("v", "tensor_max", qmax2[0:1, 0:1], qmax2[0:1, 0:1],
                             qmax2[0:1, 1:2], inc=sv)
                    P.wait("v", sv, sv.n)
                    P.wait("t", sv, sv.n)
                P.op("v", "tensor_reduce", kmax2[0:1, 0:1], kmax2[0:1, 4:5 + s_i], AX,
                     ALU.max, inc=sv)
                P.wait("v", sv, sv.n)
                P.wait("s", sv, sv.n)
                P.op("s", "activation", qmax2[0:1, 2:3], qmax2[0:1, 0:1], AF.Ln, inc=ss)
                P.op("s", "activation", kmax2[0:1, 2:3], kmax2[0:1, 0:1], AF.Ln, inc=ss)
                P.wait("v", ss, ss.n)
                P.op("v", "tensor_add", qmax2[0:1, 3:4], qmax2[0:1, 2:3],
                     kmax2[0:1, 2:3], inc=sv)
                P.wait("v", sv, sv.n)
                P.wait("s", sv, sv.n)
                P.op("s", "activation", qmax2[0:1, 1:2], qmax2[0:1, 3:4], AF.Exp,
                     scale=0.5, bias=cmt[0:1, 0:1], inc=ss)
                P.wait("v", ss, ss.n)
                P.op("v", "tensor_scalar", qmax2[0:1, 1:2], qmax2[0:1, 1:2], 1.0, None,
                     ALU.add, inc=sv)
                P.wait("v", sv, sv.n)
                P.op("v", "tensor_scalar", qmax2[0:1, 1:2], qmax2[0:1, 1:2], -1.0, None,
                     ALU.mult, inc=sv)
                P.wait("v", sv, sv.n)
                P.wait("t", sv, sv.n)
                P.op("t", "matmul", psA[0][:, 0:1], onesrow[:], qmax2[0:1, 1:2],
                     start=True, stop=True, inc=sp)
                P.wait("v", sp, sp.n)
                P.wait("v", ss, ss.n)
                P.op("v", "tensor_copy", negMb[:], psA[0][:, 0:1], inc=sv)
                negm_done = sv.n
                P.wait("s", sv, negm_done)
                P.wait("t", sv, negm_done)

                ctx_ev_done = {}
                sc_ev_done = {}
                p2_limit = sum(sum(P2Q[s2]) for s2 in range(s_i + 1))
                for h in range(HEADS):
                    p2_step(P2Q[s_i][h], p2_limit)
                    exp_done = {}
                    ready = {}
                    rsem = {}
                    ctx_done = {}

                    def emit_ctx(cp, h=h):
                        if cp == 0 and h >= 1:
                            P.wait("t", sv, ctx_ev_done[h - 1])
                        P.wait("t", rsem[cp], ready[cp])
                        pb = (cp // 2) % 2
                        psrc = pt
                        P.op("t", "matmul", psCc[:], vst[:, cp, 0:128],
                             psrc[:, pb, 0:512], start=(cp == 0), stop=False,
                             skip_group_check=True)
                        P.op("t", "matmul", psCc[:], vst[:, cp + 1, 0:128],
                             psrc[:, pb, 512:1024], start=False, stop=(cp == Es - 2),
                             skip_group_check=True)
                        P.op("t", "matmul", psL[0:1, :], onescolb[:, 0:1],
                             psrc[:, pb, 0:512], start=(cp == 0), stop=False,
                             skip_group_check=True)
                        P.op("t", "matmul", psL[0:1, :], onescolb[:, 0:1],
                             psrc[:, pb, 512:1024], start=False, stop=(cp == Es - 2),
                             skip_group_check=True, inc=sp)
                        ctx_done[cp] = sp.n

                    for cp in range(0, Es, 2):
                        pb = (cp // 2) % 2
                        pa = psAB[pb]
                        if cp >= 4:
                            P.wait("t", ss, exp_done[cp - 4])
                        P.op("t", "matmul", pa[:, 0:512], kfm[:, cp, :], qfm[:, h, :],
                             start=True, stop=True)
                        P.op("t", "matmul", pa[:, 512:1024], kfm[:, cp + 1, :], qfm[:, h, :],
                             start=True, stop=True, inc=sp)
                        sc_mm = sp.n
                        P.wait("s", sp, sc_mm)
                        if cp >= 4:
                            P.wait("s", sp, ctx_done[cp - 4])
                        P.op("s", "activation", pt[:, pb, :], pa[:, 0:1024], AF.Exp,
                             bias=negMb[:, 0:1], scale=SCALE, inc=ss)
                        exp_done[cp] = ss.n
                        if cp >= 16 * s_i:
                            P.op("v", "tensor_scalar", maskb[:], iota[:],
                                 cthr[:, cp:cp + 1], None, ALU.is_ge, inc=sv)
                            P.wait("v", sv, sv.n)
                            P.wait("v", ss, exp_done[cp])
                            P.op("v", "tensor_mul", pt[:, pb, :], pt[:, pb, :],
                                 maskb[:], inc=sv)
                            ready[cp], rsem[cp] = sv.n, sv
                        else:
                            ready[cp], rsem[cp] = exp_done[cp], ss
                        if cp >= 2:
                            emit_ctx(cp - 2)
                    emit_ctx(Es - 2)
                    P.wait("v", sp, ctx_done[Es - 2])
                    P.op("v", "reciprocal", rlrow[0:1, h % 2, :], psL[0:1, :], inc=sv)
                    P.wait("v", sv, sv.n)
                    P.op("v", "tensor_copy", rl16[0:1, :], rlrow[0:1, h % 2, :], inc=sv)
                    P.wait("t", sv, sv.n)
                    if h >= 1:
                        P.wait("t", ss, sc_ev_done[h - 1])
                    P.op("t", "matmul", psS[:], onesrowh[:], rl16[0:1, :],
                         start=True, stop=True, inc=sp)
                    P.wait("s", sp, sp.n)
                    P.op("s", "activation", rlb[:], psS[:], AF.Identity, inc=ss)
                    sc_ev_done[h] = ss.n
                    P.wait("v", ss, sc_ev_done[h])
                    P.op("v", "tensor_mul", cfm[:, h, :], psCc[:], rlb[:], inc=sv)
                    ctx_ev_done[h] = sv.n
                attn_done = sv.n

                if s_i == 0 and d_qfm is not None:
                    P.wait("g", ss, qproj_done)
                    P.dma("g", d_qfm, qfm[:].rearrange("p h q -> p (h q)"), so)
                if s_i == 0 and d_cfm is not None:
                    P.wait("g", sv, attn_done)
                    P.dma("g", d_cfm, cfm[:].rearrange("p h q -> p (h q)"), so)

                P.wait("g", ss, qproj_done)
                P.wait("g", sx, p2_sx[0])
                P.wait("g", sv, p2_sv[0])
                P.dma("g", wbig[:], woutT.rearrange("(c p) e -> p c e", p=128), sg)
                wo_loaded = sg.n
                P.wait("t", sg, wo_loaded)
                P.wait("t", sv, attn_done)
                for tl in range(4):
                    tt = 4 * s_i + tl
                    x_acquire(tl % 2)
                    P.dma("y", xbuf[:, tl % 2, :], xown[128 * tt:128 * (tt + 1), :], sx)
                    xo_ld = sx.n
                    aoev = {}
                    for dspan in range(4):
                        pa = psA[dspan % 2]
                        if dspan >= 2:
                            P.wait("t", sv, aoev[dspan - 2])
                        for ec in range(NDC):
                            P.op("t", "matmul", pa[:], cfm[:, ec, 128 * tl:128 * (tl + 1)],
                                 wbig[:, ec, 512 * dspan:512 * (dspan + 1)],
                                 start=(ec == 0), stop=(ec == NDC - 1),
                                 inc=(sp if ec == NDC - 1 else None))
                        P.wait("v", sp, sp.n)
                        if dspan == 0:
                            P.wait("v", sx, ydma_prev[0])
                            P.wait("v", so, ydma_prev[1])
                        P.op("v", "scalar_tensor_tensor", ao[:, 512 * dspan:512 * (dspan + 1)],
                             pa[:, 0:512], 1.0, outb_s[:, 512 * dspan:512 * (dspan + 1)],
                             ALU.bypass, ALU.add, inc=sv)
                        aoev[dspan] = sv.n
                    ln_stats(ao[:])
                    for q2 in range(2):
                        P.op("s", "activation", sqjunk[:, 0:1024],
                             ao[:, 1024 * q2:1024 * (q2 + 1)], AF.Identity,
                             bias=stt[:, 6:7], scale=stt[:, 5:6], inc=ss)
                        P.wait("v", ss, ss.n)
                        if q2 == 0:
                            P.wait("v", sx, xo_ld)
                        P.op("v", "tensor_add", ao[:, 1024 * q2:1024 * (q2 + 1)],
                             sqjunk[:, 0:1024], xbuf[:, tl % 2, 1024 * q2:1024 * (q2 + 1)],
                             inc=sv)
                        if q2 == 0:
                            P.wait("s", sv, sv.n)
                    xguard[tl % 2][1] = sv.n
                    y_done = sv.n
                    P.wait("y", sv, y_done)
                    P.dma("y", yscr[128 * tt:128 * (tt + 1), :], ao[:], sx)
                    if d_y is not None:
                        P.dma("y", d_y[128 * tt:128 * (tt + 1), :], ao[:], so)
                    ydma_prev = (sx.n, so.n)
                    ln_stats(ao[:])
                    P.op("s", "activation", ln2tm[:], ao[:], AF.Identity,
                         bias=stt[:, 6:7], scale=stt[:, 5:6], inc=ss)
                    P.wait("t", ss, ss.n)
                    for half in range(2):
                        P.wait("v", sx, l2dma_prev)
                        for i in range(8):
                            blk = 8 * half + i
                            P.op("t", "transpose", psT[:, 128 * i:128 * (i + 1)],
                                 ln2tm[:, 128 * blk:128 * (blk + 1)], id16[:],
                                 inc=(sp if i == 7 else None))
                        P.wait("v", sp, sp.n)
                        P.op("v", "tensor_copy", l2b[:],
                             psT[:, 0:1024].rearrange("p (c e) -> p c e", c=8), inc=sv)
                        P.wait("t", sv, sv.n)
                        P.wait("y", sv, sv.n)
                        P.dma("y", ln2scr[tt, 8 * half:8 * half + 8].rearrange("c p e -> p c e"),
                              l2b[:], sx)
                        l2dma_prev = sx.n
                        ln2rd[0] = sp.n
            P.flush()
        barrier()

        with contextlib.ExitStack() as fst:
            EF = fst.enter_context
            l2f = EF(nc.sbuf_tensor("l2f", [128, NDC, 512], BF16))
            hsp = EF(nc.sbuf_tensor("hsp", [128, NFC, 512], BF16))
            wg1 = EF(nc.sbuf_tensor("wg1", [128, 2, NDC, 512], BF16))
            wg2n = EF(nc.sbuf_tensor("wg2n", [128, 4, 1024], BF16))
            ysp = EF(nc.sbuf_tensor("ysp", [128, 2, D], F32))
            otile = EF(nc.sbuf_tensor("otile", [128, 2, D], F32))
            b2row = EF(nc.sbuf_tensor("b2rows", [128, D], F32))
            psB = [EF(nc.psum_tensor(f"psB{i}", [128, 1024], F32)) for i in range(4)]
            P.dma("s", b2row[:], b2row_i, sq)
            b2row_ld = sq.n
            ffs = wg1[:].rearrange("p a c e -> p (a c e)").bitcast(F32).rearrange(
                "p (t e) -> p t e", t=4)

            WQ = ("g", "y")
            WS = {"g": sg, "y": sx}

            a_used = {}
            b_used = {}
            outdma = {}
            ysp_guard = {}
            ffs_free = (0, 0)
            prev_a_last = 0
            span_psB_free = 0
            for sp_i in range(NSLOT):
                P.wait("s", sp, prev_a_last)
                for t_ in range(4):
                    P.dma("s", l2f[:, :, 128 * t_:128 * (t_ + 1)],
                          ln2scr[4 * sp_i + t_].rearrange("c p e -> p c e"), sq)
                l2f_ld = sq.n

                gelu_done = {}
                for g in range(16):
                    gb = g % 2
                    ka = 16 * sp_i + g
                    q_ = WQ[g % 2]
                    if ka >= 2:
                        P.wait(q_, sp, a_used[ka - 2])
                    if g < 2:
                        P.wait(q_, ss, ffs_free[0])
                        P.wait(q_, sv, ffs_free[1])
                    P.dma(q_, wg1[:, gb, :, :], w1q[:, :, 512 * g:512 * (g + 1)].rearrange(
                        "c p e -> p c e"), WS[q_])
                    g_ld = WS[q_].n
                    for fi in range(4):
                        fc = 4 * g + fi
                        pf = psB[fc % 4][:, 0:512]
                        if fi == 0:
                            P.wait("t", WS[q_], g_ld)
                        if fc == 0:
                            P.wait("t", sq, l2f_ld)
                            P.wait("t", sv, span_psB_free)
                        if fc >= 4:
                            P.wait("t", ss, gelu_done[fc - 4])
                        for dc in range(NDC):
                            P.op("t", "matmul", pf,
                                 wg1[:, gb, dc, 128 * fi:128 * (fi + 1)],
                                 l2f[:, dc, :], start=(dc == 0), stop=(dc == NDC - 1),
                                 inc=(sp if dc == NDC - 1 else None))
                        P.wait("s", sp, sp.n)
                        P.op("s", "activation", hsp[:, fc, :], pf, AF.Gelu,
                             bias=b1c[:, fc:fc + 1], inc=ss)
                        gelu_done[fc] = ss.n
                    a_used[ka] = sp.n
                a_done = ss.n
                prev_a_last = sp.n

                wg2ld = {}
                evac = {}

                def b_dma(i, sp_i=sp_i):
                    dh, fc = divmod(i, NFC)
                    q_ = WQ[i % 2]
                    if i >= 4:
                        P.wait(q_, sp, b_used[i - 4])
                    P.dma(q_, wg2n[:, i % 4, :],
                          w2q[fc, :, 1024 * dh:1024 * (dh + 1)], WS[q_])
                    wg2ld[i] = (WS[q_].n, WS[q_])

                def b_mms(i):
                    dh, fc = divmod(i, NFC)
                    n_, sem_ = wg2ld[i]
                    P.wait("t", sem_, n_)
                    if fc == 0:
                        P.wait("t", ss, a_done)
                    for tl in range(4):
                        if dh == 1 and fc == 0:
                            P.wait("t", sv, evac[(0, tl)])
                        lw = hsp[:, fc, 128 * tl:128 * (tl + 1)]
                        for dq in range(2):
                            P.op("t", "matmul", psB[tl][:, 512 * dq:512 * (dq + 1)],
                                 lw, wg2n[:, i % 4, 512 * dq:512 * (dq + 1)],
                                 start=(fc == 0), stop=(fc == NFC - 1),
                                 inc=(sp if (tl == 3 and dq == 1) else None))
                    b_used[i] = sp.n

                b_dma(0)
                b_dma(1)
                for i in range(2 * NFC):
                    if i + 2 < 2 * NFC:
                        b_dma(i + 2)
                    b_mms(i)
                    if i % NFC == NFC - 1:
                        dh = i // NFC
                        P.wait("v", sp, sp.n)
                        P.wait("v", sq, b2row_ld)
                        for tl in range(4):
                            P.op("v", "scalar_tensor_tensor",
                                 ffs[:, tl, 1024 * dh:1024 * (dh + 1)],
                                 psB[tl][:, 0:1024], 1.0,
                                 b2row[:, 1024 * dh:1024 * (dh + 1)], ALU.bypass,
                                 ALU.add, inc=sv)
                            evac[(dh, tl)] = sv.n
                span_psB_free = sv.n

                for tl in range(4):
                    tt = 4 * sp_i + tl
                    tb = tl % 2
                    par = tl % 2
                    if tt >= 2:
                        P.wait("s", sv, ysp_guard[tt - 2])
                    P.dma("s", ysp[:, tb, :], yscr[128 * tt:128 * (tt + 1), :], sq)
                    y_ld = sq.n
                    P.wait("v", sv, evac[(1, tl)])
                    P.wait("s", sv, evac[(1, tl)])
                    ln_stats(ffs[:, tl, :], par)
                    for q2 in range(2):
                        P.op("s", "activation", sqjunk[:, 0:1024],
                             ffs[:, tl, 1024 * q2:1024 * (q2 + 1)], AF.Identity,
                             bias=C(par, 6), scale=C(par, 5), inc=ss)
                        P.wait("v", ss, ss.n)
                        if q2 == 0:
                            P.wait("v", sq, y_ld)
                            if tt >= 2:
                                P.wait("v", so, outdma[tt - 2])
                        P.op("v", "tensor_add", otile[:, tb, 1024 * q2:1024 * (q2 + 1)],
                             sqjunk[:, 0:1024], ysp[:, tb, 1024 * q2:1024 * (q2 + 1)],
                             inc=sv)
                        if q2 == 0:
                            P.wait("s", sv, sv.n)
                    P.wait("y", sv, sv.n)
                    P.dma("y", out_t[128 * tt:128 * (tt + 1), :], otile[:, tb, :], so)
                    outdma[tt] = so.n
                    ysp_guard[tt] = sv.n
                ffs_free = (ss.n, sv.n)
            P.wait("y", so, so.n)
            P.flush()

    print(f"[mk] total instructions: {P.ninstr}")
    return nc



def make_inputs(L0_attn_x, L0_attn_Wqkv_w, L0_attn_Wqkv_b, L0_attn_out_w,
                L0_attn_out_b, L0_ffn_w1, L0_ffn_b1, L0_ffn_w2, L0_ffn_b2):
    f = np.float32
    wqkvT = np.ascontiguousarray(np.asarray(L0_attn_Wqkv_w, f).T)
    woutT = np.ascontiguousarray(np.asarray(L0_attn_out_w, f).T)
    w1T = np.ascontiguousarray(np.asarray(L0_ffn_w1, f).T)
    w2T = np.ascontiguousarray(np.asarray(L0_ffn_w2, f).T)
    qkvb = np.ascontiguousarray(np.asarray(L0_attn_Wqkv_b, f).reshape(18, 128).T)
    kb = np.ascontiguousarray(np.broadcast_to(np.asarray(L0_attn_Wqkv_b[2048:2176], f), (128, 128)))
    vb = np.ascontiguousarray(np.broadcast_to(np.asarray(L0_attn_Wqkv_b[2176:2304], f), (128, 128)))
    outb = np.ascontiguousarray(np.broadcast_to(
        np.asarray(L0_attn_out_b, f).astype(ml_dtypes.bfloat16), (128, D)))
    b1c = np.ascontiguousarray(np.asarray(L0_ffn_b1, f).reshape(NFC, 128).T)
    b2row = np.ascontiguousarray(np.broadcast_to(np.asarray(L0_ffn_b2, f), (128, D)))
    id16 = np.eye(128, dtype=ml_dtypes.bfloat16)
    idh = np.eye(128, dtype=np.float16)
    id32 = np.eye(128, dtype=f)
    onescol = np.ones((128, 4), f)
    onescolb = np.ones((128, 4), ml_dtypes.bfloat16)
    onescolh = np.ones((128, 4), np.float16)
    onesrow = np.ones((1, 128), f)
    onesrowh = np.ones((1, 128), ml_dtypes.bfloat16)
    iov = np.concatenate([np.arange(512, dtype=np.float16),
                          np.arange(512, dtype=np.float16) - np.float16(128.0)])
    iota = np.ascontiguousarray(np.broadcast_to(iov, (128, 1024)))

    x = np.asarray(L0_attn_x, f)
    in_maps = []
    layout = []
    for c in range(8):
        b, j = divmod(c, 4)
        plist = p_list_for(j)
        xo = np.ascontiguousarray(
            np.concatenate([x[b, 512 * p:512 * (p + 1), :] for p in plist], axis=0))
        cth = np.zeros((128, 64), np.float32)
        pvec = np.arange(128, dtype=np.float32)
        for s_i in range(4):
            Pm = plist[s_i]
            for cc in range(16 * s_i, 16 * (s_i + 1)):
                t = 128.0 * cc + pvec - 512.0 * Pm
                cth[:, cc] = np.clip(t, -640.0, 520.0)
        cth = cth.astype(np.float32)
        in_maps.append({
            "xseq": np.ascontiguousarray(x[b]),
            "xown": xo,
            "wqkvT": wqkvT, "woutT": woutT, "w1T": w1T, "w2T": w2T,
            "qkvb": qkvb, "kb": kb, "vb": vb, "outb": outb,
            "b1c": b1c, "b2row": b2row,
            "id16": id16, "idh": idh, "id32": id32,
            "onescol": onescol, "onescolb": onescolb, "onescolh": onescolh,
            "onesrow": onesrow, "onesrowh": onesrowh,
            "iota": iota, "cthr": cth,
        })
        layout.append((b, plist))
    return in_maps, layout


def kernel(L0_attn_x, L0_attn_Wqkv_w, L0_attn_Wqkv_b, L0_attn_out_w, L0_attn_out_b,
           L0_ffn_w1, L0_ffn_b1, L0_ffn_w2, L0_ffn_b2, debug=()):
    in_maps, layout = make_inputs(L0_attn_x, L0_attn_Wqkv_w, L0_attn_Wqkv_b,
                                  L0_attn_out_w, L0_attn_out_b, L0_ffn_w1,
                                  L0_ffn_b1, L0_ffn_w2, L0_ffn_b2)
    nc = build(debug=debug)
    res = run_bass_kernel_spmd(nc, in_maps, core_ids=list(range(8)))
    out = np.empty((2, S, D), np.float32)
    for c in range(8):
        b, plist = layout[c]
        o = res.results[c]["out"]
        for si, p in enumerate(plist):
            out[b, 512 * p:512 * (p + 1), :] = o[512 * si:512 * (si + 1), :]
    if debug:
        return out, res
    return out


# version 14 (final)
# speedup vs baseline: 1.2547x; 1.0089x over previous
import contextlib
import math
import numpy as np
import ml_dtypes
import concourse.bass as bass
import concourse.mybir as mybir
from concourse.bass_utils import run_bass_kernel_spmd

F32 = mybir.dt.float32
BF16 = mybir.dt.bfloat16
FP16 = mybir.dt.float16
AF = mybir.ActivationFunctionType
ALU = mybir.AluOpType
AX = mybir.AxisListType.X

D = 2048
S = 8192
HEADS = 16
DH = 128
DFF = 8192
NDC = D // 128
NFC = DFF // 128
NKC = S // 128
NT = 16
NSLOT = 4
SCALE = 1.0 / math.sqrt(DH)
EPS = 1e-5
CM_CONST = -0.5 * math.log(128.0) + math.log(1.05)

ENGINES = ("t", "v", "s", "y", "g")


class Sem:
    def __init__(self, stack, nc, name):
        self.h = stack.enter_context(nc.semaphore(name))
        self.n = 0

    def bump(self, k=1):
        self.n += k
        return self.n


class Prog:

    def __init__(self, nc):
        self.nc = nc
        self.ops = []
        self.ninstr = 0

    def op(self, eng, name, *args, inc=None, k=1, **kw):
        self.ops.append((eng, name, args, kw, inc, k))
        self.ninstr += 1
        if inc is not None:
            return inc.bump(k)
        return None

    def wait(self, eng, sem, n):
        if n > 0:
            self.ops.append((eng, "wait_ge", (sem.h, n), {}, None, 0))
            self.ninstr += 1

    def dma(self, eng, out, in_, inc, **kw):
        return self.op(eng, "dma_start", inc=inc, k=16, out=out, in_=in_, **kw)

    def flush(self):
        streams = {e: [] for e in ENGINES}
        for o in self.ops:
            streams[o[0]].append(o)
        self.ops = []

        def runner(engname):
            def run(e):
                for (_, name, args, kw, inc, k) in streams[engname]:
                    ins = getattr(e, name)(*args, **kw)
                    if inc is not None:
                        ins.then_inc(inc.h, k)
            return run

        with self.nc.Block() as block:
            for attr, key in (("tensor", "t"), ("vector", "v"), ("scalar", "s"),
                              ("sync", "y"), ("gpsimd", "g")):
                if streams[key]:
                    getattr(block, attr)(runner(key))


def p_list_for(j):
    return sorted({j, 7 - j, 8 + j, 15 - j})


def build(debug=()):
    nc = bass.Bass("TRN2", target_bir_lowering=False)
    dbg = set(debug)

    xseq = nc.dram_tensor("xseq", [S, D], F32, kind="ExternalInput").ap()
    xown = nc.dram_tensor("xown", [NT * 128, D], F32, kind="ExternalInput").ap()
    wqkvT = nc.dram_tensor("wqkvT", [D, 2304], F32, kind="ExternalInput").ap()
    woutT = nc.dram_tensor("woutT", [D, D], F32, kind="ExternalInput").ap()
    w1T = nc.dram_tensor("w1T", [D, DFF], F32, kind="ExternalInput").ap()
    w2T = nc.dram_tensor("w2T", [DFF, D], F32, kind="ExternalInput").ap()
    qkvb_i = nc.dram_tensor("qkvb", [128, 18], F32, kind="ExternalInput").ap()
    kb_i = nc.dram_tensor("kb", [128, 128], F32, kind="ExternalInput").ap()
    vb_i = nc.dram_tensor("vb", [128, 128], F32, kind="ExternalInput").ap()
    outb_i = nc.dram_tensor("outb", [128, D], BF16, kind="ExternalInput").ap()
    b1c_i = nc.dram_tensor("b1c", [128, NFC], F32, kind="ExternalInput").ap()
    b2row_i = nc.dram_tensor("b2row", [128, D], F32, kind="ExternalInput").ap()
    id16_i = nc.dram_tensor("id16", [128, 128], BF16, kind="ExternalInput").ap()
    idh_i = nc.dram_tensor("idh", [128, 128], FP16, kind="ExternalInput").ap()
    id32_i = nc.dram_tensor("id32", [128, 128], F32, kind="ExternalInput").ap()
    onescol_i = nc.dram_tensor("onescol", [128, 4], F32, kind="ExternalInput").ap()
    onescolb_i = nc.dram_tensor("onescolb", [128, 4], BF16, kind="ExternalInput").ap()
    onescolh_i = nc.dram_tensor("onescolh", [128, 4], FP16, kind="ExternalInput").ap()
    onesrow_i = nc.dram_tensor("onesrow", [1, 128], F32, kind="ExternalInput").ap()
    onesrowh_i = nc.dram_tensor("onesrowh", [1, 128], BF16, kind="ExternalInput").ap()
    iota_i = nc.dram_tensor("iota", [128, 1024], FP16, kind="ExternalInput").ap()
    cthr_i = nc.dram_tensor("cthr", [128, 64], F32, kind="ExternalInput").ap()

    yscr = nc.dram_tensor("yscr", [NT * 128, D], F32, kind="Internal").ap()
    ln2scr = nc.dram_tensor("ln2scr", [NT, NDC, 128, 128], BF16, kind="Internal").ap()
    w1q = nc.dram_tensor("w1q", [NDC, 128, DFF], BF16, kind="Internal").ap()
    w2q = nc.dram_tensor("w2q", [NFC, 128, D], BF16, kind="Internal").ap()

    out_t = nc.dram_tensor("out", [NT * 128, D], F32, kind="ExternalOutput").ap()
    dbg_t = {}

    def dbg_tensor(name, shape, dt=F32):
        if name in dbg:
            dbg_t[name] = nc.dram_tensor("dbg_" + name, shape, dt, kind="ExternalOutput").ap()
        return dbg_t.get(name)

    d_kfm = dbg_tensor("kfm", [128, NKC * 128])
    d_v = dbg_tensor("vst", [128, NKC * 128])
    d_qfm = dbg_tensor("qfm", [128, HEADS * 512])
    d_cfm = dbg_tensor("cfm", [128, HEADS * 512])
    d_wq1 = dbg_tensor("wq1", [128, DFF])
    d_thr = dbg_tensor("thr", [128, 8])
    d_ln1 = dbg_tensor("ln1t0", [128, NDC * 128])
    d_st = dbg_tensor("stat", [128, 16])
    d_y = dbg_tensor("y", [NT * 128, D])

    with contextlib.ExitStack() as gst:
        E = gst.enter_context
        sx = Sem(gst, nc, "sx")
        sg = Sem(gst, nc, "sg")
        sp = Sem(gst, nc, "sp")
        sv = Sem(gst, nc, "sv")
        ss = Sem(gst, nc, "ss")
        so = Sem(gst, nc, "so")
        sq = Sem(gst, nc, "sq")
        ALLSEMS = (sx, sg, sp, sv, ss, so, sq)

        id16 = E(nc.sbuf_tensor("id16s", [128, 128], BF16))
        idh = E(nc.sbuf_tensor("idhs", [128, 128], FP16))
        id32 = E(nc.sbuf_tensor("id32s", [128, 128], F32))
        onescol = E(nc.sbuf_tensor("onescols", [128, 4], F32))
        onescolb = E(nc.sbuf_tensor("onescolbs", [128, 4], BF16))
        onescolh = E(nc.sbuf_tensor("onescolhs", [128, 4], FP16))
        onesrow = E(nc.sbuf_tensor("onesrows", [1, 128], F32))
        onesrowh = E(nc.sbuf_tensor("onesrowhs", [1, 128], BF16))
        iota = E(nc.sbuf_tensor("iotas", [128, 1024], FP16))
        cthr = E(nc.sbuf_tensor("cthrs", [128, 64], F32))
        qkvb = E(nc.sbuf_tensor("qkvbs", [128, 18], F32))
        kb_s = E(nc.sbuf_tensor("kbs", [128, 128], F32))
        vb_s = E(nc.sbuf_tensor("vbs", [128, 128], F32))
        outb_s = E(nc.sbuf_tensor("outbs", [128, D], BF16))
        b1c = E(nc.sbuf_tensor("b1cs", [128, NFC], F32))
        cmt = E(nc.sbuf_tensor("cmt", [128, 1], F32))
        epst = E(nc.sbuf_tensor("epst", [128, 1], F32))
        negMb = E(nc.sbuf_tensor("negMb", [128, 1], F32))
        stt = E(nc.sbuf_tensor("stt", [128, 16], F32))
        kmax2 = E(nc.sbuf_tensor("kmax2", [1, 8], F32))
        qmax2 = E(nc.sbuf_tensor("qmax2", [1, 4], F32))
        sqjunk = E(nc.sbuf_tensor("sqjunk", [128, 1024], F32))
        thr1 = E(nc.sbuf_tensor("thr1", [128, 4], F32))
        absacc = E(nc.sbuf_tensor("absacc", [128, 2], F32))
        mtmp = E(nc.sbuf_tensor("mtmp", [128, 8], F32))

        P = Prog(nc)

        def barrier():
            for e in ENGINES:
                for sem in ALLSEMS:
                    P.wait(e, sem, sem.n)

        for dst, src in ((id16, id16_i), (idh, idh_i), (id32, id32_i),
                         (onescol, onescol_i), (onescolb, onescolb_i),
                         (onescolh, onescolh_i), (onesrow, onesrow_i),
                         (onesrowh, onesrowh_i), (iota, iota_i), (cthr, cthr_i),
                         (qkvb, qkvb_i), (kb_s, kb_i), (vb_s, vb_i),
                         (outb_s, outb_i), (b1c, b1c_i)):
            P.dma("y", dst[:], src, sx)
        P.op("v", "memset", cmt[:], CM_CONST, inc=sv)
        P.op("v", "memset", epst[:], EPS, inc=sv)
        consts_done = sx.n


        with contextlib.ExitStack() as ast_:
            EA = ast_.enter_context
            kfm = EA(nc.sbuf_tensor("kfm", [128, NKC, 128], BF16))
            vst = EA(nc.sbuf_tensor("vst", [128, NKC, 128], BF16))
            qfm = EA(nc.sbuf_tensor("qfm", [128, HEADS, 512], BF16))
            cfm = EA(nc.sbuf_tensor("cfm", [128, HEADS, 512], BF16))
            wbig = EA(nc.sbuf_tensor("wbig", [128, NDC, D], BF16))
            xbuf = EA(nc.sbuf_tensor("xbuf", [128, 2, D], F32))
            ln1own = EA(nc.sbuf_tensor("ln1own", [128, NDC, 512], BF16))
            pt = EA(nc.sbuf_tensor("pt", [128, 2, 1024], BF16))
            maskb = EA(nc.sbuf_tensor("maskb", [128, 1024], BF16))
            ktmp = EA(nc.sbuf_tensor("ktmp", [128, 128], BF16))
            rlrow = EA(nc.sbuf_tensor("rlrow", [1, 2, 512], F32))
            rl16 = EA(nc.sbuf_tensor("rl16", [1, 512], BF16))
            rlb = EA(nc.sbuf_tensor("rlb", [128, 512], F32))
            qsq = None
            ao = EA(nc.sbuf_tensor("ao", [128, D], F32))
            ln2tm = EA(nc.sbuf_tensor("ln2tm", [128, D], BF16))
            qsq = pt[:, 0, 0:512]
            ln1fm = ln1own[:, 8:12, :].rearrange("p c (a e) -> p (c a) e", a=4)
            l2b = maskb[:].rearrange("p (c e) -> p c e", c=8)
            psAB = [EA(nc.psum_tensor(f"psAB{i}", [128, 1024], F32)) for i in range(2)]
            psA = [psAB[0][:, 0:512], psAB[1][:, 0:512]]
            psCc = EA(nc.psum_tensor("psCc", [128, 512], F32))
            psL = EA(nc.psum_tensor("psL", [128, 512], F32))
            psS = EA(nc.psum_tensor("psS", [128, 512], F32))
            psT = EA(nc.psum_tensor("psT", [128, 1024], BF16))

            wkv_t = EA(nc.sbuf_tensor("wkv", [128, NDC, 256], BF16))
            wkv = wkv_t[:]
            P.dma("g", wkv, wqkvT[:, 2048:2304].rearrange("(c p) e -> p c e", p=128), sg)
            wkv_done = sg.n

            def C(par, i):
                return stt[:, 8 * par + i:8 * par + i + 1]

            def ln_stats1(src, par):
                P.op("v", "tensor_reduce", C(par, 0), src, AX, ALU.add, inc=sv)
                P.wait("s", sv, sv.n)
                P.op("s", "activation", sqjunk[:], src[:, 0:1024], AF.Square,
                     accum_out=C(par, 2), inc=ss)
                P.op("s", "activation", sqjunk[:], src[:, 1024:2048], AF.Square,
                     accum_out=C(par, 1), inc=ss)
                return ss.n

            def ln_stats2(par, s1done):
                P.wait("v", ss, s1done)
                P.op("v", "tensor_add", C(par, 1), C(par, 1), C(par, 2), inc=sv)
                P.op("v", "tensor_scalar", C(par, 3), C(par, 0), 1.0 / D, None, ALU.mult, inc=sv)
                P.wait("v", sv, sv.n)
                P.op("v", "tensor_mul", C(par, 4), C(par, 3), C(par, 3), inc=sv)
                P.op("v", "tensor_scalar", C(par, 2), C(par, 1), 1.0 / D, None, ALU.mult, inc=sv)
                P.wait("v", sv, sv.n)
                P.op("v", "tensor_sub", C(par, 4), C(par, 2), C(par, 4), inc=sv)
                P.wait("v", sv, sv.n)
                P.wait("s", sv, sv.n)
                P.op("s", "activation", C(par, 7), C(par, 4), AF.Ln, bias=epst[:, 0:1], inc=ss)
                P.op("s", "activation", C(par, 5), C(par, 7), AF.Exp, scale=-0.5, inc=ss)
                P.wait("v", ss, ss.n)
                P.op("v", "scalar_tensor_tensor", C(par, 6), C(par, 3), -1.0,
                     C(par, 5), ALU.mult, ALU.mult, inc=sv)
                P.wait("s", sv, sv.n)

            def ln_stats(src, par=0):
                d1 = ln_stats1(src, par)
                ln_stats2(par, d1)

            def norm_cast_transpose(src, dst_chunked, par=0):
                stg = (pt[:, 0, 0:512], pt[:, 0, 512:1024], pt[:, 1, 0:512], pt[:, 1, 512:1024])
                P.wait("s", sp, sp.n)
                for q4 in range(4):
                    P.op("s", "activation", stg[q4], src[:, 512 * q4:512 * (q4 + 1)],
                         AF.Identity, bias=C(par, 6), scale=C(par, 5), inc=ss)
                norm_done = ss.n
                P.wait("t", ss, norm_done)
                for half in range(2):
                    for i in range(8):
                        blk = 8 * half + i
                        q4, r = divmod(blk, 4)
                        P.op("t", "transpose", psT[:, 128 * i:128 * (i + 1)],
                             stg[q4][:, 128 * r:128 * (r + 1)], id16[:],
                             inc=(sp if i == 7 else None))
                    P.wait("v", sp, sp.n)
                    P.op("v", "tensor_copy",
                         dst_chunked[:, 8 * half:8 * half + 8, :],
                         psT[:, 0:1024].rearrange("p (c e) -> p c e", c=8), inc=sv)
                    P.wait("t", sv, sv.n)
                return norm_done

            wfa2 = cfm[:].bitcast(F32).rearrange("p c e -> p (c e)").rearrange(
                "p (s e) -> p s e", s=2)
            wfa3 = ao[:]
            l1B = ln1own[:, 0:8, :].rearrange("p c e -> p (c e)").rearrange(
                "p (s e) -> p s e", s=2)
            wqoB = l1B
            leb = ln2tm[:]

            def wslot(u):
                s_ = u % 3
                return wfa3 if s_ == 2 else wfa2[:, s_, :]

            def wsub_in(u):
                if u < 64:
                    i, s_ = divmod(u, 4)
                    return w1T[128 * i:128 * (i + 1), 2048 * s_:2048 * (s_ + 1)]
                return w2T[128 * (u - 64):128 * (u - 63), :]

            def wsub_out(u):
                if u < 64:
                    i, s_ = divmod(u, 4)
                    return w1q[i, :, 2048 * s_:2048 * (s_ + 1)]
                return w2q[u - 64, :, :]

            qload = {}
            quse = {}
            qdma = {}

            quse_s = {}

            def quant_load(L):
                u = L if L < 128 else L - 128
                if L >= 3:
                    P.wait("y", sv, quse.get(L - 3, 0))
                    P.wait("y", ss, quse_s.get(L - 3, 0))
                P.dma("y", wslot(L), wsub_in(u), sx)
                qload[L] = sx.n

            mean_vdone = {}

            def quant_mean_sub(u):
                p2 = (u % 2) * 2
                P.wait("s", sx, qload[u])
                if u >= 2:
                    P.wait("s", sv, mean_vdone[u - 2])
                P.op("s", "activation", sqjunk[:], wslot(u)[:, 0:1024], AF.Abs,
                     accum_out=mtmp[:, p2:p2 + 1], inc=ss)
                P.op("s", "activation", sqjunk[:], wslot(u)[:, 1024:2048], AF.Abs,
                     accum_out=mtmp[:, p2 + 1:p2 + 2], inc=ss)
                quse_s[u] = ss.n
                quse[u] = 0
                tgt = absacc[:, 0:1] if u < 64 else absacc[:, 1:2]
                P.wait("v", ss, ss.n)
                P.op("v", "tensor_add", mtmp[:, 4 + p2 // 2:5 + p2 // 2],
                     mtmp[:, p2:p2 + 1], mtmp[:, p2 + 1:p2 + 2], inc=sv)
                P.wait("v", sv, sv.n)
                if u == 0 or u == 64:
                    P.op("v", "tensor_copy", tgt, mtmp[:, 4 + p2 // 2:5 + p2 // 2], inc=sv)
                else:
                    P.op("v", "tensor_add", tgt, tgt, mtmp[:, 4 + p2 // 2:5 + p2 // 2], inc=sv)
                P.wait("v", sv, sv.n)
                mean_vdone[u] = sv.n

            def quant_thr():
                P.wait("t", sv, sv.n)
                P.op("t", "matmul", psL[0:1, 0:2], onescol[:, 0:1], absacc[:, 0:2],
                     start=True, stop=True, inc=sp)
                P.wait("v", sp, sp.n)
                P.op("v", "tensor_scalar", mtmp[0:1, 0:2], psL[0:1, 0:2],
                     0.7 / float(DFF * D), None, ALU.mult, inc=sv)
                P.wait("t", sv, sv.n)
                P.op("t", "matmul", psS[:, 0:2], onesrow[:], mtmp[0:1, 0:2],
                     start=True, stop=True, inc=sp)
                P.wait("v", sp, sp.n)
                P.op("v", "tensor_copy", thr1[:, 0:2], psS[:, 0:2], inc=sv)
                P.wait("v", sv, sv.n)
                P.op("v", "tensor_scalar", thr1[:, 2:4], thr1[:, 0:2], -1.0, None,
                     ALU.mult, inc=sv)
                P.wait("v", sv, sv.n)
                if d_thr is not None:
                    P.wait("g", sv, sv.n)
                    P.dma("g", d_thr[:, 0:4], thr1[:], so)
                    P.dma("g", d_thr[:, 4:6], absacc[:], so)

            def quant_q_sub(u):
                L = 128 + u
                tcol = 0 if u < 64 else 1
                P.wait("v", sx, qload[L])
                P.op("v", "tensor_scalar", leb, wslot(L),
                     thr1[:, 2 + tcol:3 + tcol], None, ALU.is_le, inc=sv)
                if u >= 2:
                    P.wait("v", sg, qdma[u - 2])
                P.wait("v", sv, sv.n)
                P.op("v", "scalar_tensor_tensor", wqoB[:, u % 2, :], wslot(L),
                     thr1[:, tcol:tcol + 1], leb, ALU.is_ge, ALU.subtract, inc=sv)
                quse[L] = sv.n
                quse_s[L] = 0
                P.wait("g", sv, quse[L])
                P.dma("g", wsub_out(u), wqoB[:, u % 2, :], sg)
                qdma[u] = sg.n

            P.wait("t", sg, wkv_done)
            P.wait("t", sx, consts_done)
            tile_done = [None] * NKC
            s1d = [None] * NKC

            xld = [None] * (NKC + 1)
            xguard = {0: [0, 0], 1: [0, 0]}

            def x_acquire(buf):
                P.wait("y", ss, xguard[buf][0])
                P.wait("y", sv, xguard[buf][1])

            def x_load(k):
                if k >= NKC:
                    return
                x_acquire(k % 2)
                P.dma("y", xbuf[:, k % 2, :], xseq[128 * k:128 * (k + 1), :], sx)
                xld[k] = sx.n

            def kv_part1(k):
                buf = k % 2
                P.wait("v", sx, xld[k])
                P.wait("s", sx, xld[k])
                s1d[k] = ln_stats1(xbuf[:, buf, :], k % 2)

            def kv_evac(k, pa):
                P.wait("v", sp, sp.n)
                P.op("v", "scalar_tensor_tensor", ktmp[:], pa[:, 0:128], 1.0, kb_s[:],
                     ALU.bypass, ALU.add, inc=sv)
                P.op("v", "scalar_tensor_tensor", vst[:, k, 0:128], pa[:, 128:256], 1.0,
                     vb_s[:], ALU.bypass, ALU.add, inc=sv)
                P.wait("t", sv, sv.n)
                P.op("t", "transpose", psT[:, 0:128], ktmp[:], id16[:], inc=sp)
                P.wait("v", sp, sp.n)
                P.op("v", "tensor_copy", kfm[:, k, :], psT[:, 0:128], inc=sv)
                P.wait("t", sv, sv.n)

            def kv_part2(k):
                buf = k % 2
                par = k % 2
                ln_stats2(par, s1d[k])
                tile_done[k] = norm_cast_transpose(xbuf[:, buf, :], ln1fm, par)
                xguard[buf][0] = tile_done[k]
                pa = psA[k % 2][:, 0:256]
                for dc in range(NDC):
                    P.op("t", "matmul", pa, ln1fm[:, dc, :], wkv[:, dc, :],
                         start=(dc == 0), stop=(dc == NDC - 1),
                         inc=(sp if dc == NDC - 1 else None))
                kv_evac(k, pa)

            psT32 = psT[:].bitcast(F32)

            ln2rd = [0]

            def kv_stage(k):
                par = k % 2
                ln_stats2(par, s1d[k])
                P.wait("s", sp, ln2rd[0])
                for q4 in range(4):
                    P.op("s", "activation", ln2tm[:, 512 * q4:512 * (q4 + 1)],
                         xbuf[:, k % 2, 512 * q4:512 * (q4 + 1)], AF.Identity,
                         bias=C(par, 6), scale=C(par, 5), inc=ss)
                tile_done[k] = ss.n
                xguard[k % 2][0] = ss.n

            def kv_finish(k):
                P.wait("t", ss, tile_done[k])
                for half in range(2):
                    for i in range(8):
                        blk = 8 * half + i
                        P.op("t", "transpose", psT[:, 128 * i:128 * (i + 1)],
                             ln2tm[:, 128 * blk:128 * (blk + 1)], id16[:],
                             inc=(sp if i == 7 else None))
                    P.wait("v", sp, sp.n)
                    P.op("v", "tensor_copy", ln1fm[:, 8 * half:8 * half + 8, :],
                         psT[:, 0:1024].rearrange("p (c e) -> p c e", c=8), inc=sv)
                    P.wait("t", sv, sv.n)
                ln2rd[0] = sp.n
                pa = psT32[:, 0:256]
                for dc in range(NDC):
                    P.op("t", "matmul", pa, ln1fm[:, dc, :], wkv[:, dc, :],
                         start=(dc == 0), stop=(dc == NDC - 1),
                         inc=(sp if dc == NDC - 1 else None))
                kv_evac(k, pa)

            def knorm_group(g):
                for half in range(2):
                    c0 = 16 * g + 8 * half
                    P.wait("s", sp, sp.n)
                    P.wait("s", sv, sv.n)
                    P.op("s", "activation", sqjunk[:, 0:512],
                         kfm[:, c0:c0 + 4, :].rearrange("p c e -> p (c e)"),
                         AF.Square, inc=ss)
                    P.op("s", "activation", sqjunk[:, 512:1024],
                         kfm[:, c0 + 4:c0 + 8, :].rearrange("p c e -> p (c e)"),
                         AF.Square, inc=ss)
                    P.wait("t", ss, ss.n)
                    P.op("t", "matmul", psS[0:1, :], onescol[:, 0:1], sqjunk[:, 0:512],
                         start=True, stop=False, skip_group_check=True)
                    P.op("t", "matmul", psS[0:1, :], onescol[:, 0:1], sqjunk[:, 512:1024],
                         start=False, stop=True, skip_group_check=True, inc=sp)
                    P.wait("v", sp, sp.n)
                    P.op("v", "tensor_reduce", kmax2[0:1, 1:2], psS[0:1, :], AX,
                         ALU.max, inc=sv)
                    P.wait("v", sv, sv.n)
                    if half == 0:
                        P.op("v", "tensor_copy", kmax2[0:1, 4 + g:5 + g],
                             kmax2[0:1, 1:2], inc=sv)
                    else:
                        P.op("v", "tensor_max", kmax2[0:1, 4 + g:5 + g],
                             kmax2[0:1, 4 + g:5 + g], kmax2[0:1, 1:2], inc=sv)
                    P.wait("v", sv, sv.n)
                    P.wait("t", sv, sv.n)

            x_load(0)
            quant_load(0)
            quant_load(1)
            next_load = [2]

            def quant_step(Lc):
                if next_load[0] < 128:
                    quant_load(next_load[0])
                    next_load[0] += 1
                quant_mean_sub(Lc)

            for k in range(NKC + 1):
                if k < NKC:
                    kv_part1(k)
                if k >= 1:
                    kv_part2(k - 1)
                if k < NKC:
                    x_load(k + 1)
                if k >= 1:
                    for Lc in range(2 * (k - 1), 2 * k):
                        quant_step(Lc)

            for e in ENGINES:
                P.wait(e, sx, sx.n)
                P.wait(e, sg, sg.n)
                P.wait(e, sv, sv.n)
                P.wait(e, ss, ss.n)
            quant_thr()
            for g_ in range(4):
                knorm_group(g_)
            kmax_done = sv.n

            regF = wbig[:].bitcast(F32).rearrange("p c e -> p (c e)")
            regB = wbig[:].rearrange("p c e -> p (c e)")
            wfa2n = regF[:, 8192:12288].rearrange("p (s e) -> p s e", s=2)
            wqoBn = regB[:, 24576:28672].rearrange("p (s e) -> p s e", s=2)
            lebn = regB[:, 28672:30720]
            p2_load = {}
            p2_use = {}
            p2_dma = {}
            p2_next = [0]
            p2_proc = [0]
            p2_sx = [0]
            p2_sv = [0]
            p2_gate = [None]

            def p2_load_one():
                u = p2_next[0]
                if u >= 128:
                    return
                if p2_gate[0] is not None:
                    P.wait("y", ss, p2_gate[0])
                    p2_gate[0] = None
                if u >= 2:
                    P.wait("y", sv, p2_use.get(u - 2, 0))
                P.dma("y", wfa2n[:, u % 2, :], wsub_in(u), sx)
                p2_load[u] = sx.n
                p2_sx[0] = sx.n
                p2_next[0] += 1

            def p2_step(n, limit):
                for _ in range(n):
                    while p2_next[0] < min(p2_proc[0] + 2, limit):
                        p2_load_one()
                    u = p2_proc[0]
                    if u >= limit:
                        return
                    tcol = 0 if u < 64 else 1
                    P.wait("v", sx, p2_load[u])
                    P.op("v", "tensor_scalar", lebn, wfa2n[:, u % 2, :],
                         thr1[:, 2 + tcol:3 + tcol], None, ALU.is_le, inc=sv)
                    if u >= 2:
                        P.wait("v", sg, p2_dma[u - 2])
                    P.wait("v", sv, sv.n)
                    P.op("v", "scalar_tensor_tensor", wqoBn[:, u % 2, :],
                         wfa2n[:, u % 2, :], thr1[:, tcol:tcol + 1], lebn,
                         ALU.is_ge, ALU.subtract, inc=sv)
                    p2_use[u] = sv.n
                    p2_sv[0] = sv.n
                    P.wait("g", sv, p2_use[u])
                    P.dma("g", wsub_out(u), wqoBn[:, u % 2, :], sg)
                    p2_dma[u] = sg.n
                    p2_proc[0] += 1

            P2Q = (
                [0] * HEADS,
                [3 if h_ < 4 else 2 for h_ in range(HEADS)],
                [3 if h_ < 14 else 2 for h_ in range(HEADS)],
                [3 if h_ < 14 else 2 for h_ in range(HEADS)],
            )

            ydma_prev = {0: (0, 0), 1: (0, 0)}
            l2dma_prev = [0]
            aoev_prev = {}
            ao2 = wkv_t[:].bitcast(F32).rearrange("p c e -> p (c e)")
            pend = [None]

            def ln2_flush():
                if pend[0] is None:
                    return
                tt_, ln2w_ = pend[0]
                pend[0] = None
                P.wait("t", ss, ln2w_)
                for half in range(2):
                    P.wait("v", sx, l2dma_prev[0])
                    for i_ in range(8):
                        blk = 8 * half + i_
                        P.op("t", "transpose", psT[:, 128 * i_:128 * (i_ + 1)],
                             ln2tm[:, 128 * blk:128 * (blk + 1)], id16[:],
                             inc=(sp if i_ == 7 else None))
                    P.wait("v", sp, sp.n)
                    P.op("v", "tensor_copy", l2b[:],
                         psT[:, 0:1024].rearrange("p (c e) -> p c e", c=8), inc=sv)
                    P.wait("t", sv, sv.n)
                    P.wait("y", sv, sv.n)
                    P.dma("y", ln2scr[tt_, 8 * half:8 * half + 8].rearrange("c p e -> p c e"),
                          l2b[:], sx)
                    l2dma_prev[0] = sx.n
                ln2rd[0] = sp.n
            for s_i in range(NSLOT):
                Es = 16 * (s_i + 1)
                P.wait("v", sx, sx.n)
                P.wait("g", sp, sp.n)
                P.dma("g", wbig[:], wqkvT[:, 0:2048].rearrange("(c p) e -> p c e", p=128), sg)
                wq_loaded = sg.n

                for tl in range(4):
                    tt = 4 * s_i + tl
                    buf = tl % 2
                    x_acquire(buf)
                    P.dma("y", xbuf[:, buf, :], xown[128 * tt:128 * (tt + 1), :], sx)
                    P.wait("v", sx, sx.n)
                    P.wait("s", sx, sx.n)
                    ln_stats(xbuf[:, buf, :], tl % 2)
                    norm_cast_transpose(
                        xbuf[:, buf, :],
                        ln1own[:, :, 128 * tl:128 * (tl + 1)], tl % 2)
                    xguard[buf][0] = ss.n
                P.wait("t", sg, wq_loaded)
                qev = {}
                for h in range(HEADS):
                    pa = psA[h % 2]
                    if h >= 2:
                        P.wait("t", ss, qev[h - 2])
                    for dc in range(NDC):
                        P.op("t", "matmul", pa[:, 0:512], wbig[:, dc, 128 * h:128 * (h + 1)],
                             ln1own[:, dc, :], start=(dc == 0), stop=(dc == NDC - 1),
                             inc=(sp if dc == NDC - 1 else None))
                    P.wait("s", sp, sp.n)
                    P.op("s", "activation", qfm[:, h, :],
                         pa[:, 0:512], AF.Identity, bias=qkvb[:, h:h + 1], inc=ss)
                    qev[h] = ss.n
                qproj_done = ss.n
                p2_gate[0] = qproj_done

                for h in range(HEADS):
                    P.wait("s", sp, sp.n)
                    if h == 0:
                        P.wait("s", ss, qproj_done)
                    P.op("s", "activation", qsq, qfm[:, h, :], AF.Square, inc=ss)
                    P.wait("t", ss, ss.n)
                    P.op("t", "matmul", psA[0][0:1, :], onescolb[:, 0:1], qsq,
                         start=True, stop=True, inc=sp)
                    P.wait("v", sp, sp.n)
                    P.op("v", "tensor_reduce", qmax2[0:1, 1:2], psA[0][0:1, :], AX,
                         ALU.max, inc=sv)
                    P.wait("v", sv, sv.n)
                    if h == 0:
                        P.op("v", "tensor_copy", qmax2[0:1, 0:1], qmax2[0:1, 1:2], inc=sv)
                    else:
                        P.op("v", "tensor_max", qmax2[0:1, 0:1], qmax2[0:1, 0:1],
                             qmax2[0:1, 1:2], inc=sv)
                    P.wait("v", sv, sv.n)
                    P.wait("t", sv, sv.n)
                P.op("v", "tensor_reduce", kmax2[0:1, 0:1], kmax2[0:1, 4:5 + s_i], AX,
                     ALU.max, inc=sv)
                P.wait("v", sv, sv.n)
                P.wait("s", sv, sv.n)
                P.op("s", "activation", qmax2[0:1, 2:3], qmax2[0:1, 0:1], AF.Ln, inc=ss)
                P.op("s", "activation", kmax2[0:1, 2:3], kmax2[0:1, 0:1], AF.Ln, inc=ss)
                P.wait("v", ss, ss.n)
                P.op("v", "tensor_add", qmax2[0:1, 3:4], qmax2[0:1, 2:3],
                     kmax2[0:1, 2:3], inc=sv)
                P.wait("v", sv, sv.n)
                P.wait("s", sv, sv.n)
                P.op("s", "activation", qmax2[0:1, 1:2], qmax2[0:1, 3:4], AF.Exp,
                     scale=0.5, bias=cmt[0:1, 0:1], inc=ss)
                P.wait("v", ss, ss.n)
                P.op("v", "tensor_scalar", qmax2[0:1, 1:2], qmax2[0:1, 1:2], 1.0, None,
                     ALU.add, inc=sv)
                P.wait("v", sv, sv.n)
                P.op("v", "tensor_scalar", qmax2[0:1, 1:2], qmax2[0:1, 1:2], -1.0, None,
                     ALU.mult, inc=sv)
                P.wait("v", sv, sv.n)
                P.wait("t", sv, sv.n)
                P.op("t", "matmul", psA[0][:, 0:1], onesrow[:], qmax2[0:1, 1:2],
                     start=True, stop=True, inc=sp)
                P.wait("v", sp, sp.n)
                P.wait("v", ss, ss.n)
                P.op("v", "tensor_copy", negMb[:], psA[0][:, 0:1], inc=sv)
                negm_done = sv.n
                P.wait("s", sv, negm_done)
                P.wait("t", sv, negm_done)

                ctx_ev_done = {}
                sc_ev_done = {}
                p2_limit = sum(sum(P2Q[s2]) for s2 in range(s_i + 1))
                for h in range(HEADS):
                    p2_step(P2Q[s_i][h], p2_limit)
                    exp_done = {}
                    ready = {}
                    rsem = {}
                    ctx_done = {}

                    def emit_ctx(cp, h=h):
                        if cp == 0 and h >= 1:
                            P.wait("t", sv, ctx_ev_done[h - 1])
                        P.wait("t", rsem[cp], ready[cp])
                        pb = (cp // 2) % 2
                        psrc = pt
                        P.op("t", "matmul", psCc[:], vst[:, cp, 0:128],
                             psrc[:, pb, 0:512], start=(cp == 0), stop=False,
                             skip_group_check=True)
                        P.op("t", "matmul", psCc[:], vst[:, cp + 1, 0:128],
                             psrc[:, pb, 512:1024], start=False, stop=(cp == Es - 2),
                             skip_group_check=True)
                        P.op("t", "matmul", psL[0:1, :], onescolb[:, 0:1],
                             psrc[:, pb, 0:512], start=(cp == 0), stop=False,
                             skip_group_check=True)
                        P.op("t", "matmul", psL[0:1, :], onescolb[:, 0:1],
                             psrc[:, pb, 512:1024], start=False, stop=(cp == Es - 2),
                             skip_group_check=True, inc=sp)
                        ctx_done[cp] = sp.n

                    for cp in range(0, Es, 2):
                        pb = (cp // 2) % 2
                        pa = psAB[pb]
                        if cp >= 4:
                            P.wait("t", ss, exp_done[cp - 4])
                        P.op("t", "matmul", pa[:, 0:512], kfm[:, cp, :], qfm[:, h, :],
                             start=True, stop=True)
                        P.op("t", "matmul", pa[:, 512:1024], kfm[:, cp + 1, :], qfm[:, h, :],
                             start=True, stop=True, inc=sp)
                        sc_mm = sp.n
                        P.wait("s", sp, sc_mm)
                        if cp >= 4:
                            P.wait("s", sp, ctx_done[cp - 4])
                        P.op("s", "activation", pt[:, pb, :], pa[:, 0:1024], AF.Exp,
                             bias=negMb[:, 0:1], scale=SCALE, inc=ss)
                        exp_done[cp] = ss.n
                        if cp >= 16 * s_i:
                            P.op("v", "tensor_scalar", maskb[:], iota[:],
                                 cthr[:, cp:cp + 1], None, ALU.is_ge, inc=sv)
                            P.wait("v", sv, sv.n)
                            P.wait("v", ss, exp_done[cp])
                            P.op("v", "tensor_mul", pt[:, pb, :], pt[:, pb, :],
                                 maskb[:], inc=sv)
                            ready[cp], rsem[cp] = sv.n, sv
                        else:
                            ready[cp], rsem[cp] = exp_done[cp], ss
                        if cp >= 2:
                            emit_ctx(cp - 2)
                    emit_ctx(Es - 2)
                    P.wait("v", sp, ctx_done[Es - 2])
                    P.op("v", "reciprocal", rlrow[0:1, h % 2, :], psL[0:1, :], inc=sv)
                    P.wait("v", sv, sv.n)
                    P.op("v", "tensor_copy", rl16[0:1, :], rlrow[0:1, h % 2, :], inc=sv)
                    P.wait("t", sv, sv.n)
                    if h >= 1:
                        P.wait("t", ss, sc_ev_done[h - 1])
                    P.op("t", "matmul", psS[:], onesrowh[:], rl16[0:1, :],
                         start=True, stop=True, inc=sp)
                    P.wait("s", sp, sp.n)
                    P.op("s", "activation", rlb[:], psS[:], AF.Identity, inc=ss)
                    sc_ev_done[h] = ss.n
                    P.wait("v", ss, sc_ev_done[h])
                    P.op("v", "tensor_mul", cfm[:, h, :], psCc[:], rlb[:], inc=sv)
                    ctx_ev_done[h] = sv.n
                attn_done = sv.n

                if s_i == 0 and d_qfm is not None:
                    P.wait("g", ss, qproj_done)
                    P.dma("g", d_qfm, qfm[:].rearrange("p h q -> p (h q)"), so)
                if s_i == 0 and d_cfm is not None:
                    P.wait("g", sv, attn_done)
                    P.dma("g", d_cfm, cfm[:].rearrange("p h q -> p (h q)"), so)

                P.wait("g", ss, qproj_done)
                P.wait("g", sx, p2_sx[0])
                P.wait("g", sv, p2_sv[0])
                P.dma("g", wbig[:], woutT.rearrange("(c p) e -> p c e", p=128), sg)
                wo_loaded = sg.n
                P.wait("t", sg, wo_loaded)
                P.wait("t", sv, attn_done)
                for tl in range(4):
                    tt = 4 * s_i + tl
                    tb = tl % 2
                    aop = ao if tb == 0 else ao2
                    x_acquire(tb)
                    P.dma("y", xbuf[:, tb, :], xown[128 * tt:128 * (tt + 1), :], sx)
                    xo_ld = sx.n
                    aoev = {}
                    for dspan in range(4):
                        pa = psA[dspan % 2]
                        if dspan >= 2:
                            P.wait("t", sv, aoev[dspan - 2])
                        elif tl > 0:
                            P.wait("t", sv, aoev_prev[dspan + 2])
                        for ec in range(NDC):
                            P.op("t", "matmul", pa[:], cfm[:, ec, 128 * tl:128 * (tl + 1)],
                                 wbig[:, ec, 512 * dspan:512 * (dspan + 1)],
                                 start=(ec == 0), stop=(ec == NDC - 1),
                                 inc=(sp if ec == NDC - 1 else None))
                        P.wait("v", sp, sp.n)
                        if dspan == 0:
                            P.wait("v", sx, ydma_prev[tb][0])
                            P.wait("v", so, ydma_prev[tb][1])
                        P.op("v", "scalar_tensor_tensor", aop[:, 512 * dspan:512 * (dspan + 1)],
                             pa[:, 0:512], 1.0, outb_s[:, 512 * dspan:512 * (dspan + 1)],
                             ALU.bypass, ALU.add, inc=sv)
                        aoev[dspan] = sv.n
                    aoev_prev = aoev
                    ln2_flush()
                    ln_stats(aop[:])
                    for q2 in range(2):
                        P.op("s", "activation", sqjunk[:, 0:1024],
                             aop[:, 1024 * q2:1024 * (q2 + 1)], AF.Identity,
                             bias=stt[:, 6:7], scale=stt[:, 5:6], inc=ss)
                        P.wait("v", ss, ss.n)
                        if q2 == 0:
                            P.wait("v", sx, xo_ld)
                        P.op("v", "tensor_add", aop[:, 1024 * q2:1024 * (q2 + 1)],
                             sqjunk[:, 0:1024], xbuf[:, tb, 1024 * q2:1024 * (q2 + 1)],
                             inc=sv)
                        if q2 == 0:
                            P.wait("s", sv, sv.n)
                    xguard[tb][1] = sv.n
                    y_done = sv.n
                    P.wait("y", sv, y_done)
                    P.dma("y", yscr[128 * tt:128 * (tt + 1), :], aop[:], sx)
                    if d_y is not None:
                        P.dma("y", d_y[128 * tt:128 * (tt + 1), :], aop[:], so)
                    ydma_prev[tb] = (sx.n, so.n)
                    ln_stats(aop[:])
                    P.wait("s", sp, ln2rd[0])
                    P.op("s", "activation", ln2tm[:], aop[:], AF.Identity,
                         bias=stt[:, 6:7], scale=stt[:, 5:6], inc=ss)
                    pend[0] = (tt, ss.n)
                ln2_flush()
            P.flush()
        barrier()

        with contextlib.ExitStack() as fst:
            EF = fst.enter_context
            l2f = EF(nc.sbuf_tensor("l2f", [128, NDC, 512], BF16))
            hsp = EF(nc.sbuf_tensor("hsp", [128, NFC, 512], BF16))
            wg1 = EF(nc.sbuf_tensor("wg1", [128, 2, NDC, 512], BF16))
            wg2n = EF(nc.sbuf_tensor("wg2n", [128, 4, 1024], BF16))
            ysp = EF(nc.sbuf_tensor("ysp", [128, 2, D], F32))
            otile = EF(nc.sbuf_tensor("otile", [128, 2, D], F32))
            b2row = EF(nc.sbuf_tensor("b2rows", [128, D], F32))
            ffs = EF(nc.sbuf_tensor("ffs", [128, 4, 2048], F32))
            psB = [EF(nc.psum_tensor(f"psB{i}", [128, 1024], F32)) for i in range(4)]
            P.dma("s", b2row[:], b2row_i, sq)
            b2row_ld = sq.n

            WQ = ("g", "y")
            WS = {"g": sg, "y": sx}

            a_used = {}
            b_used = {}
            outdma = {}
            ysp_guard = {}
            ffs_free = (0, 0)
            prev_a_last = 0
            span_psB_free = 0
            for sp_i in range(NSLOT):
                P.wait("s", sp, prev_a_last)
                for t_ in range(4):
                    P.dma("s", l2f[:, :, 128 * t_:128 * (t_ + 1)],
                          ln2scr[4 * sp_i + t_].rearrange("c p e -> p c e"), sq)
                l2f_ld = sq.n

                gelu_done = {}
                for g in range(16):
                    gb = g % 2
                    ka = 16 * sp_i + g
                    q_ = WQ[g % 2]
                    if ka >= 2:
                        P.wait(q_, sp, a_used[ka - 2])
                    P.dma(q_, wg1[:, gb, :, :], w1q[:, :, 512 * g:512 * (g + 1)].rearrange(
                        "c p e -> p c e"), WS[q_])
                    g_ld = WS[q_].n
                    for fi in range(4):
                        fc = 4 * g + fi
                        pf = psB[fc % 4][:, 0:512]
                        if fi == 0:
                            P.wait("t", WS[q_], g_ld)
                        if fc == 0:
                            P.wait("t", sq, l2f_ld)
                            P.wait("t", sv, span_psB_free)
                        if fc >= 4:
                            P.wait("t", ss, gelu_done[fc - 4])
                        for dc in range(NDC):
                            P.op("t", "matmul", pf,
                                 wg1[:, gb, dc, 128 * fi:128 * (fi + 1)],
                                 l2f[:, dc, :], start=(dc == 0), stop=(dc == NDC - 1),
                                 inc=(sp if dc == NDC - 1 else None))
                        P.wait("s", sp, sp.n)
                        P.op("s", "activation", hsp[:, fc, :], pf, AF.Gelu,
                             bias=b1c[:, fc:fc + 1], inc=ss)
                        gelu_done[fc] = ss.n
                    a_used[ka] = sp.n
                a_done = ss.n
                prev_a_last = sp.n

                wg2ld = {}
                evac = {}

                def b_dma(i, sp_i=sp_i):
                    dh, fc = divmod(i, NFC)
                    q_ = WQ[i % 2]
                    if i >= 4:
                        P.wait(q_, sp, b_used[i - 4])
                    P.dma(q_, wg2n[:, i % 4, :],
                          w2q[fc, :, 1024 * dh:1024 * (dh + 1)], WS[q_])
                    wg2ld[i] = (WS[q_].n, WS[q_])

                def b_mms(i):
                    dh, fc = divmod(i, NFC)
                    n_, sem_ = wg2ld[i]
                    P.wait("t", sem_, n_)
                    if fc == 0:
                        P.wait("t", ss, a_done)
                    for tl in range(4):
                        if dh == 1 and fc == 0:
                            P.wait("t", sv, evac[(0, tl)])
                        lw = hsp[:, fc, 128 * tl:128 * (tl + 1)]
                        for dq in range(2):
                            P.op("t", "matmul", psB[tl][:, 512 * dq:512 * (dq + 1)],
                                 lw, wg2n[:, i % 4, 512 * dq:512 * (dq + 1)],
                                 start=(fc == 0), stop=(fc == NFC - 1),
                                 inc=(sp if (tl == 3 and dq == 1) else None))
                    b_used[i] = sp.n

                b_dma(0)
                b_dma(1)
                for i in range(2 * NFC):
                    if i + 2 < 2 * NFC:
                        b_dma(i + 2)
                    b_mms(i)
                    if i % NFC == NFC - 1:
                        dh = i // NFC
                        P.wait("v", sp, sp.n)
                        P.wait("v", sq, b2row_ld)
                        for tl in range(4):
                            P.op("v", "scalar_tensor_tensor",
                                 ffs[:, tl, 1024 * dh:1024 * (dh + 1)],
                                 psB[tl][:, 0:1024], 1.0,
                                 b2row[:, 1024 * dh:1024 * (dh + 1)], ALU.bypass,
                                 ALU.add, inc=sv)
                            evac[(dh, tl)] = sv.n
                span_psB_free = sv.n

                for tl in range(4):
                    tt = 4 * sp_i + tl
                    tb = tl % 2
                    par = tl % 2
                    if tt >= 2:
                        P.wait("s", sv, ysp_guard[tt - 2])
                    P.dma("s", ysp[:, tb, :], yscr[128 * tt:128 * (tt + 1), :], sq)
                    y_ld = sq.n
                    P.wait("v", sv, evac[(1, tl)])
                    P.wait("s", sv, evac[(1, tl)])
                    ln_stats(ffs[:, tl, :], par)
                    for q2 in range(2):
                        P.op("s", "activation", sqjunk[:, 0:1024],
                             ffs[:, tl, 1024 * q2:1024 * (q2 + 1)], AF.Identity,
                             bias=C(par, 6), scale=C(par, 5), inc=ss)
                        P.wait("v", ss, ss.n)
                        if q2 == 0:
                            P.wait("v", sq, y_ld)
                            if tt >= 2:
                                P.wait("v", so, outdma[tt - 2])
                        P.op("v", "tensor_add", otile[:, tb, 1024 * q2:1024 * (q2 + 1)],
                             sqjunk[:, 0:1024], ysp[:, tb, 1024 * q2:1024 * (q2 + 1)],
                             inc=sv)
                        if q2 == 0:
                            P.wait("s", sv, sv.n)
                    P.wait("y", sv, sv.n)
                    P.dma("y", out_t[128 * tt:128 * (tt + 1), :], otile[:, tb, :], so)
                    outdma[tt] = so.n
                    ysp_guard[tt] = sv.n
                ffs_free = (ss.n, sv.n)
            P.wait("y", so, so.n)
            P.flush()

    print(f"[mk] total instructions: {P.ninstr}")
    return nc



def make_inputs(L0_attn_x, L0_attn_Wqkv_w, L0_attn_Wqkv_b, L0_attn_out_w,
                L0_attn_out_b, L0_ffn_w1, L0_ffn_b1, L0_ffn_w2, L0_ffn_b2):
    f = np.float32
    wqkvT = np.ascontiguousarray(np.asarray(L0_attn_Wqkv_w, f).T)
    woutT = np.ascontiguousarray(np.asarray(L0_attn_out_w, f).T)
    w1T = np.ascontiguousarray(np.asarray(L0_ffn_w1, f).T)
    w2T = np.ascontiguousarray(np.asarray(L0_ffn_w2, f).T)
    qkvb = np.ascontiguousarray(np.asarray(L0_attn_Wqkv_b, f).reshape(18, 128).T)
    kb = np.ascontiguousarray(np.broadcast_to(np.asarray(L0_attn_Wqkv_b[2048:2176], f), (128, 128)))
    vb = np.ascontiguousarray(np.broadcast_to(np.asarray(L0_attn_Wqkv_b[2176:2304], f), (128, 128)))
    outb = np.ascontiguousarray(np.broadcast_to(
        np.asarray(L0_attn_out_b, f).astype(ml_dtypes.bfloat16), (128, D)))
    b1c = np.ascontiguousarray(np.asarray(L0_ffn_b1, f).reshape(NFC, 128).T)
    b2row = np.ascontiguousarray(np.broadcast_to(np.asarray(L0_ffn_b2, f), (128, D)))
    id16 = np.eye(128, dtype=ml_dtypes.bfloat16)
    idh = np.eye(128, dtype=np.float16)
    id32 = np.eye(128, dtype=f)
    onescol = np.ones((128, 4), f)
    onescolb = np.ones((128, 4), ml_dtypes.bfloat16)
    onescolh = np.ones((128, 4), np.float16)
    onesrow = np.ones((1, 128), f)
    onesrowh = np.ones((1, 128), ml_dtypes.bfloat16)
    iov = np.concatenate([np.arange(512, dtype=np.float16),
                          np.arange(512, dtype=np.float16) - np.float16(128.0)])
    iota = np.ascontiguousarray(np.broadcast_to(iov, (128, 1024)))

    x = np.asarray(L0_attn_x, f)
    in_maps = []
    layout = []
    for c in range(8):
        b, j = divmod(c, 4)
        plist = p_list_for(j)
        xo = np.ascontiguousarray(
            np.concatenate([x[b, 512 * p:512 * (p + 1), :] for p in plist], axis=0))
        cth = np.zeros((128, 64), np.float32)
        pvec = np.arange(128, dtype=np.float32)
        for s_i in range(4):
            Pm = plist[s_i]
            for cc in range(16 * s_i, 16 * (s_i + 1)):
                t = 128.0 * cc + pvec - 512.0 * Pm
                cth[:, cc] = np.clip(t, -640.0, 520.0)
        cth = cth.astype(np.float32)
        in_maps.append({
            "xseq": np.ascontiguousarray(x[b]),
            "xown": xo,
            "wqkvT": wqkvT, "woutT": woutT, "w1T": w1T, "w2T": w2T,
            "qkvb": qkvb, "kb": kb, "vb": vb, "outb": outb,
            "b1c": b1c, "b2row": b2row,
            "id16": id16, "idh": idh, "id32": id32,
            "onescol": onescol, "onescolb": onescolb, "onescolh": onescolh,
            "onesrow": onesrow, "onesrowh": onesrowh,
            "iota": iota, "cthr": cth,
        })
        layout.append((b, plist))
    return in_maps, layout


def kernel(L0_attn_x, L0_attn_Wqkv_w, L0_attn_Wqkv_b, L0_attn_out_w, L0_attn_out_b,
           L0_ffn_w1, L0_ffn_b1, L0_ffn_w2, L0_ffn_b2, debug=()):
    in_maps, layout = make_inputs(L0_attn_x, L0_attn_Wqkv_w, L0_attn_Wqkv_b,
                                  L0_attn_out_w, L0_attn_out_b, L0_ffn_w1,
                                  L0_ffn_b1, L0_ffn_w2, L0_ffn_b2)
    nc = build(debug=debug)
    res = run_bass_kernel_spmd(nc, in_maps, core_ids=list(range(8)))
    out = np.empty((2, S, D), np.float32)
    for c in range(8):
        b, plist = layout[c]
        o = res.results[c]["out"]
        for si, p in enumerate(plist):
            out[b, 512 * p:512 * (p + 1), :] = o[512 * si:512 * (si + 1), :]
    if debug:
        return out, res
    return out


# version 15
# speedup vs baseline: 1.2672x; 1.0100x over previous
import contextlib
import math
import numpy as np
import ml_dtypes
import concourse.bass as bass
import concourse.mybir as mybir
from concourse.bass_utils import run_bass_kernel_spmd

F32 = mybir.dt.float32
BF16 = mybir.dt.bfloat16
FP16 = mybir.dt.float16
AF = mybir.ActivationFunctionType
ALU = mybir.AluOpType
AX = mybir.AxisListType.X

D = 2048
S = 8192
HEADS = 16
DH = 128
DFF = 8192
NDC = D // 128
NFC = DFF // 128
NKC = S // 128
NT = 16
NSLOT = 4
SCALE = 1.0 / math.sqrt(DH)
EPS = 1e-5
CM_CONST = -0.5 * math.log(128.0) + math.log(1.05)

ENGINES = ("t", "v", "s", "y", "g")


class Sem:
    def __init__(self, stack, nc, name):
        self.h = stack.enter_context(nc.semaphore(name))
        self.n = 0

    def bump(self, k=1):
        self.n += k
        return self.n


class Prog:

    def __init__(self, nc):
        self.nc = nc
        self.ops = []
        self.ninstr = 0

    def op(self, eng, name, *args, inc=None, k=1, **kw):
        self.ops.append((eng, name, args, kw, inc, k))
        self.ninstr += 1
        if inc is not None:
            return inc.bump(k)
        return None

    def wait(self, eng, sem, n):
        if n > 0:
            self.ops.append((eng, "wait_ge", (sem.h, n), {}, None, 0))
            self.ninstr += 1

    def dma(self, eng, out, in_, inc, **kw):
        return self.op(eng, "dma_start", inc=inc, k=16, out=out, in_=in_, **kw)

    def flush(self):
        streams = {e: [] for e in ENGINES}
        for o in self.ops:
            streams[o[0]].append(o)
        self.ops = []

        def runner(engname):
            def run(e):
                for (_, name, args, kw, inc, k) in streams[engname]:
                    ins = getattr(e, name)(*args, **kw)
                    if inc is not None:
                        ins.then_inc(inc.h, k)
            return run

        with self.nc.Block() as block:
            for attr, key in (("tensor", "t"), ("vector", "v"), ("scalar", "s"),
                              ("sync", "y"), ("gpsimd", "g")):
                if streams[key]:
                    getattr(block, attr)(runner(key))


def p_list_for(j):
    return sorted({j, 7 - j, 8 + j, 15 - j})


def build(debug=()):
    nc = bass.Bass("TRN2", target_bir_lowering=False)
    dbg = set(debug)

    xseq = nc.dram_tensor("xseq", [S, D], F32, kind="ExternalInput").ap()
    xown = nc.dram_tensor("xown", [NT * 128, D], F32, kind="ExternalInput").ap()
    wqkvT = nc.dram_tensor("wqkvT", [D, 2304], F32, kind="ExternalInput").ap()
    woutT = nc.dram_tensor("woutT", [D, D], F32, kind="ExternalInput").ap()
    w1T = nc.dram_tensor("w1T", [D, DFF], F32, kind="ExternalInput").ap()
    w2T = nc.dram_tensor("w2T", [DFF, D], F32, kind="ExternalInput").ap()
    qkvb_i = nc.dram_tensor("qkvb", [128, 18], F32, kind="ExternalInput").ap()
    kb_i = nc.dram_tensor("kb", [128, 128], F32, kind="ExternalInput").ap()
    vb_i = nc.dram_tensor("vb", [128, 128], F32, kind="ExternalInput").ap()
    outb_i = nc.dram_tensor("outb", [128, D], BF16, kind="ExternalInput").ap()
    b1c_i = nc.dram_tensor("b1c", [128, NFC], F32, kind="ExternalInput").ap()
    b2row_i = nc.dram_tensor("b2row", [128, D], F32, kind="ExternalInput").ap()
    id16_i = nc.dram_tensor("id16", [128, 128], BF16, kind="ExternalInput").ap()
    idh_i = nc.dram_tensor("idh", [128, 128], FP16, kind="ExternalInput").ap()
    id32_i = nc.dram_tensor("id32", [128, 128], F32, kind="ExternalInput").ap()
    onescol_i = nc.dram_tensor("onescol", [128, 4], F32, kind="ExternalInput").ap()
    onescolb_i = nc.dram_tensor("onescolb", [128, 4], BF16, kind="ExternalInput").ap()
    onescolh_i = nc.dram_tensor("onescolh", [128, 4], FP16, kind="ExternalInput").ap()
    onesrow_i = nc.dram_tensor("onesrow", [1, 128], F32, kind="ExternalInput").ap()
    onesrowh_i = nc.dram_tensor("onesrowh", [1, 128], BF16, kind="ExternalInput").ap()
    iota_i = nc.dram_tensor("iota", [128, 1024], FP16, kind="ExternalInput").ap()
    cthr_i = nc.dram_tensor("cthr", [128, 64], F32, kind="ExternalInput").ap()

    yscr = nc.dram_tensor("yscr", [NT * 128, D], F32, kind="Internal").ap()
    ln2scr = nc.dram_tensor("ln2scr", [NT, NDC, 128, 128], BF16, kind="Internal").ap()
    w1q = nc.dram_tensor("w1q", [NDC, 128, DFF], BF16, kind="Internal").ap()
    w2q = nc.dram_tensor("w2q", [NFC, 128, D], BF16, kind="Internal").ap()

    out_t = nc.dram_tensor("out", [NT * 128, D], F32, kind="ExternalOutput").ap()
    dbg_t = {}

    def dbg_tensor(name, shape, dt=F32):
        if name in dbg:
            dbg_t[name] = nc.dram_tensor("dbg_" + name, shape, dt, kind="ExternalOutput").ap()
        return dbg_t.get(name)

    d_kfm = dbg_tensor("kfm", [128, NKC * 128])
    d_v = dbg_tensor("vst", [128, NKC * 128])
    d_qfm = dbg_tensor("qfm", [128, HEADS * 512])
    d_cfm = dbg_tensor("cfm", [128, HEADS * 512])
    d_wq1 = dbg_tensor("wq1", [128, DFF])
    d_thr = dbg_tensor("thr", [128, 8])
    d_ln1 = dbg_tensor("ln1t0", [128, NDC * 128])
    d_st = dbg_tensor("stat", [128, 16])
    d_y = dbg_tensor("y", [NT * 128, D])

    with contextlib.ExitStack() as gst:
        E = gst.enter_context
        sx = Sem(gst, nc, "sx")
        sg = Sem(gst, nc, "sg")
        sp = Sem(gst, nc, "sp")
        sv = Sem(gst, nc, "sv")
        ss = Sem(gst, nc, "ss")
        so = Sem(gst, nc, "so")
        sq = Sem(gst, nc, "sq")
        ALLSEMS = (sx, sg, sp, sv, ss, so, sq)

        id16 = E(nc.sbuf_tensor("id16s", [128, 128], BF16))
        idh = E(nc.sbuf_tensor("idhs", [128, 128], FP16))
        id32 = E(nc.sbuf_tensor("id32s", [128, 128], F32))
        onescol = E(nc.sbuf_tensor("onescols", [128, 4], F32))
        onescolb = E(nc.sbuf_tensor("onescolbs", [128, 4], BF16))
        onescolh = E(nc.sbuf_tensor("onescolhs", [128, 4], FP16))
        onesrow = E(nc.sbuf_tensor("onesrows", [1, 128], F32))
        onesrowh = E(nc.sbuf_tensor("onesrowhs", [1, 128], BF16))
        iota = E(nc.sbuf_tensor("iotas", [128, 1024], FP16))
        cthr = E(nc.sbuf_tensor("cthrs", [128, 64], F32))
        qkvb = E(nc.sbuf_tensor("qkvbs", [128, 18], F32))
        kb_s = E(nc.sbuf_tensor("kbs", [128, 128], F32))
        vb_s = E(nc.sbuf_tensor("vbs", [128, 128], F32))
        outb_s = E(nc.sbuf_tensor("outbs", [128, D], BF16))
        b1c = E(nc.sbuf_tensor("b1cs", [128, NFC], F32))
        cmt = E(nc.sbuf_tensor("cmt", [128, 1], F32))
        epst = E(nc.sbuf_tensor("epst", [128, 1], F32))
        negMb = E(nc.sbuf_tensor("negMb", [128, 1], F32))
        stt = E(nc.sbuf_tensor("stt", [128, 16], F32))
        kmax2 = E(nc.sbuf_tensor("kmax2", [1, 8], F32))
        qmax2 = E(nc.sbuf_tensor("qmax2", [1, 4], F32))
        sqjunk = E(nc.sbuf_tensor("sqjunk", [128, 1024], F32))
        thr1 = E(nc.sbuf_tensor("thr1", [128, 4], F32))
        absacc = E(nc.sbuf_tensor("absacc", [128, 2], F32))
        mtmp = E(nc.sbuf_tensor("mtmp", [128, 8], F32))

        P = Prog(nc)

        def barrier():
            for e in ENGINES:
                for sem in ALLSEMS:
                    P.wait(e, sem, sem.n)

        for dst, src in ((id16, id16_i), (idh, idh_i), (id32, id32_i),
                         (onescol, onescol_i), (onescolb, onescolb_i),
                         (onescolh, onescolh_i), (onesrow, onesrow_i),
                         (onesrowh, onesrowh_i), (iota, iota_i), (cthr, cthr_i),
                         (qkvb, qkvb_i), (kb_s, kb_i), (vb_s, vb_i),
                         (outb_s, outb_i), (b1c, b1c_i)):
            P.dma("y", dst[:], src, sx)
        P.op("v", "memset", cmt[:], CM_CONST, inc=sv)
        P.op("v", "memset", epst[:], EPS, inc=sv)
        consts_done = sx.n


        with contextlib.ExitStack() as ast_:
            EA = ast_.enter_context
            kfm = EA(nc.sbuf_tensor("kfm", [128, NKC, 128], BF16))
            vst = EA(nc.sbuf_tensor("vst", [128, NKC, 128], BF16))
            qfm = EA(nc.sbuf_tensor("qfm", [128, HEADS, 512], BF16))
            cfm = EA(nc.sbuf_tensor("cfm", [128, HEADS, 512], BF16))
            wbig = EA(nc.sbuf_tensor("wbig", [128, NDC, D], BF16))
            xbuf = EA(nc.sbuf_tensor("xbuf", [128, 2, D], F32))
            ln1own = EA(nc.sbuf_tensor("ln1own", [128, NDC, 512], BF16))
            pt = EA(nc.sbuf_tensor("pt", [128, 2, 1024], BF16))
            maskb = EA(nc.sbuf_tensor("maskb", [128, 1024], BF16))
            ktmp = EA(nc.sbuf_tensor("ktmp", [128, 128], BF16))
            rlrow = EA(nc.sbuf_tensor("rlrow", [1, 2, 512], F32))
            rl16 = EA(nc.sbuf_tensor("rl16", [1, 512], BF16))
            rlb = EA(nc.sbuf_tensor("rlb", [128, 512], F32))
            qsq = None
            ao = EA(nc.sbuf_tensor("ao", [128, D], F32))
            ln2tm = EA(nc.sbuf_tensor("ln2tm", [128, D], BF16))
            qsq = pt[:, 0, 0:512]
            ln1fm = ln1own[:, 8:12, :].rearrange("p c (a e) -> p (c a) e", a=4)
            l2b = maskb[:].rearrange("p (c e) -> p c e", c=8)
            psAB = [EA(nc.psum_tensor(f"psAB{i}", [128, 1024], F32)) for i in range(2)]
            psA = [psAB[0][:, 0:512], psAB[1][:, 0:512]]
            psCc = EA(nc.psum_tensor("psCc", [128, 512], F32))
            psL = EA(nc.psum_tensor("psL", [128, 512], F32))
            psS = EA(nc.psum_tensor("psS", [128, 512], F32))
            psT = EA(nc.psum_tensor("psT", [128, 1024], BF16))

            wkv_t = EA(nc.sbuf_tensor("wkv", [128, NDC, 256], BF16))
            wkv = wkv_t[:]
            P.dma("g", wkv, wqkvT[:, 2048:2304].rearrange("(c p) e -> p c e", p=128), sg)
            wkv_done = sg.n

            def C(par, i):
                return stt[:, 8 * par + i:8 * par + i + 1]

            def ln_stats1(src, par):
                P.op("v", "tensor_reduce", C(par, 0), src, AX, ALU.add, inc=sv)
                P.wait("s", sv, sv.n)
                P.op("s", "activation", sqjunk[:], src[:, 0:1024], AF.Square,
                     accum_out=C(par, 2), inc=ss)
                P.op("s", "activation", sqjunk[:], src[:, 1024:2048], AF.Square,
                     accum_out=C(par, 1), inc=ss)
                return ss.n

            def ln_stats2(par, s1done):
                P.wait("v", ss, s1done)
                P.op("v", "tensor_add", C(par, 1), C(par, 1), C(par, 2), inc=sv)
                P.op("v", "tensor_scalar", C(par, 3), C(par, 0), 1.0 / D, None, ALU.mult, inc=sv)
                P.wait("v", sv, sv.n)
                P.op("v", "tensor_mul", C(par, 4), C(par, 3), C(par, 3), inc=sv)
                P.op("v", "tensor_scalar", C(par, 2), C(par, 1), 1.0 / D, None, ALU.mult, inc=sv)
                P.wait("v", sv, sv.n)
                P.op("v", "tensor_sub", C(par, 4), C(par, 2), C(par, 4), inc=sv)
                P.wait("v", sv, sv.n)
                P.wait("s", sv, sv.n)
                P.op("s", "activation", C(par, 7), C(par, 4), AF.Ln, bias=epst[:, 0:1], inc=ss)
                P.op("s", "activation", C(par, 5), C(par, 7), AF.Exp, scale=-0.5, inc=ss)
                P.wait("v", ss, ss.n)
                P.op("v", "scalar_tensor_tensor", C(par, 6), C(par, 3), -1.0,
                     C(par, 5), ALU.mult, ALU.mult, inc=sv)
                P.wait("s", sv, sv.n)

            def ln_stats(src, par=0):
                d1 = ln_stats1(src, par)
                ln_stats2(par, d1)

            def norm_cast_transpose(src, dst_chunked, par=0):
                stg = (pt[:, 0, 0:512], pt[:, 0, 512:1024], pt[:, 1, 0:512], pt[:, 1, 512:1024])
                P.wait("s", sp, sp.n)
                for q4 in range(4):
                    P.op("s", "activation", stg[q4], src[:, 512 * q4:512 * (q4 + 1)],
                         AF.Identity, bias=C(par, 6), scale=C(par, 5), inc=ss)
                norm_done = ss.n
                P.wait("t", ss, norm_done)
                for half in range(2):
                    for i in range(8):
                        blk = 8 * half + i
                        q4, r = divmod(blk, 4)
                        P.op("t", "transpose", psT[:, 128 * i:128 * (i + 1)],
                             stg[q4][:, 128 * r:128 * (r + 1)], id16[:],
                             inc=(sp if i == 7 else None))
                    P.wait("v", sp, sp.n)
                    P.op("v", "tensor_copy",
                         dst_chunked[:, 8 * half:8 * half + 8, :],
                         psT[:, 0:1024].rearrange("p (c e) -> p c e", c=8), inc=sv)
                    P.wait("t", sv, sv.n)
                return norm_done

            wfa2 = cfm[:].bitcast(F32).rearrange("p c e -> p (c e)").rearrange(
                "p (s e) -> p s e", s=2)
            wfa3 = ao[:]
            l1B = ln1own[:, 0:8, :].rearrange("p c e -> p (c e)").rearrange(
                "p (s e) -> p s e", s=2)
            wqoB = l1B
            leb = ln2tm[:]

            def wslot(u):
                s_ = u % 3
                return wfa3 if s_ == 2 else wfa2[:, s_, :]

            def wsub_in(u):
                if u < 64:
                    i, s_ = divmod(u, 4)
                    return w1T[128 * i:128 * (i + 1), 2048 * s_:2048 * (s_ + 1)]
                return w2T[128 * (u - 64):128 * (u - 63), :]

            def wsub_out(u):
                if u < 64:
                    i, s_ = divmod(u, 4)
                    return w1q[i, :, 2048 * s_:2048 * (s_ + 1)]
                return w2q[u - 64, :, :]

            qload = {}
            quse = {}
            qdma = {}

            quse_s = {}

            def quant_load(L):
                u = L if L < 128 else L - 128
                if L >= 3:
                    P.wait("y", sv, quse.get(L - 3, 0))
                    P.wait("y", ss, quse_s.get(L - 3, 0))
                P.dma("y", wslot(L), wsub_in(u), sx)
                qload[L] = sx.n

            mean_vdone = {}

            def quant_mean_sub(u):
                p2 = (u % 2) * 2
                P.wait("s", sx, qload[u])
                if u >= 2:
                    P.wait("s", sv, mean_vdone[u - 2])
                P.op("s", "activation", sqjunk[:], wslot(u)[:, 0:1024], AF.Abs,
                     accum_out=mtmp[:, p2:p2 + 1], inc=ss)
                P.op("s", "activation", sqjunk[:], wslot(u)[:, 1024:2048], AF.Abs,
                     accum_out=mtmp[:, p2 + 1:p2 + 2], inc=ss)
                quse_s[u] = ss.n
                quse[u] = 0
                tgt = absacc[:, 0:1] if u < 64 else absacc[:, 1:2]
                P.wait("v", ss, ss.n)
                P.op("v", "tensor_add", mtmp[:, 4 + p2 // 2:5 + p2 // 2],
                     mtmp[:, p2:p2 + 1], mtmp[:, p2 + 1:p2 + 2], inc=sv)
                P.wait("v", sv, sv.n)
                if u == 0 or u == 64:
                    P.op("v", "tensor_copy", tgt, mtmp[:, 4 + p2 // 2:5 + p2 // 2], inc=sv)
                else:
                    P.op("v", "tensor_add", tgt, tgt, mtmp[:, 4 + p2 // 2:5 + p2 // 2], inc=sv)
                P.wait("v", sv, sv.n)
                mean_vdone[u] = sv.n

            def quant_thr():
                P.wait("t", sv, sv.n)
                P.op("t", "matmul", psL[0:1, 0:2], onescol[:, 0:1], absacc[:, 0:2],
                     start=True, stop=True, inc=sp)
                P.wait("v", sp, sp.n)
                P.op("v", "tensor_scalar", mtmp[0:1, 0:2], psL[0:1, 0:2],
                     0.7 / float(DFF * D), None, ALU.mult, inc=sv)
                P.wait("t", sv, sv.n)
                P.op("t", "matmul", psS[:, 0:2], onesrow[:], mtmp[0:1, 0:2],
                     start=True, stop=True, inc=sp)
                P.wait("v", sp, sp.n)
                P.op("v", "tensor_copy", thr1[:, 0:2], psS[:, 0:2], inc=sv)
                P.wait("v", sv, sv.n)
                P.op("v", "tensor_scalar", thr1[:, 2:4], thr1[:, 0:2], -1.0, None,
                     ALU.mult, inc=sv)
                P.wait("v", sv, sv.n)
                if d_thr is not None:
                    P.wait("g", sv, sv.n)
                    P.dma("g", d_thr[:, 0:4], thr1[:], so)
                    P.dma("g", d_thr[:, 4:6], absacc[:], so)

            def quant_q_sub(u):
                L = 128 + u
                tcol = 0 if u < 64 else 1
                P.wait("v", sx, qload[L])
                P.op("v", "tensor_scalar", leb, wslot(L),
                     thr1[:, 2 + tcol:3 + tcol], None, ALU.is_le, inc=sv)
                if u >= 2:
                    P.wait("v", sg, qdma[u - 2])
                P.wait("v", sv, sv.n)
                P.op("v", "scalar_tensor_tensor", wqoB[:, u % 2, :], wslot(L),
                     thr1[:, tcol:tcol + 1], leb, ALU.is_ge, ALU.subtract, inc=sv)
                quse[L] = sv.n
                quse_s[L] = 0
                P.wait("g", sv, quse[L])
                P.dma("g", wsub_out(u), wqoB[:, u % 2, :], sg)
                qdma[u] = sg.n

            P.wait("t", sg, wkv_done)
            P.wait("t", sx, consts_done)
            tile_done = [None] * NKC
            s1d = [None] * NKC

            xld = [None] * (NKC + 1)
            xguard = {0: [0, 0], 1: [0, 0]}

            def x_acquire(buf):
                P.wait("y", ss, xguard[buf][0])
                P.wait("y", sv, xguard[buf][1])

            def x_load(k):
                if k >= NKC:
                    return
                x_acquire(k % 2)
                P.dma("y", xbuf[:, k % 2, :], xseq[128 * k:128 * (k + 1), :], sx)
                xld[k] = sx.n

            def kv_part1(k):
                buf = k % 2
                P.wait("v", sx, xld[k])
                P.wait("s", sx, xld[k])
                s1d[k] = ln_stats1(xbuf[:, buf, :], k % 2)

            def kv_evac(k, pa):
                P.wait("v", sp, sp.n)
                P.op("v", "scalar_tensor_tensor", ktmp[:], pa[:, 0:128], 1.0, kb_s[:],
                     ALU.bypass, ALU.add, inc=sv)
                P.op("v", "scalar_tensor_tensor", vst[:, k, 0:128], pa[:, 128:256], 1.0,
                     vb_s[:], ALU.bypass, ALU.add, inc=sv)
                P.wait("t", sv, sv.n)
                P.op("t", "transpose", psT[:, 0:128], ktmp[:], id16[:], inc=sp)
                P.wait("v", sp, sp.n)
                P.op("v", "tensor_copy", kfm[:, k, :], psT[:, 0:128], inc=sv)
                P.wait("t", sv, sv.n)

            def kv_part2(k):
                buf = k % 2
                par = k % 2
                ln_stats2(par, s1d[k])
                tile_done[k] = norm_cast_transpose(xbuf[:, buf, :], ln1fm, par)
                xguard[buf][0] = tile_done[k]
                pa = psA[k % 2][:, 0:256]
                for dc in range(NDC):
                    P.op("t", "matmul", pa, ln1fm[:, dc, :], wkv[:, dc, :],
                         start=(dc == 0), stop=(dc == NDC - 1),
                         inc=(sp if dc == NDC - 1 else None))
                kv_evac(k, pa)

            psT32 = psT[:].bitcast(F32)

            ln2rd = [0]

            def kv_stage(k):
                par = k % 2
                ln_stats2(par, s1d[k])
                P.wait("s", sp, ln2rd[0])
                for q4 in range(4):
                    P.op("s", "activation", ln2tm[:, 512 * q4:512 * (q4 + 1)],
                         xbuf[:, k % 2, 512 * q4:512 * (q4 + 1)], AF.Identity,
                         bias=C(par, 6), scale=C(par, 5), inc=ss)
                tile_done[k] = ss.n
                xguard[k % 2][0] = ss.n

            def kv_finish(k):
                P.wait("t", ss, tile_done[k])
                for half in range(2):
                    for i in range(8):
                        blk = 8 * half + i
                        P.op("t", "transpose", psT[:, 128 * i:128 * (i + 1)],
                             ln2tm[:, 128 * blk:128 * (blk + 1)], id16[:],
                             inc=(sp if i == 7 else None))
                    P.wait("v", sp, sp.n)
                    P.op("v", "tensor_copy", ln1fm[:, 8 * half:8 * half + 8, :],
                         psT[:, 0:1024].rearrange("p (c e) -> p c e", c=8), inc=sv)
                    P.wait("t", sv, sv.n)
                ln2rd[0] = sp.n
                pa = psT32[:, 0:256]
                for dc in range(NDC):
                    P.op("t", "matmul", pa, ln1fm[:, dc, :], wkv[:, dc, :],
                         start=(dc == 0), stop=(dc == NDC - 1),
                         inc=(sp if dc == NDC - 1 else None))
                kv_evac(k, pa)

            def knorm_group(g):
                for half in range(2):
                    c0 = 16 * g + 8 * half
                    P.wait("s", sp, sp.n)
                    P.wait("s", sv, sv.n)
                    P.op("s", "activation", sqjunk[:, 0:512],
                         kfm[:, c0:c0 + 4, :].rearrange("p c e -> p (c e)"),
                         AF.Square, inc=ss)
                    P.op("s", "activation", sqjunk[:, 512:1024],
                         kfm[:, c0 + 4:c0 + 8, :].rearrange("p c e -> p (c e)"),
                         AF.Square, inc=ss)
                    P.wait("t", ss, ss.n)
                    P.op("t", "matmul", psS[0:1, :], onescol[:, 0:1], sqjunk[:, 0:512],
                         start=True, stop=False, skip_group_check=True)
                    P.op("t", "matmul", psS[0:1, :], onescol[:, 0:1], sqjunk[:, 512:1024],
                         start=False, stop=True, skip_group_check=True, inc=sp)
                    P.wait("v", sp, sp.n)
                    P.op("v", "tensor_reduce", kmax2[0:1, 1:2], psS[0:1, :], AX,
                         ALU.max, inc=sv)
                    P.wait("v", sv, sv.n)
                    if half == 0:
                        P.op("v", "tensor_copy", kmax2[0:1, 4 + g:5 + g],
                             kmax2[0:1, 1:2], inc=sv)
                    else:
                        P.op("v", "tensor_max", kmax2[0:1, 4 + g:5 + g],
                             kmax2[0:1, 4 + g:5 + g], kmax2[0:1, 1:2], inc=sv)
                    P.wait("v", sv, sv.n)
                    P.wait("t", sv, sv.n)

            x_load(0)
            quant_load(0)
            quant_load(1)
            next_load = [2]

            def quant_step(Lc):
                if next_load[0] < 128:
                    quant_load(next_load[0])
                    next_load[0] += 1
                quant_mean_sub(Lc)

            for k in range(NKC + 1):
                if k < NKC:
                    kv_part1(k)
                if k >= 1:
                    kv_part2(k - 1)
                if k < NKC:
                    x_load(k + 1)
                if k >= 1:
                    for Lc in range(2 * (k - 1), 2 * k):
                        quant_step(Lc)

            for e in ENGINES:
                P.wait(e, sx, sx.n)
                P.wait(e, sg, sg.n)
                P.wait(e, sv, sv.n)
                P.wait(e, ss, ss.n)
            quant_thr()
            for g_ in range(4):
                knorm_group(g_)
            kmax_done = sv.n

            regF = wbig[:].bitcast(F32).rearrange("p c e -> p (c e)")
            regB = wbig[:].rearrange("p c e -> p (c e)")
            wfa2n = regF[:, 8192:12288].rearrange("p (s e) -> p s e", s=2)
            wqoBn = regB[:, 24576:28672].rearrange("p (s e) -> p s e", s=2)
            lebn = regB[:, 28672:30720]
            p2_load = {}
            p2_use = {}
            p2_dma = {}
            p2_next = [0]
            p2_proc = [0]
            p2_sx = [0]
            p2_sv = [0]
            p2_gate = [None]

            def p2_load_one():
                u = p2_next[0]
                if u >= 128:
                    return
                if p2_gate[0] is not None:
                    P.wait("y", ss, p2_gate[0])
                    p2_gate[0] = None
                if u >= 2:
                    P.wait("y", sv, p2_use.get(u - 2, 0))
                P.dma("y", wfa2n[:, u % 2, :], wsub_in(u), sx)
                p2_load[u] = sx.n
                p2_sx[0] = sx.n
                p2_next[0] += 1

            def p2_step(n, limit):
                for _ in range(n):
                    while p2_next[0] < min(p2_proc[0] + 2, limit):
                        p2_load_one()
                    u = p2_proc[0]
                    if u >= limit:
                        return
                    tcol = 0 if u < 64 else 1
                    P.wait("v", sx, p2_load[u])
                    P.op("v", "tensor_scalar", lebn, wfa2n[:, u % 2, :],
                         thr1[:, 2 + tcol:3 + tcol], None, ALU.is_le, inc=sv)
                    if u >= 2:
                        P.wait("v", sg, p2_dma[u - 2])
                    P.wait("v", sv, sv.n)
                    P.op("v", "scalar_tensor_tensor", wqoBn[:, u % 2, :],
                         wfa2n[:, u % 2, :], thr1[:, tcol:tcol + 1], lebn,
                         ALU.is_ge, ALU.subtract, inc=sv)
                    p2_use[u] = sv.n
                    p2_sv[0] = sv.n
                    P.wait("g", sv, p2_use[u])
                    P.dma("g", wsub_out(u), wqoBn[:, u % 2, :], sg)
                    p2_dma[u] = sg.n
                    p2_proc[0] += 1

            P2Q = (
                [0] * HEADS,
                [3 if h_ < 4 else 2 for h_ in range(HEADS)],
                [3 if h_ < 14 else 2 for h_ in range(HEADS)],
                [3 if h_ < 14 else 2 for h_ in range(HEADS)],
            )

            ydma_prev = {0: (0, 0), 1: (0, 0)}
            l2dma_prev = [0]
            aoev_prev = {}
            ao2 = wkv_t[:].bitcast(F32).rearrange("p c e -> p (c e)")
            pend = [None]

            def ln2_flush():
                if pend[0] is None:
                    return
                tt_, ln2w_ = pend[0]
                pend[0] = None
                P.wait("t", ss, ln2w_)
                for half in range(2):
                    P.wait("v", sx, l2dma_prev[0])
                    for i_ in range(8):
                        blk = 8 * half + i_
                        P.op("t", "transpose", psT[:, 128 * i_:128 * (i_ + 1)],
                             ln2tm[:, 128 * blk:128 * (blk + 1)], id16[:],
                             inc=(sp if i_ == 7 else None))
                    P.wait("v", sp, sp.n)
                    P.op("v", "tensor_copy", l2b[:],
                         psT[:, 0:1024].rearrange("p (c e) -> p c e", c=8), inc=sv)
                    P.wait("t", sv, sv.n)
                    P.wait("y", sv, sv.n)
                    P.dma("y", ln2scr[tt_, 8 * half:8 * half + 8].rearrange("c p e -> p c e"),
                          l2b[:], sx)
                    l2dma_prev[0] = sx.n
                ln2rd[0] = sp.n
            for s_i in range(NSLOT):
                Es = 16 * (s_i + 1)
                P.wait("v", sx, sx.n)
                P.wait("g", sp, sp.n)
                P.dma("g", wbig[:], wqkvT[:, 0:2048].rearrange("(c p) e -> p c e", p=128), sg)
                wq_loaded = sg.n

                for tl in range(4):
                    tt = 4 * s_i + tl
                    buf = tl % 2
                    x_acquire(buf)
                    P.dma("y", xbuf[:, buf, :], xown[128 * tt:128 * (tt + 1), :], sx)
                    P.wait("v", sx, sx.n)
                    P.wait("s", sx, sx.n)
                    ln_stats(xbuf[:, buf, :], tl % 2)
                    norm_cast_transpose(
                        xbuf[:, buf, :],
                        ln1own[:, :, 128 * tl:128 * (tl + 1)], tl % 2)
                    xguard[buf][0] = ss.n
                P.wait("t", sg, wq_loaded)
                qev = {}
                for h in range(HEADS):
                    pa = psA[h % 2]
                    if h >= 2:
                        P.wait("t", ss, qev[h - 2])
                    for dc in range(NDC):
                        P.op("t", "matmul", pa[:, 0:512], wbig[:, dc, 128 * h:128 * (h + 1)],
                             ln1own[:, dc, :], start=(dc == 0), stop=(dc == NDC - 1),
                             inc=(sp if dc == NDC - 1 else None))
                    P.wait("s", sp, sp.n)
                    P.op("s", "activation", qfm[:, h, :],
                         pa[:, 0:512], AF.Identity, bias=qkvb[:, h:h + 1], inc=ss)
                    qev[h] = ss.n
                qproj_done = ss.n
                p2_gate[0] = qproj_done

                for h in range(HEADS):
                    P.wait("s", sp, sp.n)
                    if h == 0:
                        P.wait("s", ss, qproj_done)
                    P.op("s", "activation", qsq, qfm[:, h, :], AF.Square, inc=ss)
                    P.wait("t", ss, ss.n)
                    P.op("t", "matmul", psA[0][0:1, :], onescolb[:, 0:1], qsq,
                         start=True, stop=True, inc=sp)
                    P.wait("v", sp, sp.n)
                    P.op("v", "tensor_reduce", qmax2[0:1, 1:2], psA[0][0:1, :], AX,
                         ALU.max, inc=sv)
                    P.wait("v", sv, sv.n)
                    if h == 0:
                        P.op("v", "tensor_copy", qmax2[0:1, 0:1], qmax2[0:1, 1:2], inc=sv)
                    else:
                        P.op("v", "tensor_max", qmax2[0:1, 0:1], qmax2[0:1, 0:1],
                             qmax2[0:1, 1:2], inc=sv)
                    P.wait("v", sv, sv.n)
                    P.wait("t", sv, sv.n)
                P.op("v", "tensor_reduce", kmax2[0:1, 0:1], kmax2[0:1, 4:5 + s_i], AX,
                     ALU.max, inc=sv)
                P.wait("v", sv, sv.n)
                P.wait("s", sv, sv.n)
                P.op("s", "activation", qmax2[0:1, 2:3], qmax2[0:1, 0:1], AF.Ln, inc=ss)
                P.op("s", "activation", kmax2[0:1, 2:3], kmax2[0:1, 0:1], AF.Ln, inc=ss)
                P.wait("v", ss, ss.n)
                P.op("v", "tensor_add", qmax2[0:1, 3:4], qmax2[0:1, 2:3],
                     kmax2[0:1, 2:3], inc=sv)
                P.wait("v", sv, sv.n)
                P.wait("s", sv, sv.n)
                P.op("s", "activation", qmax2[0:1, 1:2], qmax2[0:1, 3:4], AF.Exp,
                     scale=0.5, bias=cmt[0:1, 0:1], inc=ss)
                P.wait("v", ss, ss.n)
                P.op("v", "tensor_scalar", qmax2[0:1, 1:2], qmax2[0:1, 1:2], 1.0, None,
                     ALU.add, inc=sv)
                P.wait("v", sv, sv.n)
                P.op("v", "tensor_scalar", qmax2[0:1, 1:2], qmax2[0:1, 1:2], -1.0, None,
                     ALU.mult, inc=sv)
                P.wait("v", sv, sv.n)
                P.wait("t", sv, sv.n)
                P.op("t", "matmul", psA[0][:, 0:1], onesrow[:], qmax2[0:1, 1:2],
                     start=True, stop=True, inc=sp)
                P.wait("v", sp, sp.n)
                P.wait("v", ss, ss.n)
                P.op("v", "tensor_copy", negMb[:], psA[0][:, 0:1], inc=sv)
                negm_done = sv.n
                P.wait("s", sv, negm_done)
                P.wait("t", sv, negm_done)

                ctx_ev_done = {}
                sc_ev_done = {}
                p2_limit = sum(sum(P2Q[s2]) for s2 in range(s_i + 1))
                for h in range(HEADS):
                    p2_step(P2Q[s_i][h], p2_limit)
                    exp_done = {}
                    ready = {}
                    rsem = {}
                    ctx_done = {}

                    def emit_ctx(cp, h=h):
                        if cp == 0 and h >= 1:
                            P.wait("t", sv, ctx_ev_done[h - 1])
                        P.wait("t", rsem[cp], ready[cp])
                        pb = (cp // 2) % 2
                        psrc = pt
                        P.op("t", "matmul", psCc[:], vst[:, cp, 0:128],
                             psrc[:, pb, 0:512], start=(cp == 0), stop=False,
                             skip_group_check=True)
                        P.op("t", "matmul", psCc[:], vst[:, cp + 1, 0:128],
                             psrc[:, pb, 512:1024], start=False, stop=(cp == Es - 2),
                             skip_group_check=True)
                        P.op("t", "matmul", psL[0:1, :], onescolb[:, 0:1],
                             psrc[:, pb, 0:512], start=(cp == 0), stop=False,
                             skip_group_check=True)
                        P.op("t", "matmul", psL[0:1, :], onescolb[:, 0:1],
                             psrc[:, pb, 512:1024], start=False, stop=(cp == Es - 2),
                             skip_group_check=True, inc=sp)
                        ctx_done[cp] = sp.n

                    for cp in range(0, Es, 2):
                        pb = (cp // 2) % 2
                        pa = psAB[pb]
                        if cp >= 4:
                            P.wait("t", ss, exp_done[cp - 4])
                        P.op("t", "matmul", pa[:, 0:512], kfm[:, cp, :], qfm[:, h, :],
                             start=True, stop=True)
                        P.op("t", "matmul", pa[:, 512:1024], kfm[:, cp + 1, :], qfm[:, h, :],
                             start=True, stop=True, inc=sp)
                        sc_mm = sp.n
                        P.wait("s", sp, sc_mm)
                        if cp >= 4:
                            P.wait("s", sp, ctx_done[cp - 4])
                        P.op("s", "activation", pt[:, pb, :], pa[:, 0:1024], AF.Exp,
                             bias=negMb[:, 0:1], scale=SCALE, inc=ss)
                        exp_done[cp] = ss.n
                        if cp >= 16 * s_i:
                            P.op("v", "tensor_scalar", maskb[:], iota[:],
                                 cthr[:, cp:cp + 1], None, ALU.is_ge, inc=sv)
                            P.wait("v", sv, sv.n)
                            P.wait("v", ss, exp_done[cp])
                            P.op("v", "tensor_mul", pt[:, pb, :], pt[:, pb, :],
                                 maskb[:], inc=sv)
                            ready[cp], rsem[cp] = sv.n, sv
                        else:
                            ready[cp], rsem[cp] = exp_done[cp], ss
                        if cp >= 2:
                            emit_ctx(cp - 2)
                    emit_ctx(Es - 2)
                    P.wait("v", sp, ctx_done[Es - 2])
                    P.op("v", "reciprocal", rlrow[0:1, h % 2, :], psL[0:1, :], inc=sv)
                    P.wait("v", sv, sv.n)
                    P.op("v", "tensor_copy", rl16[0:1, :], rlrow[0:1, h % 2, :], inc=sv)
                    P.wait("t", sv, sv.n)
                    if h >= 1:
                        P.wait("t", ss, sc_ev_done[h - 1])
                    P.op("t", "matmul", psS[:], onesrowh[:], rl16[0:1, :],
                         start=True, stop=True, inc=sp)
                    P.wait("s", sp, sp.n)
                    P.op("s", "activation", rlb[:], psS[:], AF.Identity, inc=ss)
                    sc_ev_done[h] = ss.n
                    P.wait("v", ss, sc_ev_done[h])
                    P.op("v", "tensor_mul", cfm[:, h, :], psCc[:], rlb[:], inc=sv)
                    ctx_ev_done[h] = sv.n
                attn_done = sv.n

                if s_i == 0 and d_qfm is not None:
                    P.wait("g", ss, qproj_done)
                    P.dma("g", d_qfm, qfm[:].rearrange("p h q -> p (h q)"), so)
                if s_i == 0 and d_cfm is not None:
                    P.wait("g", sv, attn_done)
                    P.dma("g", d_cfm, cfm[:].rearrange("p h q -> p (h q)"), so)

                P.wait("g", ss, qproj_done)
                P.wait("g", sx, p2_sx[0])
                P.wait("g", sv, p2_sv[0])
                P.dma("g", wbig[:], woutT.rearrange("(c p) e -> p c e", p=128), sg)
                wo_loaded = sg.n
                P.wait("t", sg, wo_loaded)
                P.wait("t", sv, attn_done)
                for tl in range(4):
                    tt = 4 * s_i + tl
                    tb = tl % 2
                    aop = ao if tb == 0 else ao2
                    x_acquire(tb)
                    P.dma("y", xbuf[:, tb, :], xown[128 * tt:128 * (tt + 1), :], sx)
                    xo_ld = sx.n
                    aoev = {}
                    for dspan in range(4):
                        pa = psA[dspan % 2]
                        if dspan >= 2:
                            P.wait("t", sv, aoev[dspan - 2])
                        elif tl > 0:
                            P.wait("t", sv, aoev_prev[dspan + 2])
                        for ec in range(NDC):
                            P.op("t", "matmul", pa[:], cfm[:, ec, 128 * tl:128 * (tl + 1)],
                                 wbig[:, ec, 512 * dspan:512 * (dspan + 1)],
                                 start=(ec == 0), stop=(ec == NDC - 1),
                                 inc=(sp if ec == NDC - 1 else None))
                        P.wait("v", sp, sp.n)
                        if dspan == 0:
                            P.wait("v", sx, ydma_prev[tb][0])
                            P.wait("v", so, ydma_prev[tb][1])
                        P.op("v", "scalar_tensor_tensor", aop[:, 512 * dspan:512 * (dspan + 1)],
                             pa[:, 0:512], 1.0, outb_s[:, 512 * dspan:512 * (dspan + 1)],
                             ALU.bypass, ALU.add, inc=sv)
                        aoev[dspan] = sv.n
                    aoev_prev = aoev
                    ln2_flush()
                    ln_stats(aop[:])
                    for q2 in range(2):
                        P.op("s", "activation", sqjunk[:, 0:1024],
                             aop[:, 1024 * q2:1024 * (q2 + 1)], AF.Identity,
                             bias=stt[:, 6:7], scale=stt[:, 5:6], inc=ss)
                        P.wait("v", ss, ss.n)
                        if q2 == 0:
                            P.wait("v", sx, xo_ld)
                        P.op("v", "tensor_add", aop[:, 1024 * q2:1024 * (q2 + 1)],
                             sqjunk[:, 0:1024], xbuf[:, tb, 1024 * q2:1024 * (q2 + 1)],
                             inc=sv)
                        if q2 == 0:
                            P.wait("s", sv, sv.n)
                    xguard[tb][1] = sv.n
                    y_done = sv.n
                    P.wait("y", sv, y_done)
                    P.dma("y", yscr[128 * tt:128 * (tt + 1), :], aop[:], sx)
                    if d_y is not None:
                        P.dma("y", d_y[128 * tt:128 * (tt + 1), :], aop[:], so)
                    ydma_prev[tb] = (sx.n, so.n)
                    ln_stats(aop[:])
                    P.wait("s", sp, ln2rd[0])
                    P.op("s", "activation", ln2tm[:], aop[:], AF.Identity,
                         bias=stt[:, 6:7], scale=stt[:, 5:6], inc=ss)
                    pend[0] = (tt, ss.n)
                ln2_flush()
            P.flush()
        barrier()

        with contextlib.ExitStack() as fst:
            EF = fst.enter_context
            l2f = EF(nc.sbuf_tensor("l2f", [128, NDC, 512], BF16))
            hsp = EF(nc.sbuf_tensor("hsp", [128, NFC, 512], BF16))
            wg1 = EF(nc.sbuf_tensor("wg1", [128, 2, NDC, 512], BF16))
            wg2n = EF(nc.sbuf_tensor("wg2n", [128, 4, 1024], BF16))
            ysp = EF(nc.sbuf_tensor("ysp", [128, 2, D], F32))
            otile = EF(nc.sbuf_tensor("otile", [128, 2, D], F32))
            b2row = EF(nc.sbuf_tensor("b2rows", [128, D], F32))
            ffs = EF(nc.sbuf_tensor("ffs", [128, 4, 2048], F32))
            psB = [EF(nc.psum_tensor(f"psB{i}", [128, 1024], F32)) for i in range(4)]
            P.dma("s", b2row[:], b2row_i, sq)
            b2row_ld = sq.n

            WQ = ("g", "y")
            WS = {"g": sg, "y": sx}

            a_used = {}
            b_used = {}
            outdma = {}
            ysp_guard = {}
            ffs_free = (0, 0)
            prev_a_last = 0
            span_psB_free = 0
            for sp_i in range(NSLOT):
                P.wait("s", sp, prev_a_last)
                for t_ in range(4):
                    P.dma("s", l2f[:, :, 128 * t_:128 * (t_ + 1)],
                          ln2scr[4 * sp_i + t_].rearrange("c p e -> p c e"), sq)
                l2f_ld = sq.n

                gelu_done = {}
                for g in range(16):
                    gb = g % 2
                    ka = 16 * sp_i + g
                    q_ = WQ[g % 2]
                    if ka >= 2:
                        P.wait(q_, sp, a_used[ka - 2])
                    P.dma(q_, wg1[:, gb, :, :], w1q[:, :, 512 * g:512 * (g + 1)].rearrange(
                        "c p e -> p c e"), WS[q_])
                    g_ld = WS[q_].n
                    for fi in range(4):
                        fc = 4 * g + fi
                        pf = psB[fc % 4][:, 0:512]
                        if fi == 0:
                            P.wait("t", WS[q_], g_ld)
                        if fc == 0:
                            P.wait("t", sq, l2f_ld)
                            P.wait("t", sv, span_psB_free)
                        if fc >= 4:
                            P.wait("t", ss, gelu_done[fc - 4])
                        for dc in range(NDC):
                            P.op("t", "matmul", pf,
                                 wg1[:, gb, dc, 128 * fi:128 * (fi + 1)],
                                 l2f[:, dc, :], start=(dc == 0), stop=(dc == NDC - 1),
                                 inc=(sp if dc == NDC - 1 else None))
                        P.wait("s", sp, sp.n)
                        P.op("s", "activation", hsp[:, fc, :], pf, AF.Gelu,
                             bias=b1c[:, fc:fc + 1], inc=ss)
                        gelu_done[fc] = ss.n
                    a_used[ka] = sp.n
                a_done = ss.n
                prev_a_last = sp.n

                wg2ld = {}
                evac = {}

                def b_dma(i, sp_i=sp_i):
                    dh, fc = divmod(i, NFC)
                    q_ = WQ[i % 2]
                    if i >= 4:
                        P.wait(q_, sp, b_used[i - 4])
                    P.dma(q_, wg2n[:, i % 4, :],
                          w2q[fc, :, 1024 * dh:1024 * (dh + 1)], WS[q_])
                    wg2ld[i] = (WS[q_].n, WS[q_])

                def b_mms(i):
                    dh, fc = divmod(i, NFC)
                    n_, sem_ = wg2ld[i]
                    P.wait("t", sem_, n_)
                    if fc == 0:
                        P.wait("t", ss, a_done)
                    for tl in range(4):
                        if dh == 1 and fc == 0:
                            P.wait("t", sv, evac[(0, tl)])
                        lw = hsp[:, fc, 128 * tl:128 * (tl + 1)]
                        for dq in range(2):
                            P.op("t", "matmul", psB[tl][:, 512 * dq:512 * (dq + 1)],
                                 lw, wg2n[:, i % 4, 512 * dq:512 * (dq + 1)],
                                 start=(fc == 0), stop=(fc == NFC - 1),
                                 inc=(sp if (tl == 3 and dq == 1) else None))
                    b_used[i] = sp.n

                b_dma(0)
                b_dma(1)
                b_dma(2)
                for i in range(2 * NFC):
                    if i + 3 < 2 * NFC:
                        b_dma(i + 3)
                    b_mms(i)
                    if i % NFC == NFC - 1:
                        dh = i // NFC
                        P.wait("v", sp, sp.n)
                        P.wait("v", sq, b2row_ld)
                        for tl in range(4):
                            P.op("v", "scalar_tensor_tensor",
                                 ffs[:, tl, 1024 * dh:1024 * (dh + 1)],
                                 psB[tl][:, 0:1024], 1.0,
                                 b2row[:, 1024 * dh:1024 * (dh + 1)], ALU.bypass,
                                 ALU.add, inc=sv)
                            evac[(dh, tl)] = sv.n
                span_psB_free = sv.n

                for tl in range(4):
                    tt = 4 * sp_i + tl
                    tb = tl % 2
                    par = tl % 2
                    if tt >= 2:
                        P.wait("s", sv, ysp_guard[tt - 2])
                    P.dma("s", ysp[:, tb, :], yscr[128 * tt:128 * (tt + 1), :], sq)
                    y_ld = sq.n
                    P.wait("v", sv, evac[(1, tl)])
                    P.wait("s", sv, evac[(1, tl)])
                    ln_stats(ffs[:, tl, :], par)
                    for q2 in range(2):
                        P.op("s", "activation", sqjunk[:, 0:1024],
                             ffs[:, tl, 1024 * q2:1024 * (q2 + 1)], AF.Identity,
                             bias=C(par, 6), scale=C(par, 5), inc=ss)
                        P.wait("v", ss, ss.n)
                        if q2 == 0:
                            P.wait("v", sq, y_ld)
                            if tt >= 2:
                                P.wait("v", so, outdma[tt - 2])
                        P.op("v", "tensor_add", otile[:, tb, 1024 * q2:1024 * (q2 + 1)],
                             sqjunk[:, 0:1024], ysp[:, tb, 1024 * q2:1024 * (q2 + 1)],
                             inc=sv)
                        if q2 == 0:
                            P.wait("s", sv, sv.n)
                    P.wait("y", sv, sv.n)
                    P.dma("y", out_t[128 * tt:128 * (tt + 1), :], otile[:, tb, :], so)
                    outdma[tt] = so.n
                    ysp_guard[tt] = sv.n
                ffs_free = (ss.n, sv.n)
            P.wait("y", so, so.n)
            P.flush()

    print(f"[mk] total instructions: {P.ninstr}")
    return nc



def make_inputs(L0_attn_x, L0_attn_Wqkv_w, L0_attn_Wqkv_b, L0_attn_out_w,
                L0_attn_out_b, L0_ffn_w1, L0_ffn_b1, L0_ffn_w2, L0_ffn_b2):
    f = np.float32
    wqkvT = np.ascontiguousarray(np.asarray(L0_attn_Wqkv_w, f).T)
    woutT = np.ascontiguousarray(np.asarray(L0_attn_out_w, f).T)
    w1T = np.ascontiguousarray(np.asarray(L0_ffn_w1, f).T)
    w2T = np.ascontiguousarray(np.asarray(L0_ffn_w2, f).T)
    qkvb = np.ascontiguousarray(np.asarray(L0_attn_Wqkv_b, f).reshape(18, 128).T)
    kb = np.ascontiguousarray(np.broadcast_to(np.asarray(L0_attn_Wqkv_b[2048:2176], f), (128, 128)))
    vb = np.ascontiguousarray(np.broadcast_to(np.asarray(L0_attn_Wqkv_b[2176:2304], f), (128, 128)))
    outb = np.ascontiguousarray(np.broadcast_to(
        np.asarray(L0_attn_out_b, f).astype(ml_dtypes.bfloat16), (128, D)))
    b1c = np.ascontiguousarray(np.asarray(L0_ffn_b1, f).reshape(NFC, 128).T)
    b2row = np.ascontiguousarray(np.broadcast_to(np.asarray(L0_ffn_b2, f), (128, D)))
    id16 = np.eye(128, dtype=ml_dtypes.bfloat16)
    idh = np.eye(128, dtype=np.float16)
    id32 = np.eye(128, dtype=f)
    onescol = np.ones((128, 4), f)
    onescolb = np.ones((128, 4), ml_dtypes.bfloat16)
    onescolh = np.ones((128, 4), np.float16)
    onesrow = np.ones((1, 128), f)
    onesrowh = np.ones((1, 128), ml_dtypes.bfloat16)
    iov = np.concatenate([np.arange(512, dtype=np.float16),
                          np.arange(512, dtype=np.float16) - np.float16(128.0)])
    iota = np.ascontiguousarray(np.broadcast_to(iov, (128, 1024)))

    x = np.asarray(L0_attn_x, f)
    in_maps = []
    layout = []
    for c in range(8):
        b, j = divmod(c, 4)
        plist = p_list_for(j)
        xo = np.ascontiguousarray(
            np.concatenate([x[b, 512 * p:512 * (p + 1), :] for p in plist], axis=0))
        cth = np.zeros((128, 64), np.float32)
        pvec = np.arange(128, dtype=np.float32)
        for s_i in range(4):
            Pm = plist[s_i]
            for cc in range(16 * s_i, 16 * (s_i + 1)):
                t = 128.0 * cc + pvec - 512.0 * Pm
                cth[:, cc] = np.clip(t, -640.0, 520.0)
        cth = cth.astype(np.float32)
        in_maps.append({
            "xseq": np.ascontiguousarray(x[b]),
            "xown": xo,
            "wqkvT": wqkvT, "woutT": woutT, "w1T": w1T, "w2T": w2T,
            "qkvb": qkvb, "kb": kb, "vb": vb, "outb": outb,
            "b1c": b1c, "b2row": b2row,
            "id16": id16, "idh": idh, "id32": id32,
            "onescol": onescol, "onescolb": onescolb, "onescolh": onescolh,
            "onesrow": onesrow, "onesrowh": onesrowh,
            "iota": iota, "cthr": cth,
        })
        layout.append((b, plist))
    return in_maps, layout


def kernel(L0_attn_x, L0_attn_Wqkv_w, L0_attn_Wqkv_b, L0_attn_out_w, L0_attn_out_b,
           L0_ffn_w1, L0_ffn_b1, L0_ffn_w2, L0_ffn_b2, debug=()):
    in_maps, layout = make_inputs(L0_attn_x, L0_attn_Wqkv_w, L0_attn_Wqkv_b,
                                  L0_attn_out_w, L0_attn_out_b, L0_ffn_w1,
                                  L0_ffn_b1, L0_ffn_w2, L0_ffn_b2)
    nc = build(debug=debug)
    res = run_bass_kernel_spmd(nc, in_maps, core_ids=list(range(8)))
    out = np.empty((2, S, D), np.float32)
    for c in range(8):
        b, plist = layout[c]
        o = res.results[c]["out"]
        for si, p in enumerate(plist):
            out[b, 512 * p:512 * (p + 1), :] = o[512 * si:512 * (si + 1), :]
    if debug:
        return out, res
    return out
